# version 13
# speedup vs baseline: 1.3629x; 1.0147x over previous
import numpy as np
import ml_dtypes
from contextlib import ExitStack
import concourse.bass as bass
import concourse.mybir as mybir
from concourse.bass_utils import run_bass_kernel_spmd

F32 = mybir.dt.float32
BF16 = mybir.dt.bfloat16
I32 = mybir.dt.int32
AF = mybir.ActivationFunctionType
ALU = mybir.AluOpType
PAR = 128

ENGINES = ("sync", "tensor", "vector", "scalar", "gpsimd")


class Cfg:
    def __init__(self, B=4, S=4096):
        self.B = B
        self.S = S
        self.D = 2048
        self.H16 = 16
        self.HD = 128
        self.FF = 8192
        self.R = 16
        self.E = 16
        self.SLOT = 128
        self.NSLOT = 2048
        self.MH = 512
        self.n_cores = 2 * B
        self.NT_TOT = S // 128
        self.NT = self.NT_TOT // 2
        self.n_own = self.NT * 128
        self.NQC = self.n_own // 512
        self.DC = self.D // 128
        self.FC = self.FF // 128
        assert self.NT % 4 == 0


def _rope0_perm(H=16, hd=128):
    perm = []
    for h in range(H):
        base = h * hd
        perm += [base + 2 * j for j in range(hd // 2)]
        perm += [base + 2 * j + 1 for j in range(hd // 2)]
    return np.array(perm)


def _rope1_perm(H, hd=128):
    perm = []
    half = hd // 2
    for h in range(H):
        base = h * hd
        perm += [base + 2 * j for j in range(half // 2)]
        perm += [base + half + 2 * j for j in range(half // 2)]
        perm += [base + 2 * j + 1 for j in range(half // 2)]
        perm += [base + half + 2 * j + 1 for j in range(half // 2)]
    return np.array(perm)


def _rope0_tables(pos, hd=128):
    theta = 10000.0 ** (-np.arange(0, hd, 2, dtype=np.float32) / hd)
    f = pos[None, :].astype(np.float32) * theta[:, None]
    return np.cos(f).astype(np.float32), np.sin(f).astype(np.float32)


def _rope1_consts(hd=128, H=16):
    half = hd // 2
    theta = 1.0 / 10000.0 ** (np.arange(0, half, 2, dtype=np.float32) / half)
    f = np.arange(H, dtype=np.float32)[None, :] * theta[:, None]
    c = np.concatenate([np.cos(f), np.cos(f)], axis=0)
    s = np.concatenate([np.sin(f), np.sin(f)], axis=0)
    return c.astype(np.float32), s.astype(np.float32)


def _tile_w(wT, ocw=128):
    Din, Dout = wT.shape
    t = wT.reshape(Din // 128, 128, Dout // ocw, ocw)
    return np.ascontiguousarray(t.transpose(2, 1, 0, 3))


def host_prepare(cfg, inputs):
    D, FF = cfg.D, cfg.FF
    p0 = _rope0_perm()
    p1 = _rope1_perm(16)
    p1k = _rope1_perm(1)

    sh = {}
    sh["wq0T"] = _tile_w(np.asarray(inputs["L0_attn_wq"])[p0].T).astype(ml_dtypes.bfloat16)
    sh["bq0"] = np.ascontiguousarray(np.asarray(inputs["L0_attn_bq"])[p0])[None, :]
    sh["wk0T"] = _tile_w(np.asarray(inputs["L0_attn_wk"])[p0].T).astype(ml_dtypes.bfloat16)
    sh["bk0"] = np.ascontiguousarray(np.asarray(inputs["L0_attn_bk"])[p0])[None, :]
    sh["wv0T"] = _tile_w(np.asarray(inputs["L0_attn_wv"]).T, ocw=512).astype(ml_dtypes.bfloat16)
    sh["bv0"] = np.asarray(inputs["L0_attn_bv"])[None, :].copy()
    sh["wo0T"] = _tile_w(np.asarray(inputs["L0_attn_wo"]).T).astype(ml_dtypes.bfloat16)
    sh["bo0"] = np.asarray(inputs["L0_attn_bo"])[None, :].copy()
    sh["wupTq"] = _tile_w(np.asarray(inputs["L0_ffn_w_up_q"]).T)
    sh["scupT"] = _tile_w(
        np.repeat(np.asarray(inputs["L0_ffn_w_up_scale"]), 64, axis=1).T).astype(
        ml_dtypes.bfloat16)
    sh["bup"] = np.asarray(inputs["L0_ffn_b_up"])[None, :].copy()
    sh["upA"] = np.asarray(inputs["L0_ffn_w_up_lora_a"]).copy()
    sh["upB"] = np.asarray(inputs["L0_ffn_w_up_lora_b"]).copy()
    wdnq = np.asarray(inputs["L0_ffn_w_down_q"]).T
    scdn = np.repeat(np.asarray(inputs["L0_ffn_w_down_scale"]), 64, axis=1).T
    sh["wdnTq"] = np.ascontiguousarray(
        wdnq.reshape(64, 128, 16, 128).transpose(2, 1, 0, 3))
    sh["scdnT"] = np.ascontiguousarray(
        scdn.reshape(64, 128, 16, 128).transpose(2, 1, 0, 3)).astype(
        ml_dtypes.bfloat16)
    sh["bdn"] = np.asarray(inputs["L0_ffn_b_down"])[None, :].copy()
    sh["dnA"] = np.asarray(inputs["L0_ffn_w_down_lora_a"]).copy()
    sh["dnB"] = np.asarray(inputs["L0_ffn_w_down_lora_b"]).copy()

    sh["wq1T"] = _tile_w(np.asarray(inputs["L1_attn_Wq"])[p1].T).astype(ml_dtypes.bfloat16)
    sh["wk1T"] = _tile_w(np.asarray(inputs["L1_attn_Wk"])[p1k].T).astype(ml_dtypes.bfloat16)
    sh["wv1T"] = _tile_w(np.asarray(inputs["L1_attn_Wv"]).T).astype(ml_dtypes.bfloat16)
    sh["wo1T"] = _tile_w(np.asarray(inputs["L1_attn_Wo"]).T).astype(ml_dtypes.bfloat16)

    sh["gamma"] = np.asarray(inputs["L1_ffn_gamma"])[:, None].copy()
    sh["beta"] = np.asarray(inputs["L1_ffn_beta"])[:, None].copy()
    sh["mu"] = np.ascontiguousarray(np.asarray(inputs["L1_ffn_mu"]).reshape(D, cfg.NSLOT))
    sh["moescale"] = np.asarray(inputs["L1_ffn_scale"]).reshape(1, 1).copy()
    sh["W1"] = np.asarray(inputs["L1_ffn_W1"]).astype(ml_dtypes.bfloat16)
    sh["b1"] = np.asarray(inputs["L1_ffn_b1"]).copy()
    sh["W2"] = np.asarray(inputs["L1_ffn_W2"]).astype(ml_dtypes.bfloat16)
    sh["b2"] = np.asarray(inputs["L1_ffn_b2"]).astype(ml_dtypes.bfloat16)

    c1, s1 = _rope1_consts(128, 16)
    sh["cos1"] = c1
    sh["sin1"] = s1
    sh["ident"] = np.eye(128, dtype=np.float32)

    x_full = np.asarray(inputs["L0_attn_x"], np.float32)
    percore = []
    for c in range(cfg.n_cores):
        b, par = c // 2, c % 2
        tiles = [2 * i + par for i in range(cfg.NT)]
        pos = np.concatenate([np.arange(t * 128, (t + 1) * 128) for t in tiles])
        pc = {}
        pc["x_in"] = np.ascontiguousarray(
            x_full[b].reshape(cfg.NT_TOT, 128, D)[tiles])
        c0, s0 = _rope0_tables(pos)
        pc["cos0"] = c0
        pc["sin0"] = s0
        m = np.zeros((8, 128, 512), np.float32)
        for t in range(8):
            for j in range(4):
                kvpos = t * 128 + np.arange(128)
                qpos = (2 * j + par) * 128 + np.arange(128)
                m[t][:, j * 128:(j + 1) * 128] = (
                    kvpos[:, None] <= qpos[None, :]).astype(np.float32)
        pc["masks1"] = m
        percore.append(pc)
    return sh, percore


class Sem:
    __slots__ = ("h", "v", "name")

    def __init__(self, h, name):
        self.h = h
        self.v = 0
        self.name = name

    def inc(self, inst, k=1):
        inst.then_inc(self.h, k)
        self.v += k
        return inst


class Ctx:

    def __init__(self, nc, cfg, stk, io_decl):
        self.nc = nc
        self.cfg = cfg
        self.stk = stk
        self.phase_stk = None
        self.daps = {}
        self._sems = {}
        self._retired = []
        self._shared_sems = set()
        self.block = None
        self.done = {e: self.sem(f"done_{e}") for e in ENGINES}
        self._shared_sems = {f"done_{e}" for e in ENGINES}
        self._bar = {e: 0 for e in ENGINES}
        self.io_decl = io_decl
        self.sb_tensors = {}
        self._phase_i = 0
        self._hw_pool = []
        self._sw_pool = []
        self._hw_cur = 0
        self._sw_cur = 0
        self._dma_names = set()

    def _fresh(self, name):
        self._sem_n = getattr(self, "_sem_n", 0) + 1
        h = self.stk.enter_context(self.nc.semaphore(f"s{self._sem_n}_{name}"))
        return Sem(h, name)

    def sem(self, name, keep=False):
        if name not in self._sems:
            if keep:
                self._sems[name] = self._fresh(name)
                self._shared_sems.add(name)
            elif self._retired:
                sm = self._retired.pop()
                sm.name = name
                self._sems[name] = sm
            else:
                self._sems[name] = self._fresh(name)
        return self._sems[name]

    def dma_sem(self, name, sw=False):
        if name in self._sems:
            return self._sems[name]
        pool = self._sw_pool if sw else self._hw_pool
        cur = self._sw_cur if sw else self._hw_cur
        if cur < len(pool):
            sm = pool[cur]
            sm.name = name
        else:
            sm = self._fresh(name)
            pool.append(sm)
        if sw:
            self._sw_cur += 1
        else:
            self._hw_cur += 1
        self._sems[name] = sm
        self._dma_names.add(name)
        return sm

    def dram(self, name):
        if name not in self.daps:
            kind, shape, dt = self.io_decl[name]
            k = {"in": "ExternalInput", "out": "ExternalOutput",
                 "internal": "Internal"}[kind]
            self.daps[name] = self.nc.dram_tensor(name, list(shape), dt, kind=k).ap()
        return self.daps[name]

    def _alloc_stack(self):
        return self.phase_stk if self.phase_stk is not None else self.stk

    def sbuf(self, name, shape, dt):
        if name not in self.sb_tensors:
            self.sb_tensors[name] = self._alloc_stack().enter_context(
                self.nc.sbuf_tensor(f"sb{self._phase_i}_{name}", list(shape), dt))
        return self.sb_tensors[name]

    def psum(self, name, shape, dt):
        if name not in self.sb_tensors:
            self.sb_tensors[name] = self._alloc_stack().enter_context(
                self.nc.psum_tensor(f"ps{self._phase_i}_{name}", list(shape), dt))
        return self.sb_tensors[name]

    def begin_block(self):
        self.block = self.stk.enter_context(self.nc.Block())
        self.junk = self.stk.enter_context(
            self.nc.sbuf_tensor("sb_junk", [PAR, 8], F32))
        self.phase_stk = ExitStack()

    def sec(self, eng_name, fn):
        bar = dict(self._bar)
        done = self.done[eng_name]

        def body(eng):
            for e in ENGINES:
                if bar[e] > 0:
                    eng.wait_ge(self.done[e].h, bar[e])
            fn(eng)
            if eng_name == "tensor":
                done.inc(eng.nop())
            elif eng_name == "vector":
                done.inc(eng.memset(self.junk[0:1, 0:1], 0.0))
            elif eng_name == "gpsimd":
                done.inc(eng.memset(self.junk[0:1, 2:3], 0.0))
            elif eng_name == "scalar":
                done.inc(eng.memzero(self.junk[0:1, 1:2]))
            else:
                done.inc(eng.nop())

        getattr(self.block, eng_name)(body)

    def barrier(self):
        self._bar = {e: self.done[e].v for e in ENGINES}
        if self.phase_stk is not None:
            self.phase_stk.close()
            self.phase_stk = ExitStack()
        self.sb_tensors.clear()
        self._phase_i += 1
        for n in list(self._sems):
            if n in self._shared_sems:
                continue
            sm = self._sems.pop(n)
            if n in self._dma_names:
                continue
            self._retired.append(sm)
        self._dma_names.clear()
        self._hw_cur = 0
        self._sw_cur = 0

    def dma(self, eng, out, in_, sem, **kw):
        inst = eng.dma_start(out=out, in_=in_, **kw)
        sem.inc(inst, 16)
        return inst

    def w(self, eng, sem, v=None):
        eng.wait_ge(sem.h, sem.v if v is None else v)

    def slots(self, name, n, per=1, sw=False):
        return Slots(self, name, n, per, sw)


class Slots:

    def __init__(self, ctx, name, n, per=1, sw=False):
        self.ctx = ctx
        self.sems = [ctx.dma_sem(f"{name}_s{i}", sw=sw) for i in range(n)]
        self.base = [s.v for s in self.sems]
        self.n = n
        self.per = per

    def dma(self, eng, k, out, in_, **kw):
        self.ctx.dma(eng, out, in_, self.sems[k % self.n], **kw)

    def wait(self, eng, k):
        s = self.sems[k % self.n]
        eng.wait_ge(s.h, self.base[k % self.n] + 16 * self.per * (k // self.n + 1))

    def wait_all(self, eng, total_uses):
        for i, s in enumerate(self.sems):
            uses = (total_uses - i + self.n - 1) // self.n
            if uses > 0:
                eng.wait_ge(s.h, self.base[i] + 16 * self.per * uses)


def io_table(cfg):
    D, FF, R, NS = cfg.D, cfg.FF, cfg.R, cfg.NSLOT
    n, NT = cfg.n_own, cfg.NT
    t = {
        "x_in": ([NT, 128, D], F32),
        "cos0": ([64, n], F32), "sin0": ([64, n], F32),
        "masks1": ([8, 128, 512], F32),
        "wq0T": ([16, 128, 16, 128], BF16), "bq0": ([1, D], F32),
        "wk0T": ([16, 128, 16, 128], BF16), "bk0": ([1, D], F32),
        "wv0T": ([4, 128, 16, 512], BF16), "bv0": ([1, D], F32),
        "wo0T": ([16, 128, 16, 128], BF16), "bo0": ([1, D], F32),
        "wupTq": ([64, 128, 16, 128], I32), "scupT": ([64, 128, 16, 128], BF16),
        "bup": ([1, FF], F32),
        "upA": ([D, R], F32), "upB": ([R, FF], F32),
        "wdnTq": ([16, 128, 64, 128], I32), "scdnT": ([16, 128, 64, 128], BF16),
        "bdn": ([1, D], F32),
        "dnA": ([FF, R], F32), "dnB": ([R, D], F32),
        "wq1T": ([16, 128, 16, 128], BF16), "wk1T": ([1, 128, 16, 128], BF16),
        "wv1T": ([1, 128, 16, 128], BF16),
        "wo1T": ([16, 128, 16, 128], BF16),
        "gamma": ([D, 1], F32), "beta": ([D, 1], F32), "mu": ([D, NS], F32),
        "moescale": ([1, 1], F32),
        "W1": ([16, D, 512], BF16), "b1": ([16, 512], F32),
        "W2": ([16, 512, D], BF16), "b2": ([16, D], BF16),
        "cos1": ([64, 16], F32), "sin1": ([64, 16], F32),
        "ident": ([128, 128], F32),
        "xres": ([D, n], F32),
        "xres1": ([D, n], F32),
        "xres2": ([D, n], F32),
        "ctb": ([NS, n], BF16),
        "xn": ([PAR, D // 128, n], BF16),
        "varrow": ([1, n], F32),
        "qfm": ([D, n], BF16),
        "kown0": ([D, n], BF16), "vown0": ([16, PAR, NT, 128], BF16),
        "kg0": ([2, D, n], BF16), "vg0": ([2, PAR, NT, 16, 128], BF16),
        "kown1": ([128, n], BF16), "vown1": ([PAR, NT, 128], BF16),
        "kg1": ([2, 128, n], BF16), "vg1": ([2, PAR, NT, 128], BF16),
        "qn2": ([1, 16], F32), "kn2own": ([1, 16], F32), "kn2": ([1, 16], F32),
        "sfm": ([D, n], F32),
        "x2b": ([PAR, n // 512, FF // 128, 512], BF16),
        "wdnbf": ([16, 128, FF], BF16),
        "gfm": ([D, n], F32),
        "mun": ([D, NS], BF16),
        "Ebuf": ([n, NS], BF16),
        "xlnT": ([n, D], BF16),
        "rowZ": ([128, NT], F32),
        "Uown": ([NS, 2052], F32),
        "Usum": ([NS, 2052], F32),
        "out": ([NT, 128, D], F32),
    }
    return t



def ph_in(ctx):
    cfg = ctx.cfg
    x_in, xres, ident = ctx.dram("x_in"), ctx.dram("xres"), ctx.dram("ident")
    xt = ctx.sbuf("ph_in_xt", [PAR, 2, cfg.D], F32)
    xo = ctx.sbuf("ph_in_xo", [PAR, 2, 4, 128], F32)
    idf = ctx.sbuf("idf32", [PAR, 128], F32)
    pB = ctx.psum("pB", [PAR, 1024], F32)
    ldi = ctx.dma_sem("in_ldi")
    xts = ctx.slots("in_xt", 2)
    sts = ctx.slots("in_st", 2, per=4)
    trd = ctx.sem("in_tr"); evd = ctx.sem("in_ev")
    b_ldi, b_tr, b_ev = ldi.v, trd.v, evd.v
    NT, DC = cfg.NT, cfg.DC
    NG = DC // 4

    def f_sync(sync):
        ctx.dma(sync, idf[:], ident, ldi)
        for t in range(NT):
            if t >= 2:
                sync.wait_ge(trd.h, b_tr + (t - 1) * NG)
            xts.dma(sync, t, xt[:, t % 2, :], x_in[t])
            for g in range(NG):
                k = t * NG + g
                sync.wait_ge(evd.h, b_ev + k + 1)
                for j in range(4):
                    dc = g * 4 + j
                    sts.dma(sync, k, xres[dc * 128:(dc + 1) * 128,
                                          t * 128:(t + 1) * 128], xo[:, k % 2, j, :])
        sts.wait_all(sync, NT * NG)
        xts.wait_all(sync, NT)
        ctx.w(sync, ldi)

    def f_tensor(tensor):
        tensor.wait_ge(ldi.h, b_ldi + 16)
        for t in range(NT):
            xts.wait(tensor, t)
            for g in range(NG):
                k = t * NG + g
                if k >= 2:
                    tensor.wait_ge(evd.h, b_ev + k - 1)
                half = (k % 2) * 512
                for j in range(4):
                    dc = g * 4 + j
                    inst = tensor.transpose(
                        pB[:, half + j * 128:half + (j + 1) * 128],
                        xt[:, t % 2, dc * 128:(dc + 1) * 128], idf[:])
                trd.inc(inst)
        ctx.w(tensor, trd)

    def f_vector(vector):
        for t in range(NT):
            for g in range(NG):
                k = t * NG + g
                vector.wait_ge(trd.h, b_tr + k + 1)
                if k >= 2:
                    sts.wait(vector, k - 2)
                half = (k % 2) * 512
                inst = vector.tensor_copy(out=xo[:, k % 2], in_=pB[:, half:half + 512])
                evd.inc(inst)

    ctx.sec("sync", f_sync)
    ctx.sec("tensor", f_tensor)
    ctx.sec("vector", f_vector)
    ctx.barrier()


def ph_out(ctx, src="xres2"):
    cfg = ctx.cfg
    xres, out, ident = ctx.dram(src), ctx.dram("out"), ctx.dram("ident")
    xi = ctx.sbuf("ph_out_xi", [PAR, 2, cfg.DC, 128], F32)
    xo = ctx.sbuf("ph_out_xo", [PAR, 2, cfg.D], F32)
    idf = ctx.sbuf("idf32", [PAR, 128], F32)
    pB = ctx.psum("pB", [PAR, 1024], F32)
    ldi = ctx.dma_sem("out_ldi")
    xis = ctx.slots("out_xi", 2)
    sts = ctx.slots("out_st", 2)
    trd = ctx.sem("out_tr"); evd = ctx.sem("out_ev")
    b_ldi, b_tr, b_ev = ldi.v, trd.v, evd.v
    NT, DC = ctx.cfg.NT, ctx.cfg.DC
    NG = DC // 4

    def f_sync(sync):
        ctx.dma(sync, idf[:], ident, ldi)
        for t in range(NT):
            if t >= 2:
                sync.wait_ge(trd.h, b_tr + (t - 1) * NG)
            xis.dma(sync, t, xi[:, t % 2],
                    xres[:, t * 128:(t + 1) * 128].rearrange("(dc p) n -> p dc n", p=PAR))
            sync.wait_ge(evd.h, b_ev + (t + 1) * NG)
            sts.dma(sync, t, out[t], xo[:, t % 2])
        sts.wait_all(sync, NT)
        xis.wait_all(sync, NT)
        ctx.w(sync, ldi)

    def f_tensor(tensor):
        tensor.wait_ge(ldi.h, b_ldi + 16)
        for t in range(NT):
            xis.wait(tensor, t)
            for g in range(NG):
                k = t * NG + g
                if k >= 2:
                    tensor.wait_ge(evd.h, b_ev + k - 1)
                half = (k % 2) * 512
                for j in range(4):
                    dc = g * 4 + j
                    inst = tensor.transpose(
                        pB[:, half + j * 128:half + (j + 1) * 128],
                        xi[:, t % 2, dc, :], idf[:])
                trd.inc(inst)
        ctx.w(tensor, trd)

    def f_vector(vector):
        for t in range(NT):
            for g in range(NG):
                k = t * NG + g
                vector.wait_ge(trd.h, b_tr + k + 1)
                if g == 0 and t >= 2:
                    sts.wait(vector, t - 2)
                half = (k % 2) * 512
                inst = vector.tensor_copy(
                    out=xo[:, t % 2, g * 512:(g + 1) * 512], in_=pB[:, half:half + 512])
                evd.inc(inst)

    ctx.sec("sync", f_sync)
    ctx.sec("tensor", f_tensor)
    ctx.sec("vector", f_vector)
    ctx.barrier()


def ph_ln(ctx, src):
    cfg = ctx.cfg
    n, DC = cfg.n_own, cfg.DC
    D = cfg.D
    xres, xn, varrow = ctx.dram(src), ctx.dram("xn"), ctx.dram("varrow")
    ch = ctx.sbuf("ln_ch", [PAR, 2, n], F32)
    xbf = ctx.sbuf("ln_xbf", [PAR, 2, n], BF16)
    xsq = ctx.sbuf("ln_xsq", [PAR, 2, n], BF16)
    rows = ctx.sbuf("ln_rows", [1, 8, n], F32)
    ones_bf = ctx.sbuf("ones_bf_col", [PAR, 1], BF16)
    ones_f1 = ctx.sbuf("ones_f32_row", [1, PAR], F32)
    Ab = ctx.sbuf("ln_Ab", [PAR, n], F32)
    Bb = ctx.sbuf("ln_Bb", [PAR, n], F32)
    tmp = ctx.sbuf("ln_tmp", [PAR, 2, n], F32)
    xns = ctx.sbuf("ln_xns", [PAR, 2, n], BF16)
    pA = ctx.psum("pA", [PAR, 1024], F32)
    pB = ctx.psum("pB", [PAR, 1024], F32)

    def pstat(c):
        return (pA if c < 2 else pB), (c % 2) * 512
    NC512 = n // 512

    chs = ctx.slots("ln_ch", 2)
    sts = ctx.slots("ln_xns", 2)
    stv = ctx.dma_sem("ln_stv")
    cst = ctx.sem("ln_cst"); sqd = ctx.sem("ln_sq")
    mmd = ctx.sem("ln_mm"); evs = ctx.sem("ln_ev"); rowd = ctx.sem("ln_row")
    bcd = ctx.sem("ln_bc"); bce = ctx.sem("ln_bce"); p2 = ctx.sem("ln_p2")
    cv = ctx.sem("chain_v")
    B = {s.name: s.v for s in (cst, sqd, mmd, evs, rowd, bcd, bce, p2, stv)}
    vin = ctx.sem("ln_vin")
    B["ln_vin"] = vin.v

    def f_sync(sync):
        for dc in range(DC):
            if dc >= 2:
                sync.wait_ge(cst.h, B["ln_cst"] + dc - 1)
                sync.wait_ge(sqd.h, B["ln_sq"] + dc - 1)
            chs.dma(sync, dc, ch[:, dc % 2], xres[dc * 128:(dc + 1) * 128, :])
        sync.wait_ge(rowd.h, B["ln_row"] + 1)
        ctx.dma(sync, varrow, rows[0:1, 3], stv)
        for dc in range(DC):
            if dc >= 2:
                sync.wait_ge(p2.h, B["ln_p2"] + 2 * (dc - 1))
            sync.wait_ge(bce.h, B["ln_bce"] + 2)
            chs.dma(sync, DC + dc, ch[:, dc % 2], xres[dc * 128:(dc + 1) * 128, :])
            sync.wait_ge(p2.h, B["ln_p2"] + 2 * (dc + 1))
            sts.dma(sync, dc, xn[:, dc, :], xns[:, dc % 2])
        sts.wait_all(sync, DC)
        ctx.w(sync, stv)
        chs.wait_all(sync, 2 * DC)

    def f_vector(vector):
        vector.memset(ones_bf[:], 1.0)
        vin.inc(vector.memset(ones_f1[:], 1.0))
        for dc in range(DC):
            chs.wait(vector, dc)
            if dc >= 2:
                vector.wait_ge(mmd.h, B["ln_mm"] + dc - 1)
            cst.inc(vector.tensor_copy(out=xbf[:, dc % 2], in_=ch[:, dc % 2]))
        vector.wait_ge(mmd.h, B["ln_mm"] + DC)
        for c in range(NC512):
            pt, po = pstat(c)
            vector.tensor_copy(out=rows[0:1, 0, c * 512:(c + 1) * 512],
                               in_=pt[0:1, po:po + 512])
            inst = vector.tensor_copy(out=rows[0:1, 1, c * 512:(c + 1) * 512],
                                      in_=pt[32:33, po:po + 512])
        evs.inc(inst)
        ctx.w(vector, evs)
        cv.inc(vector.tensor_scalar_mul(out=rows[0:1, 2], in0=rows[0:1, 0],
                                        scalar1=1.0 / D))
        cv.inc(vector.tensor_scalar_mul(out=rows[0:1, 3], in0=rows[0:1, 1],
                                        scalar1=1.0 / D))
        ctx.w(vector, cv)
        cv.inc(vector.tensor_mul(out=rows[0:1, 5], in0=rows[0:1, 2], in1=rows[0:1, 2]))
        ctx.w(vector, cv)
        rowd.inc(vector.tensor_sub(out=rows[0:1, 3], in0=rows[0:1, 3], in1=rows[0:1, 5]))
        ctx.w(vector, rowd)
        rowd.inc(vector.tensor_scalar_add(out=rows[0:1, 6], in0=rows[0:1, 3],
                                          scalar1=1e-5))
        vector.wait_ge(sqd.h, B["ln_sq"] + DC + 1)
        cv.inc(vector.reciprocal(out=rows[0:1, 4], in_=rows[0:1, 4]))
        ctx.w(vector, cv)
        rowd.inc(vector.scalar_tensor_tensor(
            out=rows[0:1, 5], in0=rows[0:1, 2], scalar=-1.0, in1=rows[0:1, 4],
            op0=ALU.mult, op1=ALU.mult))
        vector.wait_ge(bcd.h, B["ln_bc"] + 1)
        for c in range(NC512):
            pt, po = pstat(c)
            inst = vector.tensor_copy(out=Ab[:, c * 512:(c + 1) * 512],
                                      in_=pt[:, po:po + 512])
        bce.inc(inst)
        vector.wait_ge(bcd.h, B["ln_bc"] + 2)
        for c in range(NC512):
            pt, po = pstat(c)
            inst = vector.tensor_copy(out=Bb[:, c * 512:(c + 1) * 512],
                                      in_=pt[:, po:po + 512])
        bce.inc(inst)
        ctx.w(vector, bce)
        for dc in range(DC):
            chs.wait(vector, DC + dc)
            p2.inc(vector.tensor_mul(out=tmp[:, dc % 2], in0=ch[:, dc % 2], in1=Ab[:]))
            if dc >= 2:
                sts.wait(vector, dc - 2)
            ctx.w(vector, p2)
            p2.inc(vector.tensor_add(out=xns[:, dc % 2], in0=tmp[:, dc % 2], in1=Bb[:]))

    def f_scalar(scalar):
        for dc in range(DC):
            chs.wait(scalar, dc)
            if dc >= 2:
                scalar.wait_ge(mmd.h, B["ln_mm"] + dc - 1)
            sqd.inc(scalar.activation(out=xsq[:, dc % 2], in_=ch[:, dc % 2],
                                      func=AF.Square))
        scalar.wait_ge(rowd.h, B["ln_row"] + 2)
        sqd.inc(scalar.activation(out=rows[0:1, 4], in_=rows[0:1, 6],
                                  func=AF.Sqrt))

    def f_tensor(tensor):
        tensor.wait_ge(vin.h, B["ln_vin"] + 1)
        for dc in range(DC):
            tensor.wait_ge(cst.h, B["ln_cst"] + dc + 1)
            tensor.wait_ge(sqd.h, B["ln_sq"] + dc + 1)
            for c in range(NC512):
                pt, po = pstat(c)
                tensor.matmul(pt[0:1, po:po + 512], ones_bf[:],
                              xbf[:, dc % 2, c * 512:(c + 1) * 512],
                              start=(dc == 0), stop=(dc == DC - 1))
            for c in range(NC512):
                pt, po = pstat(c)
                inst = tensor.matmul(pt[32:33, po:po + 512], ones_bf[:],
                                     xsq[:, dc % 2, c * 512:(c + 1) * 512],
                                     start=(dc == 0), stop=(dc == DC - 1))
            mmd.inc(inst)
        tensor.wait_ge(evs.h, B["ln_ev"] + 1)
        tensor.wait_ge(rowd.h, B["ln_row"] + 3)
        for c in range(NC512):
            pt, po = pstat(c)
            inst = tensor.matmul(pt[:, po:po + 512], ones_f1[:],
                                 rows[0:1, 4, c * 512:(c + 1) * 512],
                                 start=True, stop=True)
        bcd.inc(inst)
        tensor.wait_ge(bce.h, B["ln_bce"] + 1)
        for c in range(NC512):
            pt, po = pstat(c)
            inst = tensor.matmul(pt[:, po:po + 512], ones_f1[:],
                                 rows[0:1, 5, c * 512:(c + 1) * 512],
                                 start=True, stop=True)
        bcd.inc(inst)
        ctx.w(tensor, bcd)

    ctx.sec("sync", f_sync)
    ctx.sec("vector", f_vector)
    ctx.sec("scalar", f_scalar)
    ctx.sec("tensor", f_tensor)
    ctx.barrier()


def build_launch(cfg, items, keep_external=()):
    nc = bass.Bass("TRN2", target_bir_lowering=False, debug=False)
    io = {}
    written = set()
    for (_, reads, writes, _f) in items:
        for r in reads:
            if r not in written:
                io.setdefault(r, "in")
        for w in writes:
            written.add(w)
            io[w] = "out"
    if len(items) > 1:
        for w in written:
            if w != "out" and w not in keep_external:
                io[w] = "internal"
    tab = io_table(cfg)
    decl = {n: (io[n], tab[n][0], tab[n][1]) for n in io}
    with ExitStack() as stk:
        ctx = Ctx(nc, cfg, stk, decl)
        for n in sorted(decl):
            ctx.dram(n)
        ctx.begin_block()
        for (pname, _r, _w, fn) in items:
            with nc.named_scope(pname):
                fn(ctx)
        if ctx.phase_stk is not None:
            ctx.phase_stk.close()
            ctx.phase_stk = None
    ins = sorted(n for n, k in io.items() if k == "in")
    outs = sorted(n for n, k in io.items() if k == "out")
    return nc, ins, outs


def run_phases(cfg, state, groups, trace=False):
    core_ids = list(range(cfg.n_cores))
    import os
    dbg = os.environ.get("KPH_DEBUG")
    for gi, items in enumerate(groups):
        later_reads = {r for g2 in groups[gi + 1:] for it in g2 for r in it[1]}
        if dbg:
            print(f"[run_phases] launching {[it[0] for it in items]}", flush=True)
        nc, reads, writes = build_launch(cfg, items, keep_external=later_reads)
        in_maps = []
        for c in range(cfg.n_cores):
            m = {}
            for r in reads:
                v = state[r]
                m[r] = v[c] if isinstance(v, list) else v
            in_maps.append(m)
        trace = bool(os.environ.get("KPH_TRACE"))
        res = run_bass_kernel_spmd(nc, in_maps, core_ids=core_ids, trace=trace)
        if trace:
            print(f"[trace] exec_time_ns={res.exec_time_ns} "
                  f"mean={res.mean_exec_time_ns}", flush=True)
            if res.per_core_scope_times:
                for scope, t in sorted(res.per_core_scope_times.items()):
                    print(f"[trace] {scope}: {t}", flush=True)
        for w in writes:
            state[w] = [np.asarray(res.results[c][w]) for c in range(cfg.n_cores)]


def phase_list(cfg):
    P = []
    P.append(("IN", ["x_in", "ident"], ["xres"], ph_in))
    P.append(("WPREP0", ["wdnTq", "scdnT"], ["wdnbf"], ph_wprep))
    P.append(("LN0", ["xres"], ["xn", "varrow"], lambda c: ph_ln(c, "xres")))
    P.append(("QKV0",
              ["xn", "wq0T", "bq0", "wk0T", "bk0", "wv0T", "bv0", "cos0", "sin0"],
              ["qfm", "kown0", "vown0", "qn2", "kn2own"],
              lambda c: ph_qkv(c, 0)))
    P.append(("ATT0",
              ["qfm", "kown0", "vown0", "kn2own", "qn2", "wo0T", "bo0", "ident"],
              ["sfm"], lambda c: ph_att(c, 0)))
    P.append(("FFNA0", ["xn", "wupTq", "scupT", "bup", "upA", "upB"],
              ["x2b"], ph_ffna))
    P.append(("FFNB0", ["x2b", "wdnbf", "bdn", "dnA", "dnB", "xres", "sfm"],
              ["xres1"], lambda c: ph_ffnb(c, "xres", "xres1")))
    P.append(("LN1", ["xres1"], ["xn", "varrow"], lambda c: ph_ln(c, "xres1")))
    P.append(("QKV1", ["xn", "wq1T", "wk1T", "wv1T", "cos1", "sin1"],
              ["qfm", "kown1", "vown1", "qn2", "kn2own"],
              lambda c: ph_qkv(c, 1)))
    P.append(("GATHER1", ["kown1", "vown1", "kn2own"],
              ["kg1", "vg1", "kn2"], lambda c: ph_gather(c, 1)))
    P.append(("ATT1",
              ["qfm", "kg1", "vg1", "qn2", "kn2", "wo1T", "masks1", "ident"],
              ["sfm"], lambda c: ph_att(c, 1)))
    P.append(("MUPREP", ["mu"], ["mun"], ph_muprep))
    P.append(("MOEA",
              ["xn", "varrow", "gamma", "beta", "mun", "moescale", "ident"],
              ["Ebuf", "xlnT", "rowZ", "Usum"], ph_moea))
    P.append(("MOEB",
              ["Usum", "Ebuf", "rowZ", "W1", "b1", "W2", "b2", "ident"],
              ["ctb", "gfm"], ph_moeb))
    P.append(("GATE1", ["xres1", "sfm", "gfm"], ["xres2"],
              lambda c: ph_gate(c, "xres1", "xres2")))
    P.append(("OUT", ["xres2", "ident"], ["out"], lambda c: ph_out(c, "xres2")))
    return P


def kernel(**inputs):
    import os
    cfg = Cfg()
    sh, percore = host_prepare(cfg, inputs)
    state = dict(sh)
    for k in percore[0]:
        state[k] = [pc[k] for pc in percore]
    plist = phase_list(cfg)
    if os.environ.get("KPH_UNFUSED"):
        groups = [[p] for p in plist]
    elif os.environ.get("KPH_SPLIT"):
        cuts = [int(x) for x in os.environ["KPH_SPLIT"].split(",")]
        groups = []
        prev = 0
        for c in cuts + [len(plist)]:
            if c > prev:
                groups.append(plist[prev:c])
            prev = c
    else:
        groups = [plist]
    run_phases(cfg, state, groups)
    D = cfg.D
    full = np.zeros((cfg.B, cfg.S, D), np.float32)
    for c in range(cfg.n_cores):
        b, par = c // 2, c % 2
        tiles = [2 * i + par for i in range(cfg.NT)]
        full[b].reshape(cfg.NT_TOT, 128, D)[tiles] = state["out"][c]
    return full


def ph_qkv(ctx, layer):
    cfg = ctx.cfg
    DC, NQC, n = cfg.DC, cfg.NQC, cfg.n_own
    XN = ctx.sbuf("XN", [PAR, DC, n], BF16)
    wblk = ctx.sbuf("qkv_wblk", [PAR, 3, DC, 128], BF16)
    wvblk = ctx.sbuf("qkv_wvblk", [PAR, 2, DC, 512], BF16)
    qstage = ctx.sbuf("qkv_qstage", [PAR, 4, 512], BF16)
    vstage = ctx.sbuf("qkv_vstage", [PAR, 4, 512], BF16)
    sqt = ctx.sbuf("qkv_sqt", [PAR, 4, 512], BF16)
    t_a = ctx.sbuf("qkv_ta", [64, 512], F32)
    t_b = ctx.sbuf("qkv_tb", [64, 512], F32)
    t_a2 = ctx.sbuf("qkv_ta2", [64, 512], F32)
    t_b2 = ctx.sbuf("qkv_tb2", [64, 512], F32)
    cos_sb = ctx.sbuf("qkv_cos", [64, n], F32)
    sin_sb = ctx.sbuf("qkv_sin", [64, n], F32)
    cs1_sb = ctx.sbuf("qkv_cs1", [64, 32], F32)
    brow = ctx.sbuf("qkv_brow", [1, 3, cfg.D], BF16)
    ones_r512 = ctx.sbuf("ones_r512", [1, 512], BF16)
    ones_r128 = ctx.sbuf("ones_r128", [1, PAR], BF16)
    ones_c128 = ctx.sbuf("ones_c128", [PAR, 1], BF16)
    nrow = ctx.sbuf("qkv_nrow", [1, 2, 16], F32)
    ntmp = ctx.sbuf("qkv_ntmp", [1, 2], F32)
    pA = ctx.psum("pA", [PAR, 1024], F32)
    pB = ctx.psum("pB", [PAR, 1024], F32)
    pD = ctx.psum("pD", [PAR, 512], F32)
    pE = ctx.psum("pE", [PAR, 512], F32)

    xn_d, qfm = ctx.dram("xn"), ctx.dram("qfm")
    if layer == 0:
        kown, vown = ctx.dram("kown0"), ctx.dram("vown0")
        wq_d, wk_d, wv_d = ctx.dram("wq0T"), ctx.dram("wk0T"), ctx.dram("wv0T")
    else:
        kown, vown = ctx.dram("kown1"), ctx.dram("vown1")
        wq_d, wk_d, wv_d = ctx.dram("wq1T"), ctx.dram("wk1T"), ctx.dram("wv1T")
    qn2_d, kn2_d = ctx.dram("qn2"), ctx.dram("kn2own")

    ld = ctx.dma_sem("qkv_ld")
    gld = ctx.dma_sem("qkv_gld", sw=True)
    ws = ctx.slots("qkv_w", 3)
    wvs = ctx.slots("qkv_wv", 2)
    sts = ctx.slots("qkv_st", 4)
    stvs = ctx.slots("qkv_stv", 4)
    stn = ctx.dma_sem("qkv_stn")
    vin = ctx.sem("qkv_vin")
    mmq = ctx.sem("qkv_mmq")
    rv = ctx.sem("qkv_rv")
    rv2 = ctx.sem("qkv_rv2")
    rs = ctx.sem("qkv_rs")
    sqd = ctx.sem("qkv_sqd")
    nr = ctx.sem("qkv_nr")
    nf = ctx.sem("qkv_nf")
    wdn = ctx.sem("qkv_wdn")
    mmv = ctx.sem("qkv_mmv")
    ve = ctx.sem("qkv_ve")
    cv = ctx.sem("chain_v")
    sems = (ld, gld, vin, mmq, rv, rv2, rs, sqd, nr, nf, wdn, mmv, ve, stn)
    B = {s.name: s.v for s in sems}
    L0 = layer == 0

    QO = 16
    KO = 16 if L0 else 1
    NI = (QO + KO) * NQC
    VC = 4 if L0 else 1
    VN = cfg.NT
    NJ = VC * VN

    def kind_of(i):
        if i < QO * NQC:
            return ("q", i // NQC, i % NQC)
        j = i - QO * NQC
        return ("k", j // NQC, j % NQC)

    def f_sync(sync):
        ctx.dma(sync, XN[:], xn_d, ld)
        if L0:
            ctx.dma(sync, cos_sb[:], ctx.dram("cos0"), ld)
            ctx.dma(sync, sin_sb[:], ctx.dram("sin0"), ld)
        else:
            ctx.dma(sync, cs1_sb[:, 0:16], ctx.dram("cos1"), ld)
            ctx.dma(sync, cs1_sb[:, 16:32], ctx.dram("sin1"), ld)
        for i in range(NI):
            kind, oc, ncl = kind_of(i)
            sync.wait_ge(rv2.h, B["qkv_rv2"] + i + 1)
            if L0:
                sync.wait_ge(rs.h, B["qkv_rs"] + i + 1)
            dst = qfm if kind == "q" else kown
            if L0 or kind == "q":
                out_ap = dst[oc * 128:(oc + 1) * 128, ncl * 512:(ncl + 1) * 512]
            else:
                out_ap = dst[:, ncl * 512:(ncl + 1) * 512]
            sts.dma(sync, i, out_ap, qstage[:, i % 4, :])
        for j in range(NJ):
            vc, nt = j // VN, j % VN
            sync.wait_ge(ve.h, B["qkv_ve"] + j + 1)
            if L0:
                out_ap = vown[4 * vc:4 * vc + 4, :, nt, :].rearrange(
                    "h p m -> p h m")
                stvs.dma(sync, j, out_ap,
                         vstage[:, j % 4].rearrange("p (h m) -> p h m", m=128))
            else:
                stvs.dma(sync, j, vown[:, nt, :], vstage[:, j % 4, 0:128])
        sync.wait_ge(nf.h, B["qkv_nf"] + NI)
        ctx.dma(sync, qn2_d, nrow[0:1, 0], stn)
        ctx.dma(sync, kn2_d, nrow[0:1, 1], stn)
        sts.wait_all(sync, NI)
        stvs.wait_all(sync, NJ)
        ctx.w(sync, stn)
        ctx.w(sync, ld)

    def load_w(eng, wi):
        kind = "q" if wi < QO else "k"
        oc = wi if wi < QO else wi - QO
        wd = wq_d if kind == "q" else wk_d
        src = wd[oc] if (L0 or kind == "q") else wd[0]
        ws.dma(eng, wi, wblk[:, wi % 3], src)

    def f_gpsimd(gpsimd):
        if L0:
            ctx.dma(gpsimd, brow[0:1, 0], ctx.dram("bq0"), gld)
            ctx.dma(gpsimd, brow[0:1, 1], ctx.dram("bk0"), gld)
            ctx.dma(gpsimd, brow[0:1, 2], ctx.dram("bv0"), gld)
        ctx.w(gpsimd, gld)

    def f_tensor(tensor):
        tensor.wait_ge(ld.h, B["qkv_ld"] + 48)
        tensor.wait_ge(vin.h, B["qkv_vin"] + 1)
        if L0:
            tensor.wait_ge(gld.h, B["qkv_gld"] + 48)

        def norm_mm(k):
            tensor.wait_ge(sqd.h, B["qkv_sqd"] + k + 1)
            if k >= 2:
                tensor.wait_ge(nf.h, B["qkv_nf"] + k - 1)
            npt = pD if k % 2 == 0 else pE
            nr.inc(tensor.matmul(npt[0:1, 0:512], ones_c128[:],
                                 sqt[:, k % 4, :], start=True, stop=True))

        for i in range(NI):
            kind, oc, ncl = kind_of(i)
            wi = oc if kind == "q" else QO + oc
            ws.wait(tensor, wi)
            if i >= 4:
                if L0:
                    tensor.wait_ge(rs.h, B["qkv_rs"] + i - 3)
                else:
                    tensor.wait_ge(rv.h, B["qkv_rv"] + i - 3)
            pq = (pA, pB)[(i % 4) // 2]
            off = (i % 2) * 512
            for kc in range(DC):
                inst = tensor.matmul(pq[:, off:off + 512], wblk[:, wi % 3, kc, :],
                                     XN[:, kc, ncl * 512:(ncl + 1) * 512],
                                     start=(kc == 0), stop=False)
            if ncl == NQC - 1:
                wdn.inc(inst)
            bi = 0 if kind == "q" else 1
            mmq.inc(tensor.matmul(pq[:, off:off + 512],
                                  brow[0:1, bi, oc * 128:(oc + 1) * 128],
                                  ones_r512[:], start=False, stop=True))
            if i >= 2:
                norm_mm(i - 2)
        norm_mm(NI - 2)
        norm_mm(NI - 1)
        tensor.wait_ge(rv2.h, B["qkv_rv2"] + NI)
        if L0:
            tensor.wait_ge(rs.h, B["qkv_rs"] + NI)
        tensor.wait_ge(sqd.h, B["qkv_sqd"] + NI)
        for j in range(NJ):
            vc, nt = j // VN, j % VN
            wvs.wait(tensor, vc)
            if j >= 4:
                tensor.wait_ge(ve.h, B["qkv_ve"] + j - 3)
            w = 512 if L0 else 128
            pq = (pA, pB)[(j % 4) // 2]
            off = (j % 2) * 512
            for kc in range(DC):
                inst = tensor.matmul(pq[:, off:off + w],
                                     XN[:, kc, nt * 128:(nt + 1) * 128],
                                     wvblk[:, vc % 2, kc, 0:w],
                                     start=(kc == 0), stop=False)
            if nt == VN - 1:
                wdn.inc(inst)
            mmv.inc(tensor.matmul(pq[:, off:off + w], ones_r128[:],
                                  brow[0:1, 2, vc * 512:vc * 512 + w],
                                  start=False, stop=True))
        ctx.w(tensor, mmv)

    def f_vector(vector):
        vector.memset(ones_r512[:], 1.0)
        vector.memset(ones_r128[:], 1.0)
        vector.memset(ones_c128[:], 1.0)
        if not L0:
            vector.memset(brow[:], 0.0)
        vin.inc(vector.memset(nrow[:], 0.0))
        vector.wait_ge(ld.h, B["qkv_ld"] + 48)
        for i in range(NI):
            kind, oc, ncl = kind_of(i)
            vector.wait_ge(mmq.h, B["qkv_mmq"] + i + 1)
            pq = (pA, pB)[(i % 4) // 2]
            off = (i % 2) * 512
            if i >= 4:
                sts.wait(vector, i - 4)
                vector.wait_ge(sqd.h, B["qkv_sqd"] + i - 3)
            if L0:
                cv.inc(vector.tensor_tensor(out=t_a[:], in0=pq[0:64, off:off + 512],
                                            in1=cos_sb[:, ncl * 512:(ncl + 1) * 512],
                                            op=ALU.mult))
                rv.inc(vector.tensor_tensor(out=t_b[:], in0=pq[64:128, off:off + 512],
                                            in1=sin_sb[:, ncl * 512:(ncl + 1) * 512],
                                            op=ALU.mult))
                ctx.w(vector, cv)
                ctx.w(vector, rv)
                rv2.inc(vector.tensor_sub(out=qstage[0:64, i % 4, :],
                                          in0=t_a[:], in1=t_b[:]))
            elif kind == "q":
                c_col = cs1_sb[0:64, oc:oc + 1]
                s_col = cs1_sb[0:64, 16 + oc:16 + oc + 1]
                cv.inc(vector.tensor_scalar_mul(out=t_a[:], in0=pq[0:64, off:off + 512],
                                                scalar1=c_col))
                cv.inc(vector.tensor_scalar_mul(out=t_b[:], in0=pq[64:128, off:off + 512],
                                                scalar1=s_col))
                ctx.w(vector, cv)
                vector.tensor_sub(out=qstage[0:64, i % 4, :], in0=t_a[:], in1=t_b[:])
                cv.inc(vector.tensor_scalar_mul(out=t_a2[:], in0=pq[0:64, off:off + 512],
                                                scalar1=s_col))
                rv.inc(vector.tensor_scalar_mul(out=t_b2[:], in0=pq[64:128, off:off + 512],
                                                scalar1=c_col))
                ctx.w(vector, cv)
                ctx.w(vector, rv)
                rv2.inc(vector.tensor_add(out=qstage[64:128, i % 4, :],
                                          in0=t_a2[:], in1=t_b2[:]))
            else:
                inst = vector.tensor_copy(out=qstage[:, i % 4, :],
                                          in_=pq[:, off:off + 512])
                rv.inc(inst)
                ctx.w(vector, rv)
                rv2.inc(vector.tensor_copy(out=qstage[0:1, i % 4, 0:1],
                                           in_=qstage[0:1, i % 4, 0:1]))
            vector.wait_ge(nr.h, B["qkv_nr"] + i + 1)
            npt = pD if i % 2 == 0 else pE
            col = 0 if kind == "q" else 1
            cv.inc(vector.reduce_max(out=ntmp[0:1, col:col + 1], in_=npt[0:1, 0:512],
                                     axis=mybir.AxisListType.X))
            ctx.w(vector, cv)
            if i == 0:
                ctx.w(vector, vin)
            else:
                vector.wait_ge(nf.h, B["qkv_nf"] + i)
            nf.inc(vector.tensor_max(out=nrow[0:1, col, oc:oc + 1],
                                     in0=nrow[0:1, col, oc:oc + 1],
                                     in1=ntmp[0:1, col:col + 1]))
        for j in range(NJ):
            vector.wait_ge(mmv.h, B["qkv_mmv"] + j + 1)
            if j >= 4:
                stvs.wait(vector, j - 4)
            w = 512 if L0 else 128
            pq = (pA, pB)[(j % 4) // 2]
            off = (j % 2) * 512
            ve.inc(vector.tensor_copy(out=vstage[:, j % 4, 0:w],
                                      in_=pq[:, off:off + w]))

    def f_scalar(scalar):
        for wi in range(min(3, QO + KO)):
            load_w(scalar, wi)
        for i in range(NI):
            kind, oc, ncl = kind_of(i)
            pq = (pA, pB)[(i % 4) // 2]
            off = (i % 2) * 512
            if L0:
                scalar.wait_ge(rv.h, B["qkv_rv"] + i + 1)
                if i >= 4:
                    sts.wait(scalar, i - 4)
                rs.inc(scalar.copy(out=qstage[64:128, i % 4, :],
                                   in_=pq[64:128, off:off + 512]))
            scalar.wait_ge(rv2.h, B["qkv_rv2"] + i + 1)
            if L0:
                scalar.wait_ge(rs.h, B["qkv_rs"] + i + 1)
            sqd.inc(scalar.activation(out=sqt[:, i % 4, :], in_=qstage[:, i % 4, :],
                                      func=AF.Square))
            if i % NQC == NQC - 1:
                wi = i // NQC + 3
                if wi < QO + KO:
                    scalar.wait_ge(wdn.h, B["qkv_wdn"] + i // NQC + 1)
                    load_w(scalar, wi)
        for vc in range(VC):
            if vc >= 2:
                scalar.wait_ge(wdn.h, B["qkv_wdn"] + QO + KO + vc - 1)
            w = 512 if L0 else 128
            src = wv_d[vc] if L0 else wv_d[0]
            wvs.dma(scalar, vc, wvblk[:, vc % 2, :, 0:w], src[:, :, 0:w])
        ws.wait_all(scalar, QO + KO)
        wvs.wait_all(scalar, VC)

    ctx.sec("sync", f_sync)
    ctx.sec("gpsimd", f_gpsimd)
    ctx.sec("tensor", f_tensor)
    ctx.sec("vector", f_vector)
    ctx.sec("scalar", f_scalar)
    ctx.barrier()


def ph_gather(ctx, layer):
    cfg = ctx.cfg
    n = cfg.n_own
    KD = cfg.D if layer == 0 else 128
    kown = ctx.dram("kown0" if layer == 0 else "kown1")
    vown = ctx.dram("vown0" if layer == 0 else "vown1")
    kg = ctx.dram("kg0" if layer == 0 else "kg1")
    vg = ctx.dram("vg0" if layer == 0 else "vg1")
    kn2own, kn2 = ctx.dram("kn2own"), ctx.dram("kn2")
    nc = ctx.nc
    sfx = f"_g{layer}"
    kb = nc.dram_tensor("kb" + sfx, [n * KD], BF16, kind="Internal").ap()
    kgb = nc.dram_tensor("kgb" + sfx, [2 * n * KD], BF16, kind="Internal").ap()
    vb = nc.dram_tensor("vb" + sfx, [n * KD], BF16, kind="Internal").ap()
    vgb = nc.dram_tensor("vgb" + sfx, [2 * n * KD], BF16, kind="Internal").ap()
    nb = nc.dram_tensor("nb" + sfx, [1, 16], F32, kind="Internal").ap()
    nrb = nc.dram_tensor("nrb" + sfx, [1, 16], F32, kind="Internal").ap()
    groups = [[2 * i, 2 * i + 1] for i in range(cfg.n_cores // 2)]
    dmas = ctx.dma_sem("gth_dma", sw=True)
    cc1, cc2, cc3 = (ctx.sem("gth_c1", keep=True), ctx.sem("gth_c2", keep=True),
                     ctx.sem("gth_c3", keep=True))
    Bv = {x.name: x.v for x in (dmas, cc1, cc2, cc3)}

    def f_gpsimd(g):
        ctx.dma(g, kb, kown.rearrange("a b -> (a b)"), dmas)
        ctx.dma(g, vb, vown.rearrange("a b c -> (a b c)") if layer == 1
                else vown.rearrange("a b c d -> (a b c d)"), dmas)
        ctx.dma(g, nb, kn2own, dmas)
        g.wait_ge(dmas.h, Bv["gth_dma"] + 48)
        cc1.inc(g.collective_compute(
            "AllGather", ALU.bypass, replica_groups=groups,
            ins=[kb.opt()], outs=[kgb.opt()]))
        cc2.inc(g.collective_compute(
            "AllGather", ALU.bypass, replica_groups=groups,
            ins=[vb.opt()], outs=[vgb.opt()]))
        cc3.inc(g.collective_compute(
            "AllReduce", ALU.max, replica_groups=groups,
            ins=[nb.opt()], outs=[nrb.opt()]))
        g.wait_ge(cc1.h, Bv["gth_c1"] + 1)
        ctx.dma(g, kg.rearrange("a b c -> (a b c)"), kgb, dmas)
        g.wait_ge(cc2.h, Bv["gth_c2"] + 1)
        ctx.dma(g, (vg.rearrange("a b c d -> (a b c d)") if layer == 1
                    else vg.rearrange("a b c d e -> (a b c d e)")), vgb, dmas)
        g.wait_ge(cc3.h, Bv["gth_c3"] + 1)
        ctx.dma(g, kn2, nrb, dmas)
        ctx.w(g, dmas)

    ctx.sec("gpsimd", f_gpsimd)
    ctx.barrier()


def ph_att(ctx, layer):
    cfg = ctx.cfg
    L0 = layer == 0
    NQC, NT, n, DC = cfg.NQC, cfg.NT, cfg.n_own, cfg.DC
    KT = 2 * NT
    H = 16
    ISQ = 1.0 / np.sqrt(128.0)

    qfm = ctx.dram("qfm")
    kg = None if L0 else ctx.dram("kg1")
    vg = None if L0 else ctx.dram("vg1")
    qn2_d, kn2_d = ctx.dram("qn2"), ctx.dram("kn2")
    wo_d = ctx.dram("wo0T" if L0 else "wo1T")
    sfm = ctx.dram("sfm")
    ident = ctx.dram("ident")

    k_sb = ctx.sbuf("att_k", [PAR, 2, 2, NT, 128], BF16)
    v_sb = ctx.sbuf("att_v", [PAR, 2, 2, NT, 136], BF16)
    q_sb = ctx.sbuf("att_q", [PAR, 2, n], BF16)
    attn_sb = ctx.sbuf("att_p", [PAR, 2, 512], BF16)
    ctile = ctx.sbuf("att_ct", [PAR, 2, 512], BF16)
    CTX = ctx.sbuf("att_ctx", [PAR, 16, n], BF16)
    negm_b = ctx.sbuf("att_negm", [PAR, 16], F32)
    nrows = ctx.sbuf("att_nrows", [1, 4, 16], F32)
    mask_sb = ctx.sbuf("att_mask", [PAR, 8, 512], BF16)
    wblk = ctx.sbuf("att_wblk", [PAR, 2, DC, 128], BF16)
    sstage = ctx.sbuf("att_sstage", [PAR, 2, 512], F32)
    bo_row = ctx.sbuf("att_bo", [1, cfg.D], BF16)
    ones_r512 = ctx.sbuf("ones_r512", [1, 512], BF16)
    ones_f1 = ctx.sbuf("ones_f32_row", [1, PAR], F32)
    idf = ctx.sbuf("idf_bf", [PAR, 128], BF16)
    pA = ctx.psum("pA", [PAR, 1024], F32)
    pB = ctx.psum("pB", [PAR, 1024], F32)
    pC = ctx.psum("pC", [PAR, 1024], BF16)
    pC2 = ctx.psum("pC2", [PAR, 1024], BF16)
    pD = ctx.psum("pD", [PAR, 512], F32)
    pE = ctx.psum("pE", [PAR, 512], F32)

    def ctx_ps(j):
        return [(pB, 0), (pB, 512), (pD, 0), (pE, 0)][j]

    ldn = ctx.dma_sem("att_ldn")
    ldkv = ctx.dma_sem("att_ldkv")
    hs = ctx.slots("att_h", 2, per=(5 if L0 else 1))
    wos = ctx.slots("att_wo", 2)
    sts = ctx.slots("att_sst", 2)
    gld = ctx.dma_sem("att_gld", sw=True)
    nm = ctx.sem("att_nm")
    sc_r = ctx.sem("att_scr")
    expd = ctx.sem("att_expd")
    mskd = ctx.sem("att_mskd")
    pvd = ctx.sem("att_pvd")
    ce = ctx.sem("att_ce")
    trd = ctx.sem("att_trd")
    cte = ctx.sem("att_cte")
    omm = ctx.sem("att_omm")
    oev = ctx.sem("att_oev")
    wodn = ctx.sem("att_wodn")
    cv = ctx.sem("chain_v")
    sems = (ldn, ldkv, gld, nm, sc_r, expd, mskd, pvd, ce, trd, cte,
            omm, oev, wodn)
    B = {x.name: x.v for x in sems}
    if L0:
        gkv = ctx.sem("att_gkv", keep=True)
        B["att_gkv"] = gkv.v
        kown_d, vown_d = ctx.dram("kown0"), ctx.dram("vown0")
        kn2own_d = ctx.dram("kn2own")
        nrb = ctx.nc.dram_tensor("att_nrb", [1, 16], F32, kind="Internal").ap()
        kghb = ctx.nc.dram_tensor("att_kghb", [H // 2, 2, 256, n], BF16,
                                  kind="Internal").ap()
        vghb = ctx.nc.dram_tensor("att_vghb", [H // 2, 2, 2, PAR, NT, 128], BF16,
                                  kind="Internal").ap()
        cgroups = [[2 * i, 2 * i + 1] for i in range(cfg.n_cores // 2)]

    sched = []
    s_count = 0
    hq_of = {}
    for h in range(H):
        for qc in range(NQC):
            first_s = s_count
            extent = KT if L0 else min(KT, 8 * qc + 8)
            for t in range(extent):
                if L0 or t < 8 * qc:
                    bt, qs = None, 0
                else:
                    bt = t - 8 * qc
                    qs = 128 * (bt // 2)
                sched.append((h, qc, t, bt, qs, s_count))
                s_count += 1
            hq_of[(h, qc)] = (first_s, extent)
    S_TOT = s_count
    HQ = H * NQC
    n_band = sum(1 for it in sched if it[3] is not None)

    def stop_t(j, qc, extent):
        return extent - 1 if L0 else min(extent - 1, 8 * qc + 2 * j + 1)

    def end_s(hq):
        h, qc = hq // NQC, hq % NQC
        fs, ext = hq_of[(h, qc)]
        return fs + ext

    def head_end_s(h):
        return end_s(h * NQC + NQC - 1)

    def f_sync(sync):
        ctx.dma(sync, nrows[0:1, 0], qn2_d, ldn)
        if L0:
            sync.wait_ge(gkv.h, B["att_gkv"] + 1)
            ctx.dma(sync, nrows[0:1, 1], nrb, ldn)
        else:
            ctx.dma(sync, nrows[0:1, 1], kn2_d, ldn)
        if not L0:
            for par in range(2):
                ctx.dma(sync, k_sb[:, 0, par],
                        kg[par].rearrange("p (lt m) -> p lt m", m=128), ldkv)
                ctx.dma(sync, v_sb[:, 0, par, :, 0:128], vg[par], ldkv)
        for h in range(H):
            if h >= 2:
                sync.wait_ge(pvd.h, B["att_pvd"] + head_end_s(h - 2))
            hs.dma(sync, h, q_sb[:, h % 2], qfm[h * 128:(h + 1) * 128, :])
            if L0:
                sync.wait_ge(gkv.h, B["att_gkv"] + 2 * (h // 2) + 3)
                for par in range(2):
                    hs.dma(sync, h, k_sb[:, h % 2, par],
                           kghb[h // 2, par, (h % 2) * 128:(h % 2) * 128 + 128]
                           .rearrange("p (lt m) -> p lt m", m=128))
                    hs.dma(sync, h, v_sb[:, h % 2, par, :, 0:128],
                           vghb[h // 2, par, h % 2])
        for i in range(16 * NQC):
            sync.wait_ge(oev.h, B["att_oev"] + i + 1)
            oc, ncl = i // NQC, i % NQC
            sts.dma(sync, i, sfm[oc * 128:(oc + 1) * 128,
                                 ncl * 512:(ncl + 1) * 512], sstage[:, i % 2])
        sts.wait_all(sync, 16 * NQC)
        hs.wait_all(sync, H)
        ctx.w(sync, ldn)
        if not L0:
            ctx.w(sync, ldkv)

    def f_gpsimd(g):
        if L0:
            ctx.dma(g, bo_row[0:1, :], ctx.dram("bo0"), gld)
        else:
            ctx.dma(g, mask_sb[:],
                    ctx.dram("masks1").rearrange("t p q -> p t q"), gld)
        ctx.dma(g, idf[:], ident, gld)
        if L0:
            gkv.inc(g.collective_compute(
                "AllReduce", ALU.max, replica_groups=cgroups,
                ins=[kn2own_d.opt()], outs=[nrb.opt()]))
            for j in range(H // 2):
                gkv.inc(g.collective_compute(
                    "AllGather", ALU.bypass, replica_groups=cgroups,
                    ins=[kown_d[j * 256:(j + 1) * 256, :]
                         .rearrange("a b -> (a b)").opt()],
                    outs=[kghb[j].rearrange("a b c -> (a b c)").opt()]))
                gkv.inc(g.collective_compute(
                    "AllGather", ALU.bypass, replica_groups=cgroups,
                    ins=[vown_d[2 * j:2 * j + 2]
                         .rearrange("a b c d -> (a b c d)").opt()],
                    outs=[vghb[j].rearrange("a b c d e -> (a b c d e)").opt()]))
        ctx.w(g, gld)

    def f_scalar(scalar):
        scalar.wait_ge(ldn.h, B["att_ldn"] + 32)
        scalar.activation(out=nrows[0:1, 2], in_=nrows[0:1, 0], func=AF.Sqrt)
        nm.inc(scalar.activation(out=nrows[0:1, 3], in_=nrows[0:1, 1], func=AF.Sqrt))
        for (h, qc, t, bt, qs, s) in sched:
            scalar.wait_ge(nm.h, B["att_nm"] + 4)
            scalar.wait_ge(sc_r.h, B["att_scr"] + s + 1)
            if s >= 2:
                scalar.wait_ge(pvd.h, B["att_pvd"] + s - 1)
            off = (s % 2) * 512
            expd.inc(scalar.activation(
                out=attn_sb[:, s % 2, qs:512], in_=pA[:, off + qs:off + 512],
                func=AF.Exp, bias=negm_b[:, h:h + 1], scale=ISQ))
        wos.dma(scalar, 0, wblk[:, 0], wo_d[0])
        wos.dma(scalar, 1, wblk[:, 1], wo_d[1])
        for i in range(16 * NQC):
            if i % NQC == 0:
                oc2 = i // NQC + 2
                if oc2 < 16:
                    scalar.wait_ge(wodn.h, B["att_wodn"] + oc2 - 1)
                    wos.dma(scalar, oc2, wblk[:, oc2 % 2], wo_d[oc2])
            scalar.wait_ge(omm.h, B["att_omm"] + i + 1)
            if i >= 2:
                sts.wait(scalar, i - 2)
            r = i % 4
            pt = (pA, pB)[r // 2]
            off = (r % 2) * 512
            oev.inc(scalar.copy(out=sstage[:, i % 2], in_=pt[:, off:off + 512]))
        wos.wait_all(scalar, 16)

    band_prefix = []
    nb = 0
    for it in sched:
        if it[3] is not None:
            nb += 1
        band_prefix.append(nb)

    def f_vector(vector):
        vector.memset(ones_r512[:], 1.0)
        vector.memset(ones_f1[:], 1.0)
        vector.memset(v_sb[:, :, :, :, 128:129], 1.0)
        if not L0:
            vector.memset(bo_row[:], 0.0)
        rz_sb = ctx.sbuf("att_rz", [PAR, 2], F32)
        vector.wait_ge(nm.h, B["att_nm"] + 1)
        if L0:
            cv.inc(vector.tensor_mul(out=nrows[0:1, 2], in0=nrows[0:1, 2],
                                     in1=nrows[0:1, 3]))
        else:
            cv.inc(vector.tensor_scalar_mul(out=nrows[0:1, 2], in0=nrows[0:1, 2],
                                            scalar1=nrows[0:1, 3, 0:1]))
        ctx.w(vector, cv)
        nm.inc(vector.tensor_scalar_mul(out=nrows[0:1, 2], in0=nrows[0:1, 2],
                                        scalar1=-ISQ))
        vector.wait_ge(nm.h, B["att_nm"] + 3)
        nm.inc(vector.tensor_copy(out=negm_b[:], in_=pD[:, 0:16]))
        for h in range(H):
            for qc in range(NQC):
                hq = h * NQC + qc
                first_s, extent = hq_of[(h, qc)]
                if not L0:
                    for t in range(extent):
                        it = sched[first_s + t]
                        _, _, _, bt, qs, sidx = it
                        if bt is None:
                            continue
                        vector.wait_ge(expd.h, B["att_expd"] + sidx + 1)
                        mskd.inc(vector.tensor_mul(
                            out=attn_sb[:, sidx % 2, qs:512],
                            in0=attn_sb[:, sidx % 2, qs:512],
                            in1=mask_sb[:, bt, qs:512]))
                vector.wait_ge(pvd.h, B["att_pvd"] + end_s(hq))
                if hq >= 2:
                    vector.wait_ge(trd.h, B["att_trd"] + hq - 1)
                for j in range(4):
                    pt_j, joff = ctx_ps(j)
                    cv.inc(vector.reciprocal(out=rz_sb[:, j % 2:j % 2 + 1],
                                             in_=pt_j[:, joff + 128:joff + 129]))
                    ctx.w(vector, cv)
                    inst = vector.tensor_scalar_mul(
                        out=ctile[:, hq % 2, j * 128:(j + 1) * 128],
                        in0=pt_j[:, joff:joff + 128], scalar1=rz_sb[:, j % 2:j % 2 + 1])
                ce.inc(inst)
                if hq >= 1:
                    ph_, pqc = (hq - 1) // NQC, (hq - 1) % NQC
                    vector.wait_ge(trd.h, B["att_trd"] + hq)
                    pt = pC if (hq - 1) % 2 == 0 else pC2
                    cte.inc(vector.tensor_copy(
                        out=CTX[:, ph_, pqc * 512:(pqc + 1) * 512], in_=pt[:, 0:512]))
        vector.wait_ge(trd.h, B["att_trd"] + HQ)
        ph_, pqc = (HQ - 1) // NQC, (HQ - 1) % NQC
        pt = pC if (HQ - 1) % 2 == 0 else pC2
        cte.inc(vector.tensor_copy(
            out=CTX[:, ph_, pqc * 512:(pqc + 1) * 512], in_=pt[:, 0:512]))

    def f_tensor(tensor):
        if not L0:
            tensor.wait_ge(ldkv.h, B["att_ldkv"] + 64)
        tensor.wait_ge(gld.h, B["att_gld"] + 32)
        tensor.wait_ge(nm.h, B["att_nm"] + 2)
        nm.inc(tensor.matmul(pD[:, 0:16], ones_f1[:], nrows[0:1, 2],
                             start=True, stop=True))

        def emit_transpose(hq):
            tensor.wait_ge(ce.h, B["att_ce"] + hq + 1)
            if hq >= 2:
                tensor.wait_ge(cte.h, B["att_cte"] + hq - 1)
            pt = pC if hq % 2 == 0 else pC2
            for j in range(4):
                inst = tensor.transpose(pt[:, j * 128:(j + 1) * 128],
                                        ctile[:, hq % 2, j * 128:(j + 1) * 128],
                                        idf[:])
            trd.inc(inst)

        for h in range(H):
            for qc in range(NQC):
                hq = h * NQC + qc
                first_s, extent = hq_of[(h, qc)]
                if hq >= 1:
                    emit_transpose(hq - 1)
                if qc == 0:
                    hs.wait(tensor, h)
                kslot = (h % 2) if L0 else 0

                def emit_score(t):
                    _, _, _, bt, qs, s = sched[first_s + t]
                    par, loc = t % 2, t // 2
                    if s >= 2:
                        tensor.wait_ge(expd.h, B["att_expd"] + s - 1)
                    off = (s % 2) * 512
                    sc_r.inc(tensor.matmul(
                        pA[:, off + qs:off + 512], k_sb[:, kslot, par, loc, :],
                        q_sb[:, h % 2, qc * 512 + qs:(qc + 1) * 512],
                        start=True, stop=True))

                emit_score(0)
                for t in range(extent):
                    _, _, _, bt, qs, s = sched[first_s + t]
                    par, loc = t % 2, t // 2
                    if t + 1 < extent:
                        emit_score(t + 1)
                    if bt is None:
                        tensor.wait_ge(expd.h, B["att_expd"] + s + 1)
                    else:
                        tensor.wait_ge(mskd.h, B["att_mskd"] + band_prefix[s])
                    if t == 0 and hq >= 1:
                        tensor.wait_ge(ce.h, B["att_ce"] + hq)
                    j0 = 0 if bt is None else bt // 2
                    for j in range(j0, 4):
                        pt_j, joff = ctx_ps(j)
                        inst = tensor.matmul(
                            pt_j[:, joff:joff + 129],
                            attn_sb[:, s % 2, j * 128:(j + 1) * 128],
                            v_sb[:, kslot, par, loc, 0:129],
                            start=(t == 0), stop=(t == stop_t(j, qc, extent)))
                    pvd.inc(inst)
        emit_transpose(HQ - 1)
        tensor.wait_ge(cte.h, B["att_cte"] + HQ)
        tensor.wait_ge(gld.h, B["att_gld"] + 32)
        for i in range(16 * NQC):
            oc, ncl = i // NQC, i % NQC
            wos.wait(tensor, oc)
            if i >= 4:
                tensor.wait_ge(oev.h, B["att_oev"] + i - 3)
            r = i % 4
            pt = (pA, pB)[r // 2]
            off = (r % 2) * 512
            for kc in range(DC):
                inst = tensor.matmul(pt[:, off:off + 512], wblk[:, oc % 2, kc, :],
                                     CTX[:, kc, ncl * 512:(ncl + 1) * 512],
                                     start=(kc == 0), stop=False)
            if ncl == NQC - 1:
                wodn.inc(inst)
            omm.inc(tensor.matmul(pt[:, off:off + 512],
                                  bo_row[0:1, oc * 128:(oc + 1) * 128],
                                  ones_r512[:], start=False, stop=True))
        ctx.w(tensor, omm)

    ctx.sec("sync", f_sync)
    ctx.sec("gpsimd", f_gpsimd)
    ctx.sec("scalar", f_scalar)
    ctx.sec("vector", f_vector)
    ctx.sec("tensor", f_tensor)
    ctx.barrier()


def ph_wprep(ctx):
    cfg = ctx.cfg
    FC, D = cfg.FC, cfg.D
    wq_d, sc_d, out_d = ctx.dram("wdnTq"), ctx.dram("scdnT"), ctx.dram("wdnbf")
    wi = ctx.sbuf("wp_wi", [PAR, 2, D], I32)
    wsc = ctx.sbuf("wp_sc", [PAR, 2, D], BF16)
    wtmp = ctx.sbuf("wp_tmp", [PAR, 2, D], F32)
    wbf = ctx.sbuf("wp_bf", [PAR, 2, D], BF16)
    wis = ctx.slots("wp_wi", 2)
    wss = ctx.slots("wp_ws", 2)
    sts = ctx.slots("wp_st", 2, sw=True)
    cv = ctx.sem("chain_v")
    dqd = ctx.sem("wp_dqd")
    B = {"cv": cv.v, "dqd": dqd.v}

    def f_sync(sync):
        for k in range(FC):
            if k >= 2:
                sync.wait_ge(dqd.h, B["dqd"] + 2 * (k - 1))
            wis.dma(sync, k, wi[:, k % 2],
                    wq_d[k // 4].rearrange("p a b -> p (a b)")
                    [:, (k % 4) * 2048:(k % 4 + 1) * 2048])
        wis.wait_all(sync, FC)
        wss.wait_all(sync, FC)

    def f_scalar(scalar):
        for k in range(FC):
            if k >= 2:
                scalar.wait_ge(dqd.h, B["dqd"] + 2 * (k - 1))
            wss.dma(scalar, k, wsc[:, k % 2],
                    sc_d[k // 4].rearrange("p a b -> p (a b)")
                    [:, (k % 4) * 2048:(k % 4 + 1) * 2048])

    def f_vector(vector):
        for k in range(FC):
            wis.wait(vector, k)
            dqd.inc(vector.tensor_scalar_add(out=wtmp[:, k % 2], in0=wi[:, k % 2],
                                             scalar1=-7.5))
            wss.wait(vector, k)
            if k >= 2:
                sts.wait(vector, k - 2)
            ctx.w(vector, dqd)
            dqd.inc(vector.tensor_mul(out=wbf[:, k % 2], in0=wtmp[:, k % 2],
                                      in1=wsc[:, k % 2]))

    def f_gpsimd(g):
        for k in range(FC):
            g.wait_ge(dqd.h, B["dqd"] + 2 * (k + 1))
            sts.dma(g, k, out_d[k // 4][:, (k % 4) * 2048:(k % 4 + 1) * 2048],
                    wbf[:, k % 2])
        sts.wait_all(g, FC)

    ctx.sec("sync", f_sync)
    ctx.sec("scalar", f_scalar)
    ctx.sec("vector", f_vector)
    ctx.sec("gpsimd", f_gpsimd)
    ctx.barrier()


def ph_ffna(ctx):
    cfg = ctx.cfg
    DC, NQC, n, FC = cfg.DC, cfg.NQC, cfg.n_own, cfg.FC
    XN = ctx.sbuf("XN", [PAR, DC, n], BF16)
    wi = ctx.sbuf("fa_wi", [PAR, 2, DC, 128], I32)
    wsc = ctx.sbuf("fa_sc", [PAR, 2, DC, 128], BF16)
    wtmp = ctx.sbuf("fa_tmp", [PAR, 2, DC, 128], F32)
    wbf = ctx.sbuf("fa_bf", [PAR, 2, DC, 128], BF16)
    ua = ctx.sbuf("fa_ua", [PAR, DC, 16], BF16)
    ub = ctx.sbuf("fa_ub", [16, cfg.FF], BF16)
    bup_r = ctx.sbuf("fa_bup", [1, cfg.FF], BF16)
    tsb = ctx.sbuf("fa_t", [16, NQC, 512], BF16)
    x2st = ctx.sbuf("fa_x2", [PAR, 2, n], BF16)
    ones_r512 = ctx.sbuf("ones_r512", [1, 512], BF16)
    pA = ctx.psum("pA", [PAR, 1024], F32)
    pB = ctx.psum("pB", [PAR, 1024], F32)
    pD = ctx.psum("pD", [PAR, 512], F32)
    pE = ctx.psum("pE", [PAR, 512], F32)
    xn_d = ctx.dram("xn")
    wq_d, sc_d = ctx.dram("wupTq"), ctx.dram("scupT")
    x2_d = ctx.dram("x2b")

    ld = ctx.dma_sem("fa_ld")
    gld = ctx.dma_sem("fa_gld", sw=True)
    wis = ctx.slots("fa_wi", 2)
    wss = ctx.slots("fa_ws", 2)
    sts = ctx.slots("fa_st", 2)
    cg = ctx.sem("chain_g")
    dqd = ctx.sem("fa_dqd")
    wdn = ctx.sem("fa_wdn")
    tmm = ctx.sem("fa_tmm")
    tv = ctx.sem("fa_tv")
    mmf = ctx.sem("fa_mmf")
    ev = ctx.sem("fa_ev")
    vin = ctx.sem("fa_vin")
    B = {x.name if hasattr(x, 'name') else i: x.v for i, x in
         enumerate((ld, gld, cg, dqd, wdn, tmm, tv, mmf, ev, vin))}

    def f_sync(sync):
        ctx.dma(sync, XN[:], xn_d, ld)
        for hc in range(FC):
            if hc >= 2:
                sync.wait_ge(dqd.h, B["fa_dqd"] + 2 * (hc - 1))
            wis.dma(sync, hc, wi[:, hc % 2], wq_d[hc])
            if hc >= 2:
                sync.wait_ge(ev.h, B["fa_ev"] + NQC * (hc - 1))
                sts.dma(sync, hc - 2, x2_d[:, :, hc - 2, :],
                        x2st[:, hc % 2].rearrange("p (a b) -> p a b", b=512))
        for hc in (FC - 2, FC - 1):
            sync.wait_ge(ev.h, B["fa_ev"] + NQC * (hc + 1))
            sts.dma(sync, hc, x2_d[:, :, hc, :],
                    x2st[:, hc % 2].rearrange("p (a b) -> p a b", b=512))
        sts.wait_all(sync, FC)
        wis.wait_all(sync, FC)
        ctx.w(sync, ld)

    def f_gpsimd(g):
        ctx.dma(g, ua[:], ctx.dram("upA").rearrange("(dc p) r -> p dc r", p=PAR), gld)
        ctx.dma(g, ub[:], ctx.dram("upB"), gld)
        ctx.dma(g, bup_r[:], ctx.dram("bup"), gld)
        ctx.w(g, gld)

    def f_tensor(tensor):
        tensor.wait_ge(ld.h, B["fa_ld"] + 16)
        tensor.wait_ge(gld.h, B["fa_gld"] + 48)
        tensor.wait_ge(vin.h, B["fa_vin"] + 1)
        for ncl in range(NQC):
            pt = pD if ncl % 2 == 0 else pE
            if ncl >= 2:
                tensor.wait_ge(tv.h, B["fa_tv"] + ncl - 1)
            for kc in range(DC):
                inst = tensor.matmul(pt[0:16, 0:512], ua[:, kc, :],
                                     XN[:, kc, ncl * 512:(ncl + 1) * 512],
                                     start=(kc == 0), stop=(kc == DC - 1))
            tmm.inc(inst)
        for hc in range(FC):
            tensor.wait_ge(dqd.h, B["fa_dqd"] + 2 * (hc + 1))
            if hc == 0:
                tensor.wait_ge(tv.h, B["fa_tv"] + NQC)
            for ncl in range(NQC):
                i = hc * NQC + ncl
                r = i % 4
                pt = (pA, pB)[r // 2]
                off = (r % 2) * 512
                if i >= 4:
                    tensor.wait_ge(ev.h, B["fa_ev"] + i - 3)
                for kc in range(DC):
                    inst = tensor.matmul(pt[:, off:off + 512], wbf[:, hc % 2, kc, :],
                                         XN[:, kc, ncl * 512:(ncl + 1) * 512],
                                         start=(kc == 0), stop=False)
                if ncl == NQC - 1:
                    wdn.inc(inst)
                tensor.matmul(pt[:, off:off + 512], ub[0:16, hc * 128:(hc + 1) * 128],
                              tsb[0:16, ncl, :], start=False, stop=False)
                mmf.inc(tensor.matmul(pt[:, off:off + 512],
                                      bup_r[0:1, hc * 128:(hc + 1) * 128],
                                      ones_r512[:], start=False, stop=True))
        ctx.w(tensor, mmf)

    def f_vector(vector):
        vin.inc(vector.memset(ones_r512[:], 1.0))
        for ncl in range(NQC):
            pt = pD if ncl % 2 == 0 else pE
            vector.wait_ge(tmm.h, B["fa_tmm"] + ncl + 1)
            tv.inc(vector.tensor_copy(out=tsb[0:16, ncl, :], in_=pt[0:16, 0:512]))
        for hc in range(FC):
            wis.wait(vector, hc)
            if hc >= 2:
                vector.wait_ge(dqd.h, B["fa_dqd"] + 2 * (hc - 1))
            dqd.inc(vector.tensor_scalar_add(out=wtmp[:, hc % 2],
                                             in0=wi[:, hc % 2], scalar1=-7.5))
            wss.wait(vector, hc)
            if hc >= 2:
                vector.wait_ge(wdn.h, B["fa_wdn"] + hc - 1)
            ctx.w(vector, dqd)
            dqd.inc(vector.tensor_mul(out=wbf[:, hc % 2], in0=wtmp[:, hc % 2],
                                      in1=wsc[:, hc % 2]))

    def f_scalar(scalar):
        wss.dma(scalar, 0, wsc[:, 0], sc_d[0])
        wss.dma(scalar, 1, wsc[:, 1], sc_d[1])
        for hc in range(FC):
            if hc + 2 < FC:
                scalar.wait_ge(dqd.h, B["fa_dqd"] + 2 * (hc + 1))
                wss.dma(scalar, hc + 2, wsc[:, hc % 2], sc_d[hc + 2])
            for ncl in range(NQC):
                i = hc * NQC + ncl
                r = i % 4
                pt = (pA, pB)[r // 2]
                off = (r % 2) * 512
                scalar.wait_ge(mmf.h, B["fa_mmf"] + i + 1)
                if ncl == 0 and hc >= 2:
                    sts.wait(scalar, hc - 2)
                ev.inc(scalar.activation(
                    out=x2st[:, hc % 2, ncl * 512:(ncl + 1) * 512],
                    in_=pt[:, off:off + 512], func=AF.Relu))

    ctx.sec("sync", f_sync)
    ctx.sec("gpsimd", f_gpsimd)
    ctx.sec("tensor", f_tensor)
    ctx.sec("vector", f_vector)
    ctx.sec("scalar", f_scalar)
    ctx.barrier()


def ph_ffnb(ctx, src="xres", dst="xres1"):
    cfg = ctx.cfg
    DC, NQC, n, FC = cfg.DC, cfg.NQC, cfg.n_own, cfg.FC
    GS = 2 if NQC % 2 == 0 else 1
    NG = NQC // GS
    NI = NQC * DC
    x2s = ctx.sbuf("fb_x2", [PAR, 2, FC, 512], BF16)
    wd = ctx.sbuf("fb_wd", [PAR, 2, FC, 128], BF16)
    da = ctx.sbuf("fb_da", [PAR, FC, 16], BF16)
    db = ctx.sbuf("fb_db", [16, cfg.D], BF16)
    bdn_r = ctx.sbuf("fb_bdn", [1, cfg.D], BF16)
    tsb = ctx.sbuf("fb_t", [16, 2, 512], BF16)
    gst = ctx.sbuf("fb_g", [PAR, 4, 512], F32)
    xcb = ctx.sbuf("fb_xc", [PAR, 2, 512], F32)
    scb = ctx.sbuf("fb_sc", [PAR, 2, 512], F32)
    prb = ctx.sbuf("fb_pr", [PAR, 2, 512], F32)
    ocb = ctx.sbuf("fb_oc", [PAR, 2, 512], F32)
    ones_r512 = ctx.sbuf("ones_r512", [1, 512], BF16)
    pA = ctx.psum("pA", [PAR, 1024], F32)
    pB = ctx.psum("pB", [PAR, 1024], F32)
    pD = ctx.psum("pD", [PAR, 512], F32)
    x2_d, wd_d = ctx.dram("x2b"), ctx.dram("wdnbf")
    x_d, s_d, o_d = ctx.dram(src), ctx.dram("sfm"), ctx.dram(dst)

    gld = ctx.dma_sem("fb_gld", sw=True)
    x2l = ctx.slots("fb_x2l", 2)
    wde = ctx.slots("fb_wde", 1)
    wdo = ctx.slots("fb_wdo", 1)
    xls = ctx.slots("fb_xl", 2)
    sls = ctx.slots("fb_sl", 2)
    sto = ctx.slots("fb_st", 2)
    tmm = ctx.sem("fb_tmm")
    tv = ctx.sem("fb_tv")
    mmf = ctx.sem("fb_mmf")
    ev = ctx.sem("fb_ev")
    dn = ctx.sem("fb_dn")
    vin = ctx.sem("fb_vin")
    B = {x.name: x.v for x in (gld, tmm, tv, mmf, ev, dn, vin)}

    def out_ap(i):
        w, sub = i // GS, i % GS
        dc, ncl = w % DC, GS * (w // DC) + sub
        return o_d[dc * 128:(dc + 1) * 128, ncl * 512:(ncl + 1) * 512]

    def xs_ap(d, i):
        w, sub = i // GS, i % GS
        dc, ncl = w % DC, GS * (w // DC) + sub
        return d[dc * 128:(dc + 1) * 128, ncl * 512:(ncl + 1) * 512]

    def f_sync(sync):
        for w in range(NG * DC):
            if w % 2 == 0:
                if w >= 2:
                    sync.wait_ge(mmf.h, B["fb_mmf"] + GS * (w - 1))
                wde.dma(sync, w // 2, wd[:, 0],
                        wd_d[w % DC].rearrange("p (a b) -> p a b", b=128))
            for sub in range(GS):
                i = GS * w + sub
                if i == 0:
                    xls.dma(sync, 0, xcb[:, 0], xs_ap(x_d, 0))
                    xls.dma(sync, 1, xcb[:, 1], xs_ap(x_d, 1))
                if i >= 1:
                    sync.wait_ge(dn.h, B["fb_dn"] + 2 * i)
                    sto.dma(sync, i - 1, out_ap(i - 1), ocb[:, (i - 1) % 2])
                    if i + 1 < NI:
                        xls.dma(sync, i + 1, xcb[:, (i + 1) % 2], xs_ap(x_d, i + 1))
        sync.wait_ge(dn.h, B["fb_dn"] + 2 * NI)
        sto.dma(sync, NI - 1, out_ap(NI - 1), ocb[:, (NI - 1) % 2])
        sto.wait_all(sync, NI)
        wde.wait_all(sync, (NG * DC + 1) // 2)
        wdo.wait_all(sync, NG * DC // 2)
        x2l.wait_all(sync, NQC)
        xls.wait_all(sync, NI)
        sls.wait_all(sync, NI)

    def f_gpsimd(g):
        ctx.dma(g, da[:], ctx.dram("dnA").rearrange("(hc p) r -> p hc r", p=PAR), gld)
        ctx.dma(g, db[:], ctx.dram("dnB"), gld)
        ctx.dma(g, bdn_r[:], ctx.dram("bdn"), gld)
        ctx.w(g, gld)

    def f_scalar(scalar):
        for sub in range(GS):
            x2l.dma(scalar, sub, x2s[:, sub], x2_d[:, sub])
        if NG * DC > 1:
            wdo.dma(scalar, 0, wd[:, 1],
                    wd_d[1 % DC].rearrange("p (a b) -> p a b", b=128))
        for i in range(NI):
            if (i + 3) % GS == 0:
                w = (i + 3) // GS
                if w < NG * DC and w % 2 == 1:
                    scalar.wait_ge(mmf.h, B["fb_mmf"] + GS * (w - 1))
                    wdo.dma(scalar, (w - 1) // 2, wd[:, 1],
                            wd_d[w % DC].rearrange("p (a b) -> p a b", b=128))
            if i == 0:
                sls.dma(scalar, 0, scb[:, 0], xs_ap(s_d, 0))
            if i + 1 < NI:
                if i >= 1:
                    scalar.wait_ge(dn.h, B["fb_dn"] + 2 * i)
                sls.dma(scalar, i + 1, scb[:, (i + 1) % 2], xs_ap(s_d, i + 1))
            r = i % 4
            pt = (pA, pB)[r // 2]
            off = (r % 2) * 512
            scalar.wait_ge(mmf.h, B["fb_mmf"] + i + 1)
            if i >= 4:
                scalar.wait_ge(dn.h, B["fb_dn"] + 2 * i - 7)
            ev.inc(scalar.activation(out=gst[:, i % 4], in_=pt[:, off:off + 512],
                                     func=AF.Silu))
            if i % (GS * DC) == GS * DC - 1:
                grp = i // (GS * DC) + 1
                if grp < NG:
                    scalar.wait_ge(mmf.h, B["fb_mmf"] + GS * DC * grp)
                    scalar.wait_ge(tmm.h, B["fb_tmm"] + GS * grp)
                    for sub in range(GS):
                        ncl = GS * grp + sub
                        x2l.dma(scalar, ncl, x2s[:, sub], x2_d[:, ncl])

    def f_tensor(tensor):
        tensor.wait_ge(gld.h, B["fb_gld"] + 48)
        tensor.wait_ge(vin.h, B["fb_vin"] + 1)
        for g in range(NG):
            for sub in range(GS):
                ncl = GS * g + sub
                x2l.wait(tensor, ncl)
                if ncl >= 1:
                    tensor.wait_ge(tv.h, B["fb_tv"] + ncl)
                for hb in range(FC):
                    inst = tensor.matmul(pD[0:16, 0:512], da[:, hb, :],
                                         x2s[:, sub, hb, :],
                                         start=(hb == 0), stop=(hb == FC - 1))
                tmm.inc(inst)
            for dc in range(DC):
                w = g * DC + dc
                for sub in range(GS):
                    i = GS * w + sub
                    if sub == 0:
                        if w % 2 == 0:
                            wde.wait(tensor, w // 2)
                        else:
                            wdo.wait(tensor, (w - 1) // 2)
                    if dc == 0 and sub == 0:
                        tensor.wait_ge(tv.h, B["fb_tv"] + GS * g + GS)
                    r = i % 4
                    pt = (pA, pB)[r // 2]
                    off = (r % 2) * 512
                    if i >= 4:
                        tensor.wait_ge(ev.h, B["fb_ev"] + i - 3)
                    for hb in range(FC):
                        tensor.matmul(pt[:, off:off + 512], wd[:, w % 2, hb, :],
                                      x2s[:, sub, hb, :],
                                      start=(hb == 0), stop=False)
                    tensor.matmul(pt[:, off:off + 512],
                                  db[0:16, dc * 128:(dc + 1) * 128],
                                  tsb[0:16, sub, :], start=False, stop=False)
                    mmf.inc(tensor.matmul(pt[:, off:off + 512],
                                          bdn_r[0:1, dc * 128:(dc + 1) * 128],
                                          ones_r512[:], start=False, stop=True))
        ctx.w(tensor, mmf)

    def f_vector(vector):
        vin.inc(vector.memset(ones_r512[:], 1.0))
        for g in range(NG):
            for sub in range(GS):
                ncl = GS * g + sub
                vector.wait_ge(tmm.h, B["fb_tmm"] + ncl + 1)
                if g >= 1:
                    vector.wait_ge(mmf.h, B["fb_mmf"] + GS * DC * g)
                tv.inc(vector.tensor_copy(out=tsb[0:16, sub, :],
                                          in_=pD[0:16, 0:512]))
            for dc in range(DC):
                for sub in range(GS):
                    i = GS * (g * DC + dc) + sub
                    vector.wait_ge(ev.h, B["fb_ev"] + i + 1)
                    xls.wait(vector, i)
                    sls.wait(vector, i)
                    if i >= 2:
                        sto.wait(vector, i - 2)
                    dn.inc(vector.tensor_mul(out=prb[:, i % 2], in0=gst[:, i % 4],
                                             in1=scb[:, i % 2]))
                    ctx.w(vector, dn)
                    dn.inc(vector.tensor_add(out=ocb[:, i % 2], in0=xcb[:, i % 2],
                                             in1=prb[:, i % 2]))

    ctx.sec("sync", f_sync)
    ctx.sec("gpsimd", f_gpsimd)
    ctx.sec("scalar", f_scalar)
    ctx.sec("tensor", f_tensor)
    ctx.sec("vector", f_vector)
    ctx.barrier()


def ph_gate(ctx, src, dst):
    cfg = ctx.cfg
    DC, n = cfg.DC, cfg.n_own
    x_d, s_d, g_d, o_d = (ctx.dram(src), ctx.dram("sfm"), ctx.dram("gfm"),
                          ctx.dram(dst))
    xc = ctx.sbuf("gt_x", [PAR, 2, n], F32)
    sc_ = ctx.sbuf("gt_s", [PAR, 2, n], F32)
    gc = ctx.sbuf("gt_g", [PAR, 2, n], F32)
    pr = ctx.sbuf("gt_p", [PAR, 2, n], F32)
    oc_ = ctx.sbuf("gt_o", [PAR, 2, n], F32)
    xl = ctx.slots("gt_xl", 2)
    sl = ctx.slots("gt_sl", 2)
    gl = ctx.slots("gt_gl", 2)
    sts = ctx.slots("gt_st", 2)
    cv = ctx.sem("chain_v")
    dn = ctx.sem("gt_dn")
    B = {"dn": dn.v}

    def f_sync(sync):
        for dc in range(DC):
            if dc >= 2:
                sync.wait_ge(dn.h, B["dn"] + 2 * (dc - 1))
            xl.dma(sync, dc, xc[:, dc % 2], x_d[dc * 128:(dc + 1) * 128, :])
            sl.dma(sync, dc, sc_[:, dc % 2], s_d[dc * 128:(dc + 1) * 128, :])
            gl.dma(sync, dc, gc[:, dc % 2], g_d[dc * 128:(dc + 1) * 128, :])
            sync.wait_ge(dn.h, B["dn"] + 2 * (dc + 1))
            sts.dma(sync, dc, o_d[dc * 128:(dc + 1) * 128, :], oc_[:, dc % 2])
        sts.wait_all(sync, DC)

    def f_vector(vector):
        for dc in range(DC):
            sl.wait(vector, dc)
            gl.wait(vector, dc)
            dn.inc(vector.tensor_mul(out=pr[:, dc % 2], in0=sc_[:, dc % 2],
                                     in1=gc[:, dc % 2]))
            xl.wait(vector, dc)
            if dc >= 2:
                sts.wait(vector, dc - 2)
            ctx.w(vector, dn)
            dn.inc(vector.tensor_add(out=oc_[:, dc % 2], in0=xc[:, dc % 2],
                                     in1=pr[:, dc % 2]))

    ctx.sec("sync", f_sync)
    ctx.sec("vector", f_vector)
    ctx.barrier()


def ph_muprep(ctx):
    cfg = ctx.cfg
    DC, NS = cfg.DC, cfg.NSLOT
    mu_d, mun_d = ctx.dram("mu"), ctx.dram("mun")
    ch = ctx.sbuf("mp_ch", [PAR, 2, NS], F32)
    msq = ctx.sbuf("mp_sq", [PAR, 2, NS], BF16)
    rows = ctx.sbuf("mp_rows", [1, 2, NS], F32)
    rb = ctx.sbuf("mp_rb", [PAR, NS], F32)
    mst = ctx.sbuf("mp_mst", [PAR, 2, NS], BF16)
    ones_bf = ctx.sbuf("ones_bf_col", [PAR, 1], BF16)
    ones_f1 = ctx.sbuf("ones_f32_row", [1, PAR], F32)
    pA = ctx.psum("pA", [PAR, 1024], F32)
    pB = ctx.psum("pB", [PAR, 1024], F32)

    def pstat(c):
        return (pA if c < 2 else pB), (c % 2) * 512
    NC512 = NS // 512

    chs = ctx.slots("mp_ch", 2)
    sts = ctx.slots("mp_st", 2)
    sqd = ctx.sem("mp_sq"); mmd = ctx.sem("mp_mm"); evs = ctx.sem("mp_ev")
    rowd = ctx.sem("mp_row"); bcd = ctx.sem("mp_bc"); bce = ctx.sem("mp_bce")
    p2 = ctx.sem("mp_p2"); vin = ctx.sem("mp_vin")
    cv = ctx.sem("chain_v")
    B = {x.name: x.v for x in (sqd, mmd, evs, rowd, bcd, bce, p2, vin)}

    def f_sync(sync):
        for dc in range(DC):
            if dc >= 2:
                sync.wait_ge(sqd.h, B["mp_sq"] + dc - 1)
            chs.dma(sync, dc, ch[:, dc % 2], mu_d[dc * 128:(dc + 1) * 128, :])
        for dc in range(DC):
            if dc >= 2:
                sync.wait_ge(p2.h, B["mp_p2"] + dc - 1)
            sync.wait_ge(bce.h, B["mp_bce"] + 1)
            chs.dma(sync, DC + dc, ch[:, dc % 2], mu_d[dc * 128:(dc + 1) * 128, :])
            sync.wait_ge(p2.h, B["mp_p2"] + dc + 1)
            sts.dma(sync, dc, mun_d[dc * 128:(dc + 1) * 128, :], mst[:, dc % 2])
        sts.wait_all(sync, DC)
        chs.wait_all(sync, 2 * DC)

    def f_scalar(scalar):
        for dc in range(DC):
            chs.wait(scalar, dc)
            if dc >= 2:
                scalar.wait_ge(mmd.h, B["mp_mm"] + dc - 1)
            sqd.inc(scalar.activation(out=msq[:, dc % 2], in_=ch[:, dc % 2],
                                      func=AF.Square))
        scalar.wait_ge(evs.h, B["mp_ev"] + 1)
        rowd.inc(scalar.activation(out=rows[0:1, 1], in_=rows[0:1, 0],
                                   func=AF.Sqrt))

    def f_tensor(tensor):
        tensor.wait_ge(vin.h, B["mp_vin"] + 1)
        for dc in range(DC):
            tensor.wait_ge(sqd.h, B["mp_sq"] + dc + 1)
            for c in range(NC512):
                pt, po = pstat(c)
                inst = tensor.matmul(pt[0:1, po:po + 512], ones_bf[:],
                                     msq[:, dc % 2, c * 512:(c + 1) * 512],
                                     start=(dc == 0), stop=(dc == DC - 1))
            mmd.inc(inst)
        tensor.wait_ge(rowd.h, B["mp_row"] + 2)
        for c in range(NC512):
            pt, po = pstat(c)
            inst = tensor.matmul(pt[:, po:po + 512], ones_f1[:],
                                 rows[0:1, 1, c * 512:(c + 1) * 512],
                                 start=True, stop=True)
        bcd.inc(inst)
        ctx.w(tensor, bcd)

    def f_vector(vector):
        vector.memset(ones_bf[:], 1.0)
        vin.inc(vector.memset(ones_f1[:], 1.0))
        vector.wait_ge(mmd.h, B["mp_mm"] + DC)
        for c in range(NC512):
            pt, po = pstat(c)
            inst = vector.tensor_copy(out=rows[0:1, 0, c * 512:(c + 1) * 512],
                                      in_=pt[0:1, po:po + 512])
        evs.inc(inst)
        vector.wait_ge(rowd.h, B["mp_row"] + 1)
        cv.inc(vector.tensor_scalar_max(out=rows[0:1, 1], in0=rows[0:1, 1],
                                        scalar1=1e-12))
        ctx.w(vector, cv)
        rowd.inc(vector.reciprocal(out=rows[0:1, 1], in_=rows[0:1, 1]))
        vector.wait_ge(bcd.h, B["mp_bc"] + 1)
        for c in range(NC512):
            pt, po = pstat(c)
            inst = vector.tensor_copy(out=rb[:, c * 512:(c + 1) * 512],
                                      in_=pt[:, po:po + 512])
        bce.inc(inst)
        for dc in range(DC):
            chs.wait(vector, DC + dc)
            if dc >= 2:
                sts.wait(vector, dc - 2)
            p2.inc(vector.tensor_mul(out=mst[:, dc % 2], in0=ch[:, dc % 2],
                                     in1=rb[:]))

    ctx.sec("sync", f_sync)
    ctx.sec("scalar", f_scalar)
    ctx.sec("tensor", f_tensor)
    ctx.sec("vector", f_vector)
    ctx.barrier()


def ph_moea(ctx):
    cfg = ctx.cfg
    DC, n, NT, NS = cfg.DC, cfg.n_own, cfg.NT, cfg.NSLOT
    D = cfg.D
    XN = ctx.sbuf("XN", [PAR, DC, n], BF16)
    XLN = ctx.sbuf("XLN", [PAR, DC, n], BF16)
    XNflat = XN[:].rearrange("p a b -> p (a b)")
    rows = ctx.sbuf("ma_rows", [1, 2, n], F32)
    R2b = ctx.sbuf("ma_r2b", [PAR, n], F32)
    gb_sb = ctx.sbuf("ma_gb", [PAR, DC, 2], F32)
    xsq = ctx.sbuf("ma_xsq", [PAR, 2, n], BF16)
    tmpf = ctx.sbuf("ma_tmp", [PAR, 2, n], F32)
    sc_sb = ctx.sbuf("ma_sc", [1, 4], F32)
    negst = ctx.sbuf("ma_negst", [PAR, 1], F32)
    munb = ctx.sbuf("ma_mun", [PAR, 1, DC, 512], BF16)
    est = ctx.sbuf("ma_est", [PAR, 2, 512], BF16)
    rowZt = ctx.sbuf("ma_rzt", [PAR, NT, 4], F32)
    rowZ_sb = ctx.sbuf("ma_rz", [PAR, NT], F32)
    etile = ctx.sbuf("ma_et", [PAR, 2, 128], BF16)
    ust = ctx.sbuf("ma_ust", [PAR, 1, 2052], F32)
    ones_bf = ctx.sbuf("ones_bf_col", [PAR, 1], BF16)
    ones_f1 = ctx.sbuf("ones_f32_row", [1, PAR], F32)
    idf = ctx.sbuf("idf_bf", [PAR, 128], BF16)
    pA = ctx.psum("pA", [PAR, 1024], F32)
    pB = ctx.psum("pB", [PAR, 1024], F32)
    pC = ctx.psum("pC", [PAR, 1024], BF16)
    pC2 = ctx.psum("pC2", [PAR, 1024], BF16)
    pD = ctx.psum("pD", [PAR, 512], F32)

    def pstat(c):
        return (pA if c < 2 else pB), (c % 2) * 512
    NC5 = n // 512
    ST = NS // 128

    xn_d, var_d = ctx.dram("xn"), ctx.dram("varrow")
    mun_d, E_d, xlt_d = ctx.dram("mun"), ctx.dram("Ebuf"), ctx.dram("xlnT")
    rz_d, us_d = ctx.dram("rowZ"), ctx.dram("Usum")
    nc = ctx.nc
    ub_i = nc.dram_tensor("ma_ub", [NS * 2052], F32, kind="Internal").ap()
    ug_i = nc.dram_tensor("ma_ug", [NS * 2052], F32, kind="Internal").ap()
    groups = [[2 * i, 2 * i + 1] for i in range(cfg.n_cores // 2)]

    ld = ctx.dma_sem("ma_ld")
    muls = ctx.slots("ma_mul", 2)
    els = ctx.slots("ma_el", 2)
    stE = ctx.slots("ma_stE", 2)
    stX = ctx.slots("ma_stX", 2)
    stU = ctx.slots("ma_stU", 2)
    stZ = ctx.dma_sem("ma_stZ")
    cv = ctx.sem("chain_v")
    r2a = ctx.sem("ma_r2a"); r2s = ctx.sem("ma_r2s"); r2c = ctx.sem("ma_r2c")
    scd = ctx.sem("ma_scd")
    bcd = ctx.sem("ma_bcd"); bce = ctx.sem("ma_bce")
    xld = ctx.sem("ma_xld")
    sqd = ctx.sem("ma_sqd")
    nmm = ctx.sem("ma_nmm"); nev = ctx.sem("ma_nev"); nsq = ctx.sem("ma_nsq")
    srd = ctx.sem("ma_srd")
    sbc = ctx.sem("ma_sbc"); sbe = ctx.sem("ma_sbe")
    xnd = ctx.sem("ma_xnd")
    trd = ctx.sem("ma_trd"); cte = ctx.sem("ma_cte")
    lmm = ctx.sem("ma_lmm"); eed = ctx.sem("ma_eed")
    rzd = ctx.sem("ma_rzd")
    umm = ctx.sem("ma_umm"); uev = ctx.sem("ma_uev")
    ccd = ctx.sem("ma_ccd", keep=True); gth = ctx.dma_sem("ma_gth", sw=True)
    gidf = ctx.dma_sem("ma_gidf", sw=True)
    vin = ctx.sem("ma_vin")
    B = {x.name: x.v for x in (ld, cv, r2a, r2s, r2c, scd, bcd, bce, xld, sqd,
                               nmm, nev, nsq, srd, sbc, sbe, xnd, trd, cte,
                               lmm, eed, rzd, umm, uev, ccd, gth, stZ, vin,
                               gidf)}
    NLD = 3 + 2 * DC

    def f_sync(sync):
        ctx.dma(sync, XN[:], xn_d, ld)
        ctx.dma(sync, rows[0:1, 0], var_d, ld)
        ctx.dma(sync, sc_sb[0:1, 0:1], ctx.dram("moescale"), ld)
        for dc in range(DC):
            ctx.dma(sync, gb_sb[:, dc, 0:1],
                    ctx.dram("gamma")[dc * 128:(dc + 1) * 128, :], ld)
            ctx.dma(sync, gb_sb[:, dc, 1:2],
                    ctx.dram("beta")[dc * 128:(dc + 1) * 128, :], ld)
        for nt in range(NT):
            sync.wait_ge(cte.h, B["ma_cte"] + 2 * (nt + 1))
            stX.dma(sync, nt, xlt_d[nt * 128:(nt + 1) * 128, :],
                    XNflat[:, nt * D:(nt + 1) * D])
        for sc in range(4):
            if sc >= 1:
                sync.wait_ge(lmm.h, B["ma_lmm"] + NT * sc)
            muls.dma(sync, sc, munb[:, 0],
                     mun_d[:, sc * 512:(sc + 1) * 512]
                     .rearrange("(dc p) m -> p dc m", p=PAR))
            for nt in range(NT):
                k = sc * NT + nt
                sync.wait_ge(eed.h, B["ma_eed"] + k + 1)
                stE.dma(sync, k, E_d[nt * 128:(nt + 1) * 128,
                                     sc * 512:(sc + 1) * 512], est[:, k % 2])
        sync.wait_ge(rzd.h, B["ma_rzd"] + 1)
        ctx.dma(sync, rz_d, rowZ_sb[:], stZ)
        stE.wait_all(sync, 4 * NT)
        for st in range(ST):
            for nt in range(NT):
                k = st * NT + nt
                if k >= 2:
                    sync.wait_ge(umm.h, B["ma_umm"] + k - 1)
                els.dma(sync, k, etile[:, k % 2],
                        E_d[nt * 128:(nt + 1) * 128, st * 128:(st + 1) * 128])
            sync.wait_ge(uev.h, B["ma_uev"] + st + 1)
            stU.dma(sync, st,
                    ub_i.rearrange("(a b) -> a b", b=2052)[st * 128:(st + 1) * 128, :],
                    ust[:, 0])
        stE.wait_all(sync, 4 * NT)
        stX.wait_all(sync, NT)
        stU.wait_all(sync, ST)
        muls.wait_all(sync, 4)
        els.wait_all(sync, ST * NT)
        ctx.w(sync, ld)
        ctx.w(sync, stZ)

    def f_gpsimd(g):
        ctx.dma(g, idf[:], ctx.dram("ident"), gidf)
        stU.wait_all(g, ST)
        ccd.inc(g.collective_compute(
            "AllReduce", ALU.add, replica_groups=groups,
            ins=[ub_i.opt()], outs=[ug_i.opt()]))
        ctx.w(g, ccd)
        ctx.dma(g, us_d.rearrange("a b -> (a b)"), ug_i, gth)
        ctx.w(g, gth)

    def f_vector(vector):
        vector.memset(ones_bf[:], 1.0)
        vector.memset(ust[:, :, 2049:2052], 0.0)
        vin.inc(vector.memset(ones_f1[:], 1.0))
        vector.wait_ge(ld.h, B["ma_ld"] + 16 * NLD)
        cv.inc(vector.tensor_scalar_add(out=rows[0:1, 1], in0=rows[0:1, 0],
                                        scalar1=1e-5))
        ctx.w(vector, cv)
        cv.inc(vector.reciprocal(out=rows[0:1, 1], in_=rows[0:1, 1]))
        ctx.w(vector, cv)
        cv.inc(vector.tensor_mul(out=rows[0:1, 1], in0=rows[0:1, 0],
                                 in1=rows[0:1, 1]))
        ctx.w(vector, cv)
        r2a.inc(vector.tensor_scalar_add(out=rows[0:1, 1], in0=rows[0:1, 1],
                                         scalar1=1e-5))
        vector.wait_ge(r2s.h, B["ma_r2s"] + 1)
        r2c.inc(vector.reciprocal(out=rows[0:1, 1], in_=rows[0:1, 0]))
        vector.wait_ge(bcd.h, B["ma_bcd"] + 1)
        for c in range(NC5):
            pt, po = pstat(c)
            inst = vector.tensor_copy(out=R2b[:, c * 512:(c + 1) * 512],
                                      in_=pt[:, po:po + 512])
        bce.inc(inst)
        ctx.w(vector, bce)
        for dc in range(DC):
            xld.inc(vector.tensor_mul(out=tmpf[:, dc % 2], in0=XN[:, dc, :],
                                      in1=R2b[:]))
            ctx.w(vector, xld)
            xld.inc(vector.tensor_scalar(out=XLN[:, dc, :], in0=tmpf[:, dc % 2],
                                         scalar1=gb_sb[:, dc, 0:1],
                                         scalar2=gb_sb[:, dc, 1:2],
                                         op0=ALU.mult, op1=ALU.add))
        vector.wait_ge(nmm.h, B["ma_nmm"] + DC)
        for c in range(NC5):
            pt, po = pstat(c)
            inst = vector.tensor_copy(out=rows[0:1, 0, c * 512:(c + 1) * 512],
                                      in_=pt[0:1, po:po + 512])
        nev.inc(inst)
        vector.wait_ge(nsq.h, B["ma_nsq"] + 1)
        vector.wait_ge(scd.h, B["ma_scd"] + 1)
        cv.inc(vector.tensor_scalar_max(out=rows[0:1, 1], in0=rows[0:1, 1],
                                        scalar1=1e-12))
        ctx.w(vector, cv)
        cv.inc(vector.reciprocal(out=rows[0:1, 1], in_=rows[0:1, 1]))
        ctx.w(vector, cv)
        srd.inc(vector.tensor_scalar_mul(out=rows[0:1, 1], in0=rows[0:1, 1],
                                         scalar1=sc_sb[0:1, 1:2]))
        vector.wait_ge(sbc.h, B["ma_sbc"] + 1)
        for c in range(NC5):
            pt, po = pstat(c)
            inst = vector.tensor_copy(out=R2b[:, c * 512:(c + 1) * 512],
                                      in_=pt[:, po:po + 512])
        sbe.inc(inst)
        vector.wait_ge(sbc.h, B["ma_sbc"] + 2)
        sbe.inc(vector.tensor_copy(out=negst[:], in_=pD[:, 0:1]))
        for nt in range(NT):
            for half in range(2):
                vector.wait_ge(trd.h, B["ma_trd"] + 2 * nt + half + 1)
                pt = pC if half == 0 else pC2
                cte.inc(vector.tensor_copy(
                    out=XNflat[:, nt * D + half * 1024:nt * D + half * 1024 + 1024],
                    in_=pt[:, 0:1024]))
        vector.wait_ge(sqd.h, B["ma_sqd"] + DC)
        ctx.w(vector, sbe)
        for dc in range(DC):
            xnd.inc(vector.tensor_mul(out=XLN[:, dc, :], in0=XLN[:, dc, :],
                                      in1=R2b[:]))
        vector.wait_ge(eed.h, B["ma_eed"] + 4 * NT)
        cv.inc(vector.reduce_sum(out=rowZt[:, :, 0:1], in_=rowZt[:],
                                 axis=mybir.AxisListType.X))
        ctx.w(vector, cv)
        rzd.inc(vector.tensor_copy(out=rowZ_sb[:], in_=rowZt[:, :, 0]))
        for st in range(ST):
            vector.wait_ge(umm.h, B["ma_umm"] + NT * (st + 1))
            if st >= 1:
                stU.wait(vector, st - 1)
            for q in range(4):
                pt, po = pstat(q)
                vector.tensor_copy(out=ust[:, 0, q * 512:(q + 1) * 512],
                                   in_=pt[:, po:po + 512])
            uev.inc(vector.tensor_copy(out=ust[:, 0, 2048:2049],
                                       in_=pD[:, 0:1]))

    def f_scalar(scalar):
        scalar.wait_ge(ld.h, B["ma_ld"] + 16 * NLD)
        scd.inc(scalar.activation(out=sc_sb[0:1, 1:2], in_=sc_sb[0:1, 0:1],
                                  func=AF.Abs))
        ctx.w(scalar, scd)
        scd.inc(scalar.mul(out=sc_sb[0:1, 2:3], in_=sc_sb[0:1, 1:2], mul=-1.0))
        scalar.wait_ge(r2a.h, B["ma_r2a"] + 1)
        r2s.inc(scalar.activation(out=rows[0:1, 0], in_=rows[0:1, 1],
                                  func=AF.Sqrt))
        for dc in range(DC):
            scalar.wait_ge(xld.h, B["ma_xld"] + 2 * (dc + 1))
            if dc >= 2:
                scalar.wait_ge(nmm.h, B["ma_nmm"] + dc - 1)
            sqd.inc(scalar.activation(out=xsq[:, dc % 2], in_=XLN[:, dc, :],
                                      func=AF.Square))
        scalar.wait_ge(nev.h, B["ma_nev"] + 1)
        scalar.wait_ge(bce.h, B["ma_bce"] + 1)
        nsq.inc(scalar.activation(out=rows[0:1, 1], in_=rows[0:1, 0],
                                  func=AF.Sqrt))
        scalar.wait_ge(sbe.h, B["ma_sbe"] + 2)
        for sc in range(4):
            for nt in range(NT):
                k = sc * NT + nt
                scalar.wait_ge(lmm.h, B["ma_lmm"] + k + 1)
                if k >= 2:
                    stE.wait(scalar, k - 2)
                r = k % 4
                pt = (pA, pB)[r // 2]
                off = (r % 2) * 512
                eed.inc(scalar.activation(
                    out=est[:, k % 2], in_=pt[:, off:off + 512], func=AF.Exp,
                    bias=negst[:, 0:1], accum_out=rowZt[:, nt, sc:sc + 1]))

    def f_tensor(tensor):
        tensor.wait_ge(vin.h, B["ma_vin"] + 1)
        tensor.wait_ge(ld.h, B["ma_ld"] + 16 * NLD)
        tensor.wait_ge(r2c.h, B["ma_r2c"] + 1)
        for c in range(NC5):
            pt, po = pstat(c)
            inst = tensor.matmul(pt[:, po:po + 512], ones_f1[:],
                                 rows[0:1, 1, c * 512:(c + 1) * 512],
                                 start=True, stop=True)
        bcd.inc(inst)
        tensor.wait_ge(bce.h, B["ma_bce"] + 1)
        for dc in range(DC):
            tensor.wait_ge(sqd.h, B["ma_sqd"] + dc + 1)
            for c in range(NC5):
                pt, po = pstat(c)
                inst = tensor.matmul(pt[0:1, po:po + 512], ones_bf[:],
                                     xsq[:, dc % 2, c * 512:(c + 1) * 512],
                                     start=(dc == 0), stop=(dc == DC - 1))
            nmm.inc(inst)
        tensor.wait_ge(srd.h, B["ma_srd"] + 1)
        tensor.wait_ge(nev.h, B["ma_nev"] + 1)
        for c in range(NC5):
            pt, po = pstat(c)
            inst = tensor.matmul(pt[:, po:po + 512], ones_f1[:],
                                 rows[0:1, 1, c * 512:(c + 1) * 512],
                                 start=True, stop=True)
        sbc.inc(inst)
        tensor.wait_ge(scd.h, B["ma_scd"] + 2)
        sbc.inc(tensor.matmul(pD[:, 0:1], ones_f1[:], sc_sb[0:1, 2:3],
                              start=True, stop=True))
        tensor.wait_ge(gidf.h, B["ma_gidf"] + 16)
        tensor.wait_ge(xld.h, B["ma_xld"] + 2 * DC)
        for nt in range(NT):
            for half in range(2):
                hidx = 2 * nt + half
                pt = pC if half == 0 else pC2
                if hidx >= 2:
                    tensor.wait_ge(cte.h, B["ma_cte"] + hidx - 1)
                for j in range(8):
                    dc = half * 8 + j
                    inst = tensor.transpose(pt[:, j * 128:(j + 1) * 128],
                                            XLN[:, dc, nt * 128:(nt + 1) * 128],
                                            idf[:])
                trd.inc(inst)
        tensor.wait_ge(xnd.h, B["ma_xnd"] + DC)
        tensor.wait_ge(sbe.h, B["ma_sbe"] + 2)
        for sc in range(4):
            muls.wait(tensor, sc)
            for nt in range(NT):
                k = sc * NT + nt
                r = k % 4
                pt = (pA, pB)[r // 2]
                off = (r % 2) * 512
                if k >= 4:
                    tensor.wait_ge(eed.h, B["ma_eed"] + k - 3)
                for kc in range(DC):
                    inst = tensor.matmul(pt[:, off:off + 512],
                                         XLN[:, kc, nt * 128:(nt + 1) * 128],
                                         munb[:, 0, kc, :],
                                         start=(kc == 0), stop=(kc == DC - 1))
                lmm.inc(inst)
        tensor.wait_ge(eed.h, B["ma_eed"] + 4 * NT)
        tensor.wait_ge(cte.h, B["ma_cte"] + 2 * NT)
        for st in range(ST):
            if st >= 2:
                tensor.wait_ge(uev.h, B["ma_uev"] + st - 1)
            for nt in range(NT):
                k = st * NT + nt
                els.wait(tensor, k)
                for q in range(4):
                    pt, po = pstat(q)
                    tensor.matmul(pt[:, po:po + 512], etile[:, k % 2],
                                  XNflat[:, nt * D + q * 512:nt * D + (q + 1) * 512],
                                  start=(nt == 0), stop=(nt == NT - 1))
                umm.inc(tensor.matmul(pD[:, 0:1], etile[:, k % 2], ones_bf[:],
                                      start=(nt == 0), stop=(nt == NT - 1)))
        ctx.w(tensor, umm)

    ctx.sec("sync", f_sync)
    ctx.sec("gpsimd", f_gpsimd)
    ctx.sec("vector", f_vector)
    ctx.sec("scalar", f_scalar)
    ctx.sec("tensor", f_tensor)
    ctx.barrier()


def ph_moeb(ctx):
    cfg = ctx.cfg
    DC, NQC, n, NT, NS = cfg.DC, cfg.NQC, cfg.n_own, cfg.NT, cfg.NSLOT
    D, E = cfg.D, cfg.E
    ST = NS // 128
    SO = ctx.sbuf("SO", [PAR, E, D], BF16)
    ust = ctx.sbuf("mb_ust", [PAR, 1, 2052], F32)
    sist = ctx.sbuf("mb_sist", [PAR, 2, NS], BF16)
    sitb = ctx.sbuf("mb_sitb", [PAR, 2, DC, 128], BF16)
    rz2 = ctx.sbuf("mb_rz", [PAR, 2], F32)
    w1s = ctx.sbuf("mb_w1", [PAR, 2, DC, 512], BF16)
    w2s = ctx.sbuf("mb_w2", [PAR, 2, 4, D], BF16)
    b1f = ctx.sbuf("mb_b1", [PAR, E, 4], F32)
    b2r = ctx.sbuf("mb_b2", [1, 2, D], BF16)
    h_sb = ctx.sbuf("mb_h", [PAR, 4, 128], BF16)
    eblk = ctx.sbuf("mb_eblk", [PAR, 2, NS], BF16)
    cstg = ctx.sbuf("mb_cstg", [PAR, 2, NS], BF16)
    ctst = ctx.sbuf("mb_ctst", [PAR, 2, 1024], BF16)
    rowZ_sb = ctx.sbuf("mb_rowz", [PAR, NT], F32)
    cts = ctx.sbuf("mb_cts", [PAR, 16, 512], BF16)
    gst = ctx.sbuf("mb_g", [PAR, 2, 512], F32)
    ones_r128 = ctx.sbuf("ones_r128", [1, PAR], BF16)
    idf = ctx.sbuf("idf_bf", [PAR, 128], BF16)
    pA = ctx.psum("pA", [PAR, 1024], F32)
    pB = ctx.psum("pB", [PAR, 1024], F32)
    pC = ctx.psum("pC", [PAR, 1024], BF16)
    pC2 = ctx.psum("pC2", [PAR, 1024], BF16)
    pD = ctx.psum("pD", [PAR, 512], F32)
    pE = ctx.psum("pE", [PAR, 512], F32)

    us_d, E_d, rz_d = ctx.dram("Usum"), ctx.dram("Ebuf"), ctx.dram("rowZ")
    ct_d, g_d = ctx.dram("ctb"), ctx.dram("gfm")
    w1_d, b1_d = ctx.dram("W1"), ctx.dram("b1")
    w2_d, b2_d = ctx.dram("W2"), ctx.dram("b2")

    ld = ctx.dma_sem("mb_ld")
    uls = ctx.slots("mb_uls", 2)
    wl1 = ctx.slots("mb_wl1", 2)
    wl2 = ctx.slots("mb_wl2", 2)
    wl3 = ctx.slots("mb_wl3", 2)
    elb = ctx.slots("mb_elb", 2)
    ctl = ctx.slots("mb_ctl", 2)
    stC = ctx.slots("mb_stC", 2)
    stG = ctx.slots("mb_stG", 2)
    cv = ctx.sem("chain_v")
    vsi = ctx.sem("mb_vsi")
    trA = ctx.sem("mb_trA"); ctA = ctx.sem("mb_ctA")
    hmm_ = ctx.sem("mb_hmm")
    geld = ctx.sem("mb_geld")
    smm = ctx.sem("mb_smm")
    soe = ctx.sem("mb_soe")
    vcs = ctx.sem("mb_vcs")
    trC = ctx.sem("mb_trC"); ctC = ctx.sem("mb_ctC")
    fmm = ctx.sem("mb_fmm")
    fev = ctx.sem("mb_fev")
    gidf = ctx.dma_sem("mb_gidf", sw=True)
    vin = ctx.sem("mb_vin")
    B = {x.name: x.v for x in (ld, cv, vsi, trA, ctA, hmm_, geld, smm, soe,
                               vcs, trC, ctC, fmm, fev, vin, gidf)}

    def f_sync(sync):
        ctx.dma(sync, rowZ_sb[:], rz_d, ld)
        for e in range(E):
            for hc in range(4):
                ctx.dma(sync, b1f[:, e, hc:hc + 1],
                        b1_d[e, hc * 128:(hc + 1) * 128]
                        .rearrange("(a b) -> a b", b=1), ld)
        for st in range(ST):
            if st >= 1:
                sync.wait_ge(vsi.h, B["mb_vsi"] + st)
            uls.dma(sync, st, ust[:, 0],
                    us_d[st * 128:(st + 1) * 128, :])
        for nt in range(NT):
            if nt >= 2:
                sync.wait_ge(vcs.h, B["mb_vcs"] + nt - 1)
            elb.dma(sync, nt, eblk[:, nt % 2], E_d[nt * 128:(nt + 1) * 128, :])
            for half in range(2):
                k = 2 * nt + half
                sync.wait_ge(ctC.h, B["mb_ctC"] + k + 1)
                stC.dma(sync, k,
                        ct_d[half * 1024:(half + 1) * 1024,
                             nt * 128:(nt + 1) * 128]
                        .rearrange("(j p) m -> p j m", p=PAR),
                        ctst[:, k % 2].rearrange("p (j m) -> p j m", m=128))
        stC.wait_all(sync, 2 * NT)
        for ncl in range(NQC):
            if ncl >= 1:
                sync.wait_ge(fmm.h, B["mb_fmm"] + DC * ncl)
            ctl.dma(sync, ncl, cts[:],
                    ct_d[:, ncl * 512:(ncl + 1) * 512]
                    .rearrange("(st p) m -> p st m", p=PAR))
            for dc in range(DC):
                i = ncl * DC + dc
                sync.wait_ge(fev.h, B["mb_fev"] + i + 1)
                stG.dma(sync, i, g_d[dc * 128:(dc + 1) * 128,
                                     ncl * 512:(ncl + 1) * 512], gst[:, i % 2])
        stG.wait_all(sync, NQC * DC)
        uls.wait_all(sync, ST)
        elb.wait_all(sync, NT)
        ctl.wait_all(sync, NQC)
        ctx.w(sync, ld)

    def f_gpsimd(g):
        ctx.dma(g, idf[:], ctx.dram("ident"), gidf)
        ctx.w(g, gidf)

    def load_we(eng, e):
        wl1.dma(eng, e, w1s[:, e % 2],
                w1_d[e].rearrange("(kc p) h -> p kc h", p=PAR))
        wl2.dma(eng, e, w2s[:, e % 2],
                w2_d[e].rearrange("(hc p) d -> p hc d", p=PAR))
        wl3.dma(eng, e, b2r[0:1, e % 2], b2_d[e].rearrange("(b a) -> b a", b=1))

    def f_vector(vector):
        vin.inc(vector.memset(ones_r128[:], 1.0))
        def so_evacs(e):
            for dq in range(4):
                i = e * 4 + dq
                vector.wait_ge(smm.h, B["mb_smm"] + i + 1)
                r = i % 4
                pt = (pA, pB)[r // 2]
                off = (r % 2) * 512
                soe.inc(vector.tensor_copy(out=SO[:, e, dq * 512:(dq + 1) * 512],
                                           in_=pt[:, off:off + 512]))

        for st in range(ST):
            uls.wait(vector, st)
            cv.inc(vector.reciprocal(out=rz2[:, st % 2:st % 2 + 1],
                                     in_=ust[:, 0, 2048:2049]))
            ctx.w(vector, cv)
            if st >= 2:
                vector.wait_ge(trA.h, B["mb_trA"] + 2 * (st - 1))
            vsi.inc(vector.tensor_scalar_mul(out=sist[:, st % 2],
                                             in0=ust[:, 0, 0:2048],
                                             scalar1=rz2[:, st % 2:st % 2 + 1]))
            if st >= 2:
                vector.wait_ge(hmm_.h, B["mb_hmm"] + 4 * (st - 1))
            for half in range(2):
                k = 2 * st + half
                vector.wait_ge(trA.h, B["mb_trA"] + k + 1)
                pt = pC if half == 0 else pC2
                ctA.inc(vector.tensor_copy(
                    out=sitb[:, st % 2, half * 8:(half + 1) * 8, :],
                    in_=pt[:, 0:1024].rearrange("p (j m) -> p j m", m=128)))
            if st >= 1:
                so_evacs(st - 1)
        so_evacs(E - 1)
        for nt in range(NT):
            elb.wait(vector, nt)
            cv.inc(vector.reciprocal(out=rz2[:, nt % 2:nt % 2 + 1],
                                     in_=rowZ_sb[:, nt:nt + 1]))
            ctx.w(vector, cv)
            if nt >= 2:
                vector.wait_ge(trC.h, B["mb_trC"] + 2 * (nt - 1))
            vcs.inc(vector.tensor_scalar_mul(out=cstg[:, nt % 2],
                                             in0=eblk[:, nt % 2],
                                             scalar1=rz2[:, nt % 2:nt % 2 + 1]))
            for half in range(2):
                k = 2 * nt + half
                vector.wait_ge(trC.h, B["mb_trC"] + k + 1)
                if k >= 2:
                    stC.wait(vector, k - 2)
                pt = pC if half == 0 else pC2
                ctC.inc(vector.tensor_copy(out=ctst[:, k % 2], in_=pt[:, 0:1024]))

    def f_scalar(scalar):
        load_we(scalar, 0)
        load_we(scalar, 1)
        scalar.wait_ge(ld.h, B["mb_ld"] + 16 * (1 + 4 * E))
        for e in range(E):
            if e >= 1:
                scalar.wait_ge(smm.h, B["mb_smm"] + 4 * e)
            for hc in range(4):
                i = e * 4 + hc
                scalar.wait_ge(hmm_.h, B["mb_hmm"] + i + 1)
                pt = pD if hc % 2 == 0 else pE
                geld.inc(scalar.activation(out=h_sb[:, hc, :], in_=pt[:, 0:128],
                                           func=AF.Gelu,
                                           bias=b1f[:, e, hc:hc + 1]))
            if e + 2 < E:
                scalar.wait_ge(hmm_.h, B["mb_hmm"] + 4 * (e + 1))
                scalar.wait_ge(smm.h, B["mb_smm"] + 4 * (e + 1))
                load_we(scalar, e + 2)
        for ncl in range(NQC):
            for dc in range(DC):
                i = ncl * DC + dc
                scalar.wait_ge(fmm.h, B["mb_fmm"] + i + 1)
                if i >= 2:
                    stG.wait(scalar, i - 2)
                r = i % 4
                pt = (pA, pB)[r // 2]
                off = (r % 2) * 512
                fev.inc(scalar.activation(out=gst[:, i % 2],
                                          in_=pt[:, off:off + 512], func=AF.Silu))
        wl1.wait_all(scalar, E)
        wl2.wait_all(scalar, E)
        wl3.wait_all(scalar, E)

    def f_tensor(tensor):
        tensor.wait_ge(vin.h, B["mb_vin"] + 1)
        tensor.wait_ge(ld.h, B["mb_ld"] + 16 * (1 + 4 * E))
        tensor.wait_ge(gidf.h, B["mb_gidf"] + 16)
        for e in range(E):
            tensor.wait_ge(vsi.h, B["mb_vsi"] + e + 1)
            for half in range(2):
                k = 2 * e + half
                pt = pC if half == 0 else pC2
                if k >= 2:
                    tensor.wait_ge(ctA.h, B["mb_ctA"] + k - 1)
                for j in range(8):
                    dcc = half * 8 + j
                    inst = tensor.transpose(pt[:, j * 128:(j + 1) * 128],
                                            sist[:, e % 2,
                                                 dcc * 128:(dcc + 1) * 128],
                                            idf[:])
                trA.inc(inst)
            wl1.wait(tensor, e)
            tensor.wait_ge(ctA.h, B["mb_ctA"] + 2 * (e + 1))
            for hc in range(4):
                i = e * 4 + hc
                pt = pD if hc % 2 == 0 else pE
                if i >= 2:
                    tensor.wait_ge(geld.h, B["mb_geld"] + i - 1)
                for kc in range(DC):
                    inst = tensor.matmul(pt[:, 0:128],
                                         w1s[:, e % 2, kc, hc * 128:(hc + 1) * 128],
                                         sitb[:, e % 2, kc, :],
                                         start=(kc == 0), stop=(kc == DC - 1))
                hmm_.inc(inst)
            wl2.wait(tensor, e)
            wl3.wait(tensor, e)
            tensor.wait_ge(geld.h, B["mb_geld"] + 4 * (e + 1))
            for dq in range(4):
                i = e * 4 + dq
                r = i % 4
                pt = (pA, pB)[r // 2]
                off = (r % 2) * 512
                if i >= 4:
                    tensor.wait_ge(soe.h, B["mb_soe"] + i - 3)
                for hc in range(4):
                    tensor.matmul(pt[:, off:off + 512], h_sb[:, hc, :],
                                  w2s[:, e % 2, hc, dq * 512:(dq + 1) * 512],
                                  start=(hc == 0), stop=False)
                smm.inc(tensor.matmul(pt[:, off:off + 512], ones_r128[:],
                                      b2r[0:1, e % 2, dq * 512:(dq + 1) * 512],
                                      start=False, stop=True))
        tensor.wait_ge(soe.h, B["mb_soe"] + 4 * E)
        for nt in range(NT):
            tensor.wait_ge(vcs.h, B["mb_vcs"] + nt + 1)
            for half in range(2):
                k = 2 * nt + half
                pt = pC if half == 0 else pC2
                if k >= 2:
                    tensor.wait_ge(ctC.h, B["mb_ctC"] + k - 1)
                for j in range(8):
                    scc = half * 8 + j
                    inst = tensor.transpose(pt[:, j * 128:(j + 1) * 128],
                                            cstg[:, nt % 2,
                                                 scc * 128:(scc + 1) * 128],
                                            idf[:])
                trC.inc(inst)
        for ncl in range(NQC):
            ctl.wait(tensor, ncl)
            for dc in range(DC):
                i = ncl * DC + dc
                r = i % 4
                pt = (pA, pB)[r // 2]
                off = (r % 2) * 512
                if i >= 4:
                    tensor.wait_ge(fev.h, B["mb_fev"] + i - 3)
                for st in range(ST):
                    inst = tensor.matmul(pt[:, off:off + 512],
                                         SO[:, st, dc * 128:(dc + 1) * 128],
                                         cts[:, st, :],
                                         start=(st == 0), stop=(st == ST - 1))
                fmm.inc(inst)
        ctx.w(tensor, fmm)

    ctx.sec("sync", f_sync)
    ctx.sec("gpsimd", f_gpsimd)
    ctx.sec("vector", f_vector)
    ctx.sec("scalar", f_scalar)
    ctx.sec("tensor", f_tensor)
    ctx.barrier()



# version 14
# speedup vs baseline: 1.3856x; 1.0167x over previous
import numpy as np
import ml_dtypes
from contextlib import ExitStack
import concourse.bass as bass
import concourse.mybir as mybir
from concourse.bass_utils import run_bass_kernel_spmd

F32 = mybir.dt.float32
BF16 = mybir.dt.bfloat16
I32 = mybir.dt.int32
AF = mybir.ActivationFunctionType
ALU = mybir.AluOpType
PAR = 128

ENGINES = ("sync", "tensor", "vector", "scalar", "gpsimd")


class Cfg:
    def __init__(self, B=4, S=4096):
        self.B = B
        self.S = S
        self.D = 2048
        self.H16 = 16
        self.HD = 128
        self.FF = 8192
        self.R = 16
        self.E = 16
        self.SLOT = 128
        self.NSLOT = 2048
        self.MH = 512
        self.n_cores = 2 * B
        self.NT_TOT = S // 128
        self.NT = self.NT_TOT // 2
        self.n_own = self.NT * 128
        self.NQC = self.n_own // 512
        self.DC = self.D // 128
        self.FC = self.FF // 128
        assert self.NT % 4 == 0


def _rope0_perm(H=16, hd=128):
    perm = []
    for h in range(H):
        base = h * hd
        perm += [base + 2 * j for j in range(hd // 2)]
        perm += [base + 2 * j + 1 for j in range(hd // 2)]
    return np.array(perm)


def _rope1_perm(H, hd=128):
    perm = []
    half = hd // 2
    for h in range(H):
        base = h * hd
        perm += [base + 2 * j for j in range(half // 2)]
        perm += [base + half + 2 * j for j in range(half // 2)]
        perm += [base + 2 * j + 1 for j in range(half // 2)]
        perm += [base + half + 2 * j + 1 for j in range(half // 2)]
    return np.array(perm)


def _rope0_tables(pos, hd=128):
    theta = 10000.0 ** (-np.arange(0, hd, 2, dtype=np.float32) / hd)
    f = pos[None, :].astype(np.float32) * theta[:, None]
    return np.cos(f).astype(np.float32), np.sin(f).astype(np.float32)


def _rope1_consts(hd=128, H=16):
    half = hd // 2
    theta = 1.0 / 10000.0 ** (np.arange(0, half, 2, dtype=np.float32) / half)
    f = np.arange(H, dtype=np.float32)[None, :] * theta[:, None]
    c = np.concatenate([np.cos(f), np.cos(f)], axis=0)
    s = np.concatenate([np.sin(f), np.sin(f)], axis=0)
    return c.astype(np.float32), s.astype(np.float32)


def _tile_w(wT, ocw=128):
    Din, Dout = wT.shape
    t = wT.reshape(Din // 128, 128, Dout // ocw, ocw)
    return np.ascontiguousarray(t.transpose(2, 1, 0, 3))


def host_prepare(cfg, inputs):
    D, FF = cfg.D, cfg.FF
    p0 = _rope0_perm()
    p1 = _rope1_perm(16)
    p1k = _rope1_perm(1)

    sh = {}
    sh["wq0T"] = _tile_w(np.asarray(inputs["L0_attn_wq"])[p0].T).astype(ml_dtypes.bfloat16)
    sh["bq0"] = np.ascontiguousarray(np.asarray(inputs["L0_attn_bq"])[p0])[None, :]
    sh["wk0T"] = _tile_w(np.asarray(inputs["L0_attn_wk"])[p0].T).astype(ml_dtypes.bfloat16)
    sh["bk0"] = np.ascontiguousarray(np.asarray(inputs["L0_attn_bk"])[p0])[None, :]
    sh["wv0T"] = _tile_w(np.asarray(inputs["L0_attn_wv"]).T, ocw=512).astype(ml_dtypes.bfloat16)
    sh["bv0"] = np.asarray(inputs["L0_attn_bv"])[None, :].copy()
    sh["wo0T"] = _tile_w(np.asarray(inputs["L0_attn_wo"]).T).astype(ml_dtypes.bfloat16)
    sh["bo0"] = np.asarray(inputs["L0_attn_bo"])[None, :].copy()
    sh["wupTq"] = _tile_w(np.asarray(inputs["L0_ffn_w_up_q"]).T)
    sh["scupT"] = _tile_w(
        np.repeat(np.asarray(inputs["L0_ffn_w_up_scale"]), 64, axis=1).T).astype(
        ml_dtypes.bfloat16)
    sh["bup"] = np.asarray(inputs["L0_ffn_b_up"])[None, :].copy()
    sh["upA"] = np.asarray(inputs["L0_ffn_w_up_lora_a"]).copy()
    sh["upB"] = np.asarray(inputs["L0_ffn_w_up_lora_b"]).copy()
    wdnq = np.asarray(inputs["L0_ffn_w_down_q"]).T
    scdn = np.repeat(np.asarray(inputs["L0_ffn_w_down_scale"]), 64, axis=1).T
    sh["wdnTq"] = np.ascontiguousarray(
        wdnq.reshape(64, 128, 16, 128).transpose(2, 1, 0, 3))
    sh["scdnT"] = np.ascontiguousarray(
        scdn.reshape(64, 128, 16, 128).transpose(2, 1, 0, 3)).astype(
        ml_dtypes.bfloat16)
    sh["bdn"] = np.asarray(inputs["L0_ffn_b_down"])[None, :].copy()
    sh["dnA"] = np.asarray(inputs["L0_ffn_w_down_lora_a"]).copy()
    sh["dnB"] = np.asarray(inputs["L0_ffn_w_down_lora_b"]).copy()

    sh["wq1T"] = _tile_w(np.asarray(inputs["L1_attn_Wq"])[p1].T).astype(ml_dtypes.bfloat16)
    sh["wk1T"] = _tile_w(np.asarray(inputs["L1_attn_Wk"])[p1k].T).astype(ml_dtypes.bfloat16)
    sh["wv1T"] = _tile_w(np.asarray(inputs["L1_attn_Wv"]).T).astype(ml_dtypes.bfloat16)
    sh["wo1T"] = _tile_w(np.asarray(inputs["L1_attn_Wo"]).T).astype(ml_dtypes.bfloat16)

    sh["gamma"] = np.asarray(inputs["L1_ffn_gamma"])[:, None].copy()
    sh["beta"] = np.asarray(inputs["L1_ffn_beta"])[:, None].copy()
    sh["mu"] = np.ascontiguousarray(np.asarray(inputs["L1_ffn_mu"]).reshape(D, cfg.NSLOT))
    sh["moescale"] = np.asarray(inputs["L1_ffn_scale"]).reshape(1, 1).copy()
    sh["W1"] = np.asarray(inputs["L1_ffn_W1"]).astype(ml_dtypes.bfloat16)
    sh["b1"] = np.asarray(inputs["L1_ffn_b1"]).copy()
    sh["W2"] = np.asarray(inputs["L1_ffn_W2"]).astype(ml_dtypes.bfloat16)
    sh["b2"] = np.asarray(inputs["L1_ffn_b2"]).astype(ml_dtypes.bfloat16)

    c1, s1 = _rope1_consts(128, 16)
    sh["cos1"] = c1
    sh["sin1"] = s1
    sh["ident"] = np.eye(128, dtype=np.float32)

    x_full = np.asarray(inputs["L0_attn_x"], np.float32)
    percore = []
    for c in range(cfg.n_cores):
        b, par = c // 2, c % 2
        tiles = [2 * i + par for i in range(cfg.NT)]
        pos = np.concatenate([np.arange(t * 128, (t + 1) * 128) for t in tiles])
        pc = {}
        pc["x_in"] = np.ascontiguousarray(
            x_full[b].reshape(cfg.NT_TOT, 128, D)[tiles])
        c0, s0 = _rope0_tables(pos)
        pc["cos0"] = c0
        pc["sin0"] = s0
        m = np.zeros((8, 128, 512), np.float32)
        for t in range(8):
            for j in range(4):
                kvpos = t * 128 + np.arange(128)
                qpos = (2 * j + par) * 128 + np.arange(128)
                m[t][:, j * 128:(j + 1) * 128] = (
                    kvpos[:, None] <= qpos[None, :]).astype(np.float32)
        pc["masks1"] = m
        percore.append(pc)
    return sh, percore


class Sem:
    __slots__ = ("h", "v", "name")

    def __init__(self, h, name):
        self.h = h
        self.v = 0
        self.name = name

    def inc(self, inst, k=1):
        inst.then_inc(self.h, k)
        self.v += k
        return inst


class Ctx:

    def __init__(self, nc, cfg, stk, io_decl):
        self.nc = nc
        self.cfg = cfg
        self.stk = stk
        self.phase_stk = None
        self.daps = {}
        self._sems = {}
        self._retired = []
        self._shared_sems = set()
        self.block = None
        self.done = {e: self.sem(f"done_{e}") for e in ENGINES}
        self._shared_sems = {f"done_{e}" for e in ENGINES}
        self._bar = {e: 0 for e in ENGINES}
        self.io_decl = io_decl
        self.sb_tensors = {}
        self._phase_i = 0
        self._hw_pool = []
        self._sw_pool = []
        self._hw_cur = 0
        self._sw_cur = 0
        self._dma_names = set()

    def _fresh(self, name):
        self._sem_n = getattr(self, "_sem_n", 0) + 1
        h = self.stk.enter_context(self.nc.semaphore(f"s{self._sem_n}_{name}"))
        return Sem(h, name)

    def sem(self, name, keep=False):
        if name not in self._sems:
            if keep:
                self._sems[name] = self._fresh(name)
                self._shared_sems.add(name)
            elif self._retired:
                sm = self._retired.pop()
                sm.name = name
                self._sems[name] = sm
            else:
                self._sems[name] = self._fresh(name)
        return self._sems[name]

    def dma_sem(self, name, sw=False):
        if name in self._sems:
            return self._sems[name]
        pool = self._sw_pool if sw else self._hw_pool
        cur = self._sw_cur if sw else self._hw_cur
        if cur < len(pool):
            sm = pool[cur]
            sm.name = name
        else:
            sm = self._fresh(name)
            pool.append(sm)
        if sw:
            self._sw_cur += 1
        else:
            self._hw_cur += 1
        self._sems[name] = sm
        self._dma_names.add(name)
        return sm

    def dram(self, name):
        if name not in self.daps:
            kind, shape, dt = self.io_decl[name]
            k = {"in": "ExternalInput", "out": "ExternalOutput",
                 "internal": "Internal"}[kind]
            self.daps[name] = self.nc.dram_tensor(name, list(shape), dt, kind=k).ap()
        return self.daps[name]

    def _alloc_stack(self):
        return self.phase_stk if self.phase_stk is not None else self.stk

    def sbuf(self, name, shape, dt):
        if name not in self.sb_tensors:
            self.sb_tensors[name] = self._alloc_stack().enter_context(
                self.nc.sbuf_tensor(f"sb{self._phase_i}_{name}", list(shape), dt))
        return self.sb_tensors[name]

    def psum(self, name, shape, dt):
        if name not in self.sb_tensors:
            self.sb_tensors[name] = self._alloc_stack().enter_context(
                self.nc.psum_tensor(f"ps{self._phase_i}_{name}", list(shape), dt))
        return self.sb_tensors[name]

    def begin_block(self):
        self.block = self.stk.enter_context(self.nc.Block())
        self.junk = self.stk.enter_context(
            self.nc.sbuf_tensor("sb_junk", [PAR, 8], F32))
        self.phase_stk = ExitStack()

    def sec(self, eng_name, fn):
        bar = dict(self._bar)
        done = self.done[eng_name]

        def body(eng):
            for e in ENGINES:
                if bar[e] > 0:
                    eng.wait_ge(self.done[e].h, bar[e])
            fn(eng)
            if eng_name == "tensor":
                done.inc(eng.nop())
            elif eng_name == "vector":
                done.inc(eng.memset(self.junk[0:1, 0:1], 0.0))
            elif eng_name == "gpsimd":
                done.inc(eng.memset(self.junk[0:1, 2:3], 0.0))
            elif eng_name == "scalar":
                done.inc(eng.memzero(self.junk[0:1, 1:2]))
            else:
                done.inc(eng.nop())

        getattr(self.block, eng_name)(body)

    def barrier(self):
        self._bar = {e: self.done[e].v for e in ENGINES}
        if self.phase_stk is not None:
            self.phase_stk.close()
            self.phase_stk = ExitStack()
        self.sb_tensors.clear()
        self._phase_i += 1
        for n in list(self._sems):
            if n in self._shared_sems:
                continue
            sm = self._sems.pop(n)
            if n in self._dma_names:
                continue
            self._retired.append(sm)
        self._dma_names.clear()
        self._hw_cur = 0
        self._sw_cur = 0

    def dma(self, eng, out, in_, sem, **kw):
        inst = eng.dma_start(out=out, in_=in_, **kw)
        sem.inc(inst, 16)
        return inst

    def w(self, eng, sem, v=None):
        eng.wait_ge(sem.h, sem.v if v is None else v)

    def slots(self, name, n, per=1, sw=False):
        return Slots(self, name, n, per, sw)


class Slots:

    def __init__(self, ctx, name, n, per=1, sw=False):
        self.ctx = ctx
        self.sems = [ctx.dma_sem(f"{name}_s{i}", sw=sw) for i in range(n)]
        self.base = [s.v for s in self.sems]
        self.n = n
        self.per = per

    def dma(self, eng, k, out, in_, **kw):
        self.ctx.dma(eng, out, in_, self.sems[k % self.n], **kw)

    def wait(self, eng, k):
        s = self.sems[k % self.n]
        eng.wait_ge(s.h, self.base[k % self.n] + 16 * self.per * (k // self.n + 1))

    def wait_all(self, eng, total_uses):
        for i, s in enumerate(self.sems):
            uses = (total_uses - i + self.n - 1) // self.n
            if uses > 0:
                eng.wait_ge(s.h, self.base[i] + 16 * self.per * uses)


def io_table(cfg):
    D, FF, R, NS = cfg.D, cfg.FF, cfg.R, cfg.NSLOT
    n, NT = cfg.n_own, cfg.NT
    t = {
        "x_in": ([NT, 128, D], F32),
        "cos0": ([64, n], F32), "sin0": ([64, n], F32),
        "masks1": ([8, 128, 512], F32),
        "wq0T": ([16, 128, 16, 128], BF16), "bq0": ([1, D], F32),
        "wk0T": ([16, 128, 16, 128], BF16), "bk0": ([1, D], F32),
        "wv0T": ([4, 128, 16, 512], BF16), "bv0": ([1, D], F32),
        "wo0T": ([16, 128, 16, 128], BF16), "bo0": ([1, D], F32),
        "wupTq": ([64, 128, 16, 128], I32), "scupT": ([64, 128, 16, 128], BF16),
        "bup": ([1, FF], F32),
        "upA": ([D, R], F32), "upB": ([R, FF], F32),
        "wdnTq": ([16, 128, 64, 128], I32), "scdnT": ([16, 128, 64, 128], BF16),
        "bdn": ([1, D], F32),
        "dnA": ([FF, R], F32), "dnB": ([R, D], F32),
        "wq1T": ([16, 128, 16, 128], BF16), "wk1T": ([1, 128, 16, 128], BF16),
        "wv1T": ([1, 128, 16, 128], BF16),
        "wo1T": ([16, 128, 16, 128], BF16),
        "gamma": ([D, 1], F32), "beta": ([D, 1], F32), "mu": ([D, NS], F32),
        "moescale": ([1, 1], F32),
        "W1": ([16, D, 512], BF16), "b1": ([16, 512], F32),
        "W2": ([16, 512, D], BF16), "b2": ([16, D], BF16),
        "cos1": ([64, 16], F32), "sin1": ([64, 16], F32),
        "ident": ([128, 128], F32),
        "xres": ([D, n], F32),
        "xres1": ([D, n], F32),
        "xres2": ([D, n], F32),
        "ctb": ([NS, n], BF16),
        "xn": ([PAR, D // 128, n], BF16),
        "varrow": ([1, n], F32),
        "qfm": ([D, n], BF16),
        "kown0": ([D, n], BF16), "vown0": ([16, PAR, NT, 128], BF16),
        "kg0": ([2, D, n], BF16), "vg0": ([2, PAR, NT, 16, 128], BF16),
        "kown1": ([128, n], BF16), "vown1": ([PAR, NT, 128], BF16),
        "kg1": ([2, 128, n], BF16), "vg1": ([2, PAR, NT, 128], BF16),
        "qn2": ([1, 16], F32), "kn2own": ([1, 16], F32), "kn2": ([1, 16], F32),
        "sfm": ([D, n], F32),
        "x2b": ([PAR, n // 512, FF // 128, 512], BF16),
        "wdnbf": ([16, 128, FF], BF16),
        "gfm": ([D, n], F32),
        "mun": ([D, NS], BF16),
        "Ebuf": ([n, NS], BF16),
        "xlnT": ([n, D], BF16),
        "rowZ": ([128, NT], F32),
        "Uown": ([NS, 2052], F32),
        "Usum": ([NS, 2052], F32),
        "out": ([NT, 128, D], F32),
    }
    return t



def ph_in(ctx):
    cfg = ctx.cfg
    x_in, xres, ident = ctx.dram("x_in"), ctx.dram("xres"), ctx.dram("ident")
    xt = ctx.sbuf("ph_in_xt", [PAR, 2, cfg.D], F32)
    xo = ctx.sbuf("ph_in_xo", [PAR, 2, 4, 128], F32)
    idf = ctx.sbuf("idf32", [PAR, 128], F32)
    pB = ctx.psum("pB", [PAR, 1024], F32)
    ldi = ctx.dma_sem("in_ldi")
    xts = ctx.slots("in_xt", 2)
    sts = ctx.slots("in_st", 2, per=4)
    trd = ctx.sem("in_tr"); evd = ctx.sem("in_ev")
    b_ldi, b_tr, b_ev = ldi.v, trd.v, evd.v
    NT, DC = cfg.NT, cfg.DC
    NG = DC // 4

    def f_sync(sync):
        ctx.dma(sync, idf[:], ident, ldi)
        for t in range(NT):
            if t >= 2:
                sync.wait_ge(trd.h, b_tr + (t - 1) * NG)
            xts.dma(sync, t, xt[:, t % 2, :], x_in[t])
            for g in range(NG):
                k = t * NG + g
                sync.wait_ge(evd.h, b_ev + k + 1)
                for j in range(4):
                    dc = g * 4 + j
                    sts.dma(sync, k, xres[dc * 128:(dc + 1) * 128,
                                          t * 128:(t + 1) * 128], xo[:, k % 2, j, :])
        sts.wait_all(sync, NT * NG)
        xts.wait_all(sync, NT)
        ctx.w(sync, ldi)

    def f_tensor(tensor):
        tensor.wait_ge(ldi.h, b_ldi + 16)
        for t in range(NT):
            xts.wait(tensor, t)
            for g in range(NG):
                k = t * NG + g
                if k >= 2:
                    tensor.wait_ge(evd.h, b_ev + k - 1)
                half = (k % 2) * 512
                for j in range(4):
                    dc = g * 4 + j
                    inst = tensor.transpose(
                        pB[:, half + j * 128:half + (j + 1) * 128],
                        xt[:, t % 2, dc * 128:(dc + 1) * 128], idf[:])
                trd.inc(inst)
        ctx.w(tensor, trd)

    def f_vector(vector):
        for t in range(NT):
            for g in range(NG):
                k = t * NG + g
                vector.wait_ge(trd.h, b_tr + k + 1)
                if k >= 2:
                    sts.wait(vector, k - 2)
                half = (k % 2) * 512
                inst = vector.tensor_copy(out=xo[:, k % 2], in_=pB[:, half:half + 512])
                evd.inc(inst)

    ctx.sec("sync", f_sync)
    ctx.sec("tensor", f_tensor)
    ctx.sec("vector", f_vector)
    ctx.barrier()


def ph_out(ctx, src="xres2"):
    cfg = ctx.cfg
    xres, out, ident = ctx.dram(src), ctx.dram("out"), ctx.dram("ident")
    xi = ctx.sbuf("ph_out_xi", [PAR, 2, cfg.DC, 128], F32)
    xo = ctx.sbuf("ph_out_xo", [PAR, 2, cfg.D], F32)
    idf = ctx.sbuf("idf32", [PAR, 128], F32)
    pB = ctx.psum("pB", [PAR, 1024], F32)
    ldi = ctx.dma_sem("out_ldi")
    xis = ctx.slots("out_xi", 2)
    sts = ctx.slots("out_st", 2)
    trd = ctx.sem("out_tr"); evd = ctx.sem("out_ev")
    b_ldi, b_tr, b_ev = ldi.v, trd.v, evd.v
    NT, DC = ctx.cfg.NT, ctx.cfg.DC
    NG = DC // 4

    def f_sync(sync):
        ctx.dma(sync, idf[:], ident, ldi)
        for t in range(NT):
            if t >= 2:
                sync.wait_ge(trd.h, b_tr + (t - 1) * NG)
            xis.dma(sync, t, xi[:, t % 2],
                    xres[:, t * 128:(t + 1) * 128].rearrange("(dc p) n -> p dc n", p=PAR))
            sync.wait_ge(evd.h, b_ev + (t + 1) * NG)
            sts.dma(sync, t, out[t], xo[:, t % 2])
        sts.wait_all(sync, NT)
        xis.wait_all(sync, NT)
        ctx.w(sync, ldi)

    def f_tensor(tensor):
        tensor.wait_ge(ldi.h, b_ldi + 16)
        for t in range(NT):
            xis.wait(tensor, t)
            for g in range(NG):
                k = t * NG + g
                if k >= 2:
                    tensor.wait_ge(evd.h, b_ev + k - 1)
                half = (k % 2) * 512
                for j in range(4):
                    dc = g * 4 + j
                    inst = tensor.transpose(
                        pB[:, half + j * 128:half + (j + 1) * 128],
                        xi[:, t % 2, dc, :], idf[:])
                trd.inc(inst)
        ctx.w(tensor, trd)

    def f_vector(vector):
        for t in range(NT):
            for g in range(NG):
                k = t * NG + g
                vector.wait_ge(trd.h, b_tr + k + 1)
                if g == 0 and t >= 2:
                    sts.wait(vector, t - 2)
                half = (k % 2) * 512
                inst = vector.tensor_copy(
                    out=xo[:, t % 2, g * 512:(g + 1) * 512], in_=pB[:, half:half + 512])
                evd.inc(inst)

    ctx.sec("sync", f_sync)
    ctx.sec("tensor", f_tensor)
    ctx.sec("vector", f_vector)
    ctx.barrier()


def ph_ln(ctx, src):
    cfg = ctx.cfg
    n, DC = cfg.n_own, cfg.DC
    D = cfg.D
    xres, xn, varrow = ctx.dram(src), ctx.dram("xn"), ctx.dram("varrow")
    ch = ctx.sbuf("ln_ch", [PAR, 2, n], F32)
    xbf = ctx.sbuf("ln_xbf", [PAR, 2, n], BF16)
    xsq = ctx.sbuf("ln_xsq", [PAR, 2, n], BF16)
    rows = ctx.sbuf("ln_rows", [1, 8, n], F32)
    ones_bf = ctx.sbuf("ones_bf_col", [PAR, 1], BF16)
    ones_f1 = ctx.sbuf("ones_f32_row", [1, PAR], F32)
    Ab = ctx.sbuf("ln_Ab", [PAR, n], F32)
    Bb = ctx.sbuf("ln_Bb", [PAR, n], F32)
    tmp = ctx.sbuf("ln_tmp", [PAR, 2, n], F32)
    xns = ctx.sbuf("ln_xns", [PAR, 2, n], BF16)
    pA = ctx.psum("pA", [PAR, 1024], F32)
    pB = ctx.psum("pB", [PAR, 1024], F32)

    def pstat(c):
        return (pA if c < 2 else pB), (c % 2) * 512
    NC512 = n // 512

    chs = ctx.slots("ln_ch", 2)
    sts = ctx.slots("ln_xns", 2)
    stv = ctx.dma_sem("ln_stv")
    cst = ctx.sem("ln_cst"); sqd = ctx.sem("ln_sq")
    mmd = ctx.sem("ln_mm"); evs = ctx.sem("ln_ev"); rowd = ctx.sem("ln_row")
    bcd = ctx.sem("ln_bc"); bce = ctx.sem("ln_bce"); p2 = ctx.sem("ln_p2")
    cv = ctx.sem("chain_v")
    B = {s.name: s.v for s in (cst, sqd, mmd, evs, rowd, bcd, bce, p2, stv)}
    vin = ctx.sem("ln_vin")
    B["ln_vin"] = vin.v

    def f_sync(sync):
        for dc in range(DC):
            if dc >= 2:
                sync.wait_ge(cst.h, B["ln_cst"] + dc - 1)
                sync.wait_ge(sqd.h, B["ln_sq"] + dc - 1)
            chs.dma(sync, dc, ch[:, dc % 2], xres[dc * 128:(dc + 1) * 128, :])
        sync.wait_ge(rowd.h, B["ln_row"] + 1)
        ctx.dma(sync, varrow, rows[0:1, 3], stv)
        for dc in range(DC):
            if dc >= 2:
                sync.wait_ge(p2.h, B["ln_p2"] + 2 * (dc - 1))
            sync.wait_ge(bce.h, B["ln_bce"] + 2)
            chs.dma(sync, DC + dc, ch[:, dc % 2], xres[dc * 128:(dc + 1) * 128, :])
            sync.wait_ge(p2.h, B["ln_p2"] + 2 * (dc + 1))
            sts.dma(sync, dc, xn[:, dc, :], xns[:, dc % 2])
        sts.wait_all(sync, DC)
        ctx.w(sync, stv)
        chs.wait_all(sync, 2 * DC)

    def f_vector(vector):
        vector.memset(ones_bf[:], 1.0)
        vin.inc(vector.memset(ones_f1[:], 1.0))
        for dc in range(DC):
            chs.wait(vector, dc)
            if dc >= 2:
                vector.wait_ge(mmd.h, B["ln_mm"] + dc - 1)
            cst.inc(vector.tensor_copy(out=xbf[:, dc % 2], in_=ch[:, dc % 2]))
        vector.wait_ge(mmd.h, B["ln_mm"] + DC)
        for c in range(NC512):
            pt, po = pstat(c)
            vector.tensor_copy(out=rows[0:1, 0, c * 512:(c + 1) * 512],
                               in_=pt[0:1, po:po + 512])
            inst = vector.tensor_copy(out=rows[0:1, 1, c * 512:(c + 1) * 512],
                                      in_=pt[32:33, po:po + 512])
        evs.inc(inst)
        ctx.w(vector, evs)
        cv.inc(vector.tensor_scalar_mul(out=rows[0:1, 2], in0=rows[0:1, 0],
                                        scalar1=1.0 / D))
        cv.inc(vector.tensor_scalar_mul(out=rows[0:1, 3], in0=rows[0:1, 1],
                                        scalar1=1.0 / D))
        ctx.w(vector, cv)
        cv.inc(vector.tensor_mul(out=rows[0:1, 5], in0=rows[0:1, 2], in1=rows[0:1, 2]))
        ctx.w(vector, cv)
        rowd.inc(vector.tensor_sub(out=rows[0:1, 3], in0=rows[0:1, 3], in1=rows[0:1, 5]))
        ctx.w(vector, rowd)
        rowd.inc(vector.tensor_scalar_add(out=rows[0:1, 6], in0=rows[0:1, 3],
                                          scalar1=1e-5))
        vector.wait_ge(sqd.h, B["ln_sq"] + DC + 1)
        cv.inc(vector.reciprocal(out=rows[0:1, 4], in_=rows[0:1, 4]))
        ctx.w(vector, cv)
        rowd.inc(vector.scalar_tensor_tensor(
            out=rows[0:1, 5], in0=rows[0:1, 2], scalar=-1.0, in1=rows[0:1, 4],
            op0=ALU.mult, op1=ALU.mult))
        vector.wait_ge(bcd.h, B["ln_bc"] + 1)
        for c in range(NC512):
            pt, po = pstat(c)
            inst = vector.tensor_copy(out=Ab[:, c * 512:(c + 1) * 512],
                                      in_=pt[:, po:po + 512])
        bce.inc(inst)
        vector.wait_ge(bcd.h, B["ln_bc"] + 2)
        for c in range(NC512):
            pt, po = pstat(c)
            inst = vector.tensor_copy(out=Bb[:, c * 512:(c + 1) * 512],
                                      in_=pt[:, po:po + 512])
        bce.inc(inst)
        ctx.w(vector, bce)
        for dc in range(DC):
            chs.wait(vector, DC + dc)
            p2.inc(vector.tensor_mul(out=tmp[:, dc % 2], in0=ch[:, dc % 2], in1=Ab[:]))
            if dc >= 2:
                sts.wait(vector, dc - 2)
            ctx.w(vector, p2)
            p2.inc(vector.tensor_add(out=xns[:, dc % 2], in0=tmp[:, dc % 2], in1=Bb[:]))

    def f_scalar(scalar):
        for dc in range(DC):
            chs.wait(scalar, dc)
            if dc >= 2:
                scalar.wait_ge(mmd.h, B["ln_mm"] + dc - 1)
            sqd.inc(scalar.activation(out=xsq[:, dc % 2], in_=ch[:, dc % 2],
                                      func=AF.Square))
        scalar.wait_ge(rowd.h, B["ln_row"] + 2)
        sqd.inc(scalar.activation(out=rows[0:1, 4], in_=rows[0:1, 6],
                                  func=AF.Sqrt))

    def f_tensor(tensor):
        tensor.wait_ge(vin.h, B["ln_vin"] + 1)
        for dc in range(DC):
            tensor.wait_ge(cst.h, B["ln_cst"] + dc + 1)
            tensor.wait_ge(sqd.h, B["ln_sq"] + dc + 1)
            for c in range(NC512):
                pt, po = pstat(c)
                tensor.matmul(pt[0:1, po:po + 512], ones_bf[:],
                              xbf[:, dc % 2, c * 512:(c + 1) * 512],
                              start=(dc == 0), stop=(dc == DC - 1))
            for c in range(NC512):
                pt, po = pstat(c)
                inst = tensor.matmul(pt[32:33, po:po + 512], ones_bf[:],
                                     xsq[:, dc % 2, c * 512:(c + 1) * 512],
                                     start=(dc == 0), stop=(dc == DC - 1))
            mmd.inc(inst)
        tensor.wait_ge(evs.h, B["ln_ev"] + 1)
        tensor.wait_ge(rowd.h, B["ln_row"] + 3)
        for c in range(NC512):
            pt, po = pstat(c)
            inst = tensor.matmul(pt[:, po:po + 512], ones_f1[:],
                                 rows[0:1, 4, c * 512:(c + 1) * 512],
                                 start=True, stop=True)
        bcd.inc(inst)
        tensor.wait_ge(bce.h, B["ln_bce"] + 1)
        for c in range(NC512):
            pt, po = pstat(c)
            inst = tensor.matmul(pt[:, po:po + 512], ones_f1[:],
                                 rows[0:1, 5, c * 512:(c + 1) * 512],
                                 start=True, stop=True)
        bcd.inc(inst)
        ctx.w(tensor, bcd)

    ctx.sec("sync", f_sync)
    ctx.sec("vector", f_vector)
    ctx.sec("scalar", f_scalar)
    ctx.sec("tensor", f_tensor)
    ctx.barrier()


def build_launch(cfg, items, keep_external=()):
    nc = bass.Bass("TRN2", target_bir_lowering=False, debug=False)
    io = {}
    written = set()
    for (_, reads, writes, _f) in items:
        for r in reads:
            if r not in written:
                io.setdefault(r, "in")
        for w in writes:
            written.add(w)
            io[w] = "out"
    if len(items) > 1:
        for w in written:
            if w != "out" and w not in keep_external:
                io[w] = "internal"
    tab = io_table(cfg)
    decl = {n: (io[n], tab[n][0], tab[n][1]) for n in io}
    with ExitStack() as stk:
        ctx = Ctx(nc, cfg, stk, decl)
        for n in sorted(decl):
            ctx.dram(n)
        ctx.begin_block()
        for (pname, _r, _w, fn) in items:
            with nc.named_scope(pname):
                fn(ctx)
        if ctx.phase_stk is not None:
            ctx.phase_stk.close()
            ctx.phase_stk = None
    ins = sorted(n for n, k in io.items() if k == "in")
    outs = sorted(n for n, k in io.items() if k == "out")
    return nc, ins, outs


def run_phases(cfg, state, groups, trace=False):
    core_ids = list(range(cfg.n_cores))
    import os
    dbg = os.environ.get("KPH_DEBUG")
    for gi, items in enumerate(groups):
        later_reads = {r for g2 in groups[gi + 1:] for it in g2 for r in it[1]}
        if dbg:
            print(f"[run_phases] launching {[it[0] for it in items]}", flush=True)
        nc, reads, writes = build_launch(cfg, items, keep_external=later_reads)
        in_maps = []
        for c in range(cfg.n_cores):
            m = {}
            for r in reads:
                v = state[r]
                m[r] = v[c] if isinstance(v, list) else v
            in_maps.append(m)
        trace = bool(os.environ.get("KPH_TRACE"))
        res = run_bass_kernel_spmd(nc, in_maps, core_ids=core_ids, trace=trace)
        if trace:
            print(f"[trace] exec_time_ns={res.exec_time_ns} "
                  f"mean={res.mean_exec_time_ns}", flush=True)
            if res.per_core_scope_times:
                for scope, t in sorted(res.per_core_scope_times.items()):
                    print(f"[trace] {scope}: {t}", flush=True)
        for w in writes:
            state[w] = [np.asarray(res.results[c][w]) for c in range(cfg.n_cores)]


def phase_list(cfg):
    P = []
    P.append(("IN", ["x_in", "ident"], ["xres"], ph_in))
    P.append(("WPREP0", ["wdnTq", "scdnT"], ["wdnbf"], ph_wprep))
    P.append(("LN0", ["xres"], ["xn", "varrow"], lambda c: ph_ln(c, "xres")))
    P.append(("QKV0",
              ["xn", "wq0T", "bq0", "wk0T", "bk0", "wv0T", "bv0", "cos0", "sin0"],
              ["qfm", "kown0", "vown0", "qn2", "kn2own"],
              lambda c: ph_qkv(c, 0)))
    P.append(("ATT0",
              ["qfm", "kown0", "vown0", "kn2own", "qn2", "wo0T", "bo0", "ident"],
              ["sfm"], lambda c: ph_att(c, 0)))
    P.append(("FFNA0", ["xn", "wupTq", "scupT", "bup", "upA", "upB"],
              ["x2b"], ph_ffna))
    P.append(("FFNB0", ["x2b", "wdnbf", "bdn", "dnA", "dnB", "xres", "sfm"],
              ["xres1"], lambda c: ph_ffnb(c, "xres", "xres1")))
    P.append(("LN1", ["xres1"], ["xn", "varrow"], lambda c: ph_ln(c, "xres1")))
    P.append(("QKV1", ["xn", "wq1T", "wk1T", "wv1T", "cos1", "sin1"],
              ["qfm", "kown1", "vown1", "qn2", "kn2own"],
              lambda c: ph_qkv(c, 1)))
    P.append(("GATHER1", ["kown1", "vown1", "kn2own"],
              ["kg1", "vg1", "kn2"], lambda c: ph_gather(c, 1)))
    P.append(("ATT1",
              ["qfm", "kg1", "vg1", "qn2", "kn2", "wo1T", "masks1", "ident"],
              ["sfm"], lambda c: ph_att(c, 1)))
    P.append(("MUPREP", ["mu"], ["mun"], ph_muprep))
    P.append(("MOEA",
              ["xn", "varrow", "gamma", "beta", "mun", "moescale", "ident"],
              ["Ebuf", "xlnT", "rowZ", "Usum"], ph_moea))
    P.append(("MOEB",
              ["Usum", "Ebuf", "rowZ", "W1", "b1", "W2", "b2", "ident",
               "xres1", "sfm"],
              ["ctb", "xres2"], lambda c: ph_moeb(c, "xres1", "xres2")))
    P.append(("OUT", ["xres2", "ident"], ["out"], lambda c: ph_out(c, "xres2")))
    return P


def kernel(**inputs):
    import os
    cfg = Cfg()
    sh, percore = host_prepare(cfg, inputs)
    state = dict(sh)
    for k in percore[0]:
        state[k] = [pc[k] for pc in percore]
    plist = phase_list(cfg)
    if os.environ.get("KPH_UNFUSED"):
        groups = [[p] for p in plist]
    elif os.environ.get("KPH_SPLIT"):
        cuts = [int(x) for x in os.environ["KPH_SPLIT"].split(",")]
        groups = []
        prev = 0
        for c in cuts + [len(plist)]:
            if c > prev:
                groups.append(plist[prev:c])
            prev = c
    else:
        groups = [plist]
    run_phases(cfg, state, groups)
    D = cfg.D
    full = np.zeros((cfg.B, cfg.S, D), np.float32)
    for c in range(cfg.n_cores):
        b, par = c // 2, c % 2
        tiles = [2 * i + par for i in range(cfg.NT)]
        full[b].reshape(cfg.NT_TOT, 128, D)[tiles] = state["out"][c]
    return full


def ph_qkv(ctx, layer):
    cfg = ctx.cfg
    DC, NQC, n = cfg.DC, cfg.NQC, cfg.n_own
    XN = ctx.sbuf("XN", [PAR, DC, n], BF16)
    wblk = ctx.sbuf("qkv_wblk", [PAR, 3, DC, 128], BF16)
    wvblk = ctx.sbuf("qkv_wvblk", [PAR, 2, DC, 512], BF16)
    qstage = ctx.sbuf("qkv_qstage", [PAR, 4, 512], BF16)
    vstage = ctx.sbuf("qkv_vstage", [PAR, 4, 512], BF16)
    sqt = ctx.sbuf("qkv_sqt", [PAR, 4, 512], BF16)
    t_a = ctx.sbuf("qkv_ta", [64, 512], F32)
    t_b = ctx.sbuf("qkv_tb", [64, 512], F32)
    t_a2 = ctx.sbuf("qkv_ta2", [64, 512], F32)
    t_b2 = ctx.sbuf("qkv_tb2", [64, 512], F32)
    cos_sb = ctx.sbuf("qkv_cos", [64, n], F32)
    sin_sb = ctx.sbuf("qkv_sin", [64, n], F32)
    cs1_sb = ctx.sbuf("qkv_cs1", [64, 32], F32)
    brow = ctx.sbuf("qkv_brow", [1, 3, cfg.D], BF16)
    ones_r512 = ctx.sbuf("ones_r512", [1, 512], BF16)
    ones_r128 = ctx.sbuf("ones_r128", [1, PAR], BF16)
    ones_c128 = ctx.sbuf("ones_c128", [PAR, 1], BF16)
    nrow = ctx.sbuf("qkv_nrow", [1, 2, 16], F32)
    ntmp = ctx.sbuf("qkv_ntmp", [1, 2], F32)
    pA = ctx.psum("pA", [PAR, 1024], F32)
    pB = ctx.psum("pB", [PAR, 1024], F32)
    pD = ctx.psum("pD", [PAR, 512], F32)
    pE = ctx.psum("pE", [PAR, 512], F32)

    xn_d, qfm = ctx.dram("xn"), ctx.dram("qfm")
    if layer == 0:
        kown, vown = ctx.dram("kown0"), ctx.dram("vown0")
        wq_d, wk_d, wv_d = ctx.dram("wq0T"), ctx.dram("wk0T"), ctx.dram("wv0T")
    else:
        kown, vown = ctx.dram("kown1"), ctx.dram("vown1")
        wq_d, wk_d, wv_d = ctx.dram("wq1T"), ctx.dram("wk1T"), ctx.dram("wv1T")
    qn2_d, kn2_d = ctx.dram("qn2"), ctx.dram("kn2own")

    ld = ctx.dma_sem("qkv_ld")
    gld = ctx.dma_sem("qkv_gld", sw=True)
    ws = ctx.slots("qkv_w", 3)
    wvs = ctx.slots("qkv_wv", 2)
    sts = ctx.slots("qkv_st", 4)
    stvs = ctx.slots("qkv_stv", 4)
    stn = ctx.dma_sem("qkv_stn")
    vin = ctx.sem("qkv_vin")
    mmq = ctx.sem("qkv_mmq")
    rv = ctx.sem("qkv_rv")
    rv2 = ctx.sem("qkv_rv2")
    rs = ctx.sem("qkv_rs")
    sqd = ctx.sem("qkv_sqd")
    nr = ctx.sem("qkv_nr")
    nf = ctx.sem("qkv_nf")
    wdn = ctx.sem("qkv_wdn")
    mmv = ctx.sem("qkv_mmv")
    ve = ctx.sem("qkv_ve")
    cv = ctx.sem("chain_v")
    sems = (ld, gld, vin, mmq, rv, rv2, rs, sqd, nr, nf, wdn, mmv, ve, stn)
    B = {s.name: s.v for s in sems}
    L0 = layer == 0

    QO = 16
    KO = 16 if L0 else 1
    NI = (QO + KO) * NQC
    VC = 4 if L0 else 1
    VN = cfg.NT
    NJ = VC * VN

    def kind_of(i):
        if i < QO * NQC:
            return ("q", i // NQC, i % NQC)
        j = i - QO * NQC
        return ("k", j // NQC, j % NQC)

    def f_sync(sync):
        ctx.dma(sync, XN[:], xn_d, ld)
        if L0:
            ctx.dma(sync, cos_sb[:], ctx.dram("cos0"), ld)
            ctx.dma(sync, sin_sb[:], ctx.dram("sin0"), ld)
        else:
            ctx.dma(sync, cs1_sb[:, 0:16], ctx.dram("cos1"), ld)
            ctx.dma(sync, cs1_sb[:, 16:32], ctx.dram("sin1"), ld)
        for i in range(NI):
            kind, oc, ncl = kind_of(i)
            sync.wait_ge(rv2.h, B["qkv_rv2"] + i + 1)
            if L0:
                sync.wait_ge(rs.h, B["qkv_rs"] + i + 1)
            dst = qfm if kind == "q" else kown
            if L0 or kind == "q":
                out_ap = dst[oc * 128:(oc + 1) * 128, ncl * 512:(ncl + 1) * 512]
            else:
                out_ap = dst[:, ncl * 512:(ncl + 1) * 512]
            sts.dma(sync, i, out_ap, qstage[:, i % 4, :])
        for j in range(NJ):
            vc, nt = j // VN, j % VN
            sync.wait_ge(ve.h, B["qkv_ve"] + j + 1)
            if L0:
                out_ap = vown[4 * vc:4 * vc + 4, :, nt, :].rearrange(
                    "h p m -> p h m")
                stvs.dma(sync, j, out_ap,
                         vstage[:, j % 4].rearrange("p (h m) -> p h m", m=128))
            else:
                stvs.dma(sync, j, vown[:, nt, :], vstage[:, j % 4, 0:128])
        sync.wait_ge(nf.h, B["qkv_nf"] + NI)
        ctx.dma(sync, qn2_d, nrow[0:1, 0], stn)
        ctx.dma(sync, kn2_d, nrow[0:1, 1], stn)
        sts.wait_all(sync, NI)
        stvs.wait_all(sync, NJ)
        ctx.w(sync, stn)
        ctx.w(sync, ld)

    def load_w(eng, wi):
        kind = "q" if wi < QO else "k"
        oc = wi if wi < QO else wi - QO
        wd = wq_d if kind == "q" else wk_d
        src = wd[oc] if (L0 or kind == "q") else wd[0]
        ws.dma(eng, wi, wblk[:, wi % 3], src)

    def f_gpsimd(gpsimd):
        if L0:
            ctx.dma(gpsimd, brow[0:1, 0], ctx.dram("bq0"), gld)
            ctx.dma(gpsimd, brow[0:1, 1], ctx.dram("bk0"), gld)
            ctx.dma(gpsimd, brow[0:1, 2], ctx.dram("bv0"), gld)
        ctx.w(gpsimd, gld)

    def f_tensor(tensor):
        tensor.wait_ge(ld.h, B["qkv_ld"] + 48)
        tensor.wait_ge(vin.h, B["qkv_vin"] + 1)
        if L0:
            tensor.wait_ge(gld.h, B["qkv_gld"] + 48)

        def norm_mm(k):
            tensor.wait_ge(sqd.h, B["qkv_sqd"] + k + 1)
            if k >= 2:
                tensor.wait_ge(nf.h, B["qkv_nf"] + k - 1)
            npt = pD if k % 2 == 0 else pE
            nr.inc(tensor.matmul(npt[0:1, 0:512], ones_c128[:],
                                 sqt[:, k % 4, :], start=True, stop=True))

        for i in range(NI):
            kind, oc, ncl = kind_of(i)
            wi = oc if kind == "q" else QO + oc
            ws.wait(tensor, wi)
            if i >= 4:
                if L0:
                    tensor.wait_ge(rs.h, B["qkv_rs"] + i - 3)
                else:
                    tensor.wait_ge(rv.h, B["qkv_rv"] + i - 3)
            pq = (pA, pB)[(i % 4) // 2]
            off = (i % 2) * 512
            for kc in range(DC):
                inst = tensor.matmul(pq[:, off:off + 512], wblk[:, wi % 3, kc, :],
                                     XN[:, kc, ncl * 512:(ncl + 1) * 512],
                                     start=(kc == 0), stop=False)
            if ncl == NQC - 1:
                wdn.inc(inst)
            bi = 0 if kind == "q" else 1
            mmq.inc(tensor.matmul(pq[:, off:off + 512],
                                  brow[0:1, bi, oc * 128:(oc + 1) * 128],
                                  ones_r512[:], start=False, stop=True))
            if i >= 2:
                norm_mm(i - 2)
        norm_mm(NI - 2)
        norm_mm(NI - 1)
        tensor.wait_ge(rv2.h, B["qkv_rv2"] + NI)
        if L0:
            tensor.wait_ge(rs.h, B["qkv_rs"] + NI)
        tensor.wait_ge(sqd.h, B["qkv_sqd"] + NI)
        for j in range(NJ):
            vc, nt = j // VN, j % VN
            wvs.wait(tensor, vc)
            if j >= 4:
                tensor.wait_ge(ve.h, B["qkv_ve"] + j - 3)
            w = 512 if L0 else 128
            pq = (pA, pB)[(j % 4) // 2]
            off = (j % 2) * 512
            for kc in range(DC):
                inst = tensor.matmul(pq[:, off:off + w],
                                     XN[:, kc, nt * 128:(nt + 1) * 128],
                                     wvblk[:, vc % 2, kc, 0:w],
                                     start=(kc == 0), stop=False)
            if nt == VN - 1:
                wdn.inc(inst)
            mmv.inc(tensor.matmul(pq[:, off:off + w], ones_r128[:],
                                  brow[0:1, 2, vc * 512:vc * 512 + w],
                                  start=False, stop=True))
        ctx.w(tensor, mmv)

    def f_vector(vector):
        vector.memset(ones_r512[:], 1.0)
        vector.memset(ones_r128[:], 1.0)
        vector.memset(ones_c128[:], 1.0)
        if not L0:
            vector.memset(brow[:], 0.0)
        vin.inc(vector.memset(nrow[:], 0.0))
        vector.wait_ge(ld.h, B["qkv_ld"] + 48)
        for i in range(NI):
            kind, oc, ncl = kind_of(i)
            vector.wait_ge(mmq.h, B["qkv_mmq"] + i + 1)
            pq = (pA, pB)[(i % 4) // 2]
            off = (i % 2) * 512
            if i >= 4:
                sts.wait(vector, i - 4)
                vector.wait_ge(sqd.h, B["qkv_sqd"] + i - 3)
            if L0:
                cv.inc(vector.tensor_tensor(out=t_a[:], in0=pq[0:64, off:off + 512],
                                            in1=cos_sb[:, ncl * 512:(ncl + 1) * 512],
                                            op=ALU.mult))
                rv.inc(vector.tensor_tensor(out=t_b[:], in0=pq[64:128, off:off + 512],
                                            in1=sin_sb[:, ncl * 512:(ncl + 1) * 512],
                                            op=ALU.mult))
                ctx.w(vector, cv)
                ctx.w(vector, rv)
                rv2.inc(vector.tensor_sub(out=qstage[0:64, i % 4, :],
                                          in0=t_a[:], in1=t_b[:]))
            elif kind == "q":
                c_col = cs1_sb[0:64, oc:oc + 1]
                s_col = cs1_sb[0:64, 16 + oc:16 + oc + 1]
                cv.inc(vector.tensor_scalar_mul(out=t_a[:], in0=pq[0:64, off:off + 512],
                                                scalar1=c_col))
                cv.inc(vector.tensor_scalar_mul(out=t_b[:], in0=pq[64:128, off:off + 512],
                                                scalar1=s_col))
                ctx.w(vector, cv)
                vector.tensor_sub(out=qstage[0:64, i % 4, :], in0=t_a[:], in1=t_b[:])
                cv.inc(vector.tensor_scalar_mul(out=t_a2[:], in0=pq[0:64, off:off + 512],
                                                scalar1=s_col))
                rv.inc(vector.tensor_scalar_mul(out=t_b2[:], in0=pq[64:128, off:off + 512],
                                                scalar1=c_col))
                ctx.w(vector, cv)
                ctx.w(vector, rv)
                rv2.inc(vector.tensor_add(out=qstage[64:128, i % 4, :],
                                          in0=t_a2[:], in1=t_b2[:]))
            else:
                inst = vector.tensor_copy(out=qstage[:, i % 4, :],
                                          in_=pq[:, off:off + 512])
                rv.inc(inst)
                ctx.w(vector, rv)
                rv2.inc(vector.tensor_copy(out=qstage[0:1, i % 4, 0:1],
                                           in_=qstage[0:1, i % 4, 0:1]))
            vector.wait_ge(nr.h, B["qkv_nr"] + i + 1)
            npt = pD if i % 2 == 0 else pE
            col = 0 if kind == "q" else 1
            cv.inc(vector.reduce_max(out=ntmp[0:1, col:col + 1], in_=npt[0:1, 0:512],
                                     axis=mybir.AxisListType.X))
            ctx.w(vector, cv)
            if i == 0:
                ctx.w(vector, vin)
            else:
                vector.wait_ge(nf.h, B["qkv_nf"] + i)
            nf.inc(vector.tensor_max(out=nrow[0:1, col, oc:oc + 1],
                                     in0=nrow[0:1, col, oc:oc + 1],
                                     in1=ntmp[0:1, col:col + 1]))
        for j in range(NJ):
            vector.wait_ge(mmv.h, B["qkv_mmv"] + j + 1)
            if j >= 4:
                stvs.wait(vector, j - 4)
            w = 512 if L0 else 128
            pq = (pA, pB)[(j % 4) // 2]
            off = (j % 2) * 512
            ve.inc(vector.tensor_copy(out=vstage[:, j % 4, 0:w],
                                      in_=pq[:, off:off + w]))

    def f_scalar(scalar):
        for wi in range(min(3, QO + KO)):
            load_w(scalar, wi)
        for i in range(NI):
            kind, oc, ncl = kind_of(i)
            pq = (pA, pB)[(i % 4) // 2]
            off = (i % 2) * 512
            if L0:
                scalar.wait_ge(rv.h, B["qkv_rv"] + i + 1)
                if i >= 4:
                    sts.wait(scalar, i - 4)
                rs.inc(scalar.copy(out=qstage[64:128, i % 4, :],
                                   in_=pq[64:128, off:off + 512]))
            scalar.wait_ge(rv2.h, B["qkv_rv2"] + i + 1)
            if L0:
                scalar.wait_ge(rs.h, B["qkv_rs"] + i + 1)
            sqd.inc(scalar.activation(out=sqt[:, i % 4, :], in_=qstage[:, i % 4, :],
                                      func=AF.Square))
            if i % NQC == NQC - 1:
                wi = i // NQC + 3
                if wi < QO + KO:
                    scalar.wait_ge(wdn.h, B["qkv_wdn"] + i // NQC + 1)
                    load_w(scalar, wi)
        for vc in range(VC):
            if vc >= 2:
                scalar.wait_ge(wdn.h, B["qkv_wdn"] + QO + KO + vc - 1)
            w = 512 if L0 else 128
            src = wv_d[vc] if L0 else wv_d[0]
            wvs.dma(scalar, vc, wvblk[:, vc % 2, :, 0:w], src[:, :, 0:w])
        ws.wait_all(scalar, QO + KO)
        wvs.wait_all(scalar, VC)

    ctx.sec("sync", f_sync)
    ctx.sec("gpsimd", f_gpsimd)
    ctx.sec("tensor", f_tensor)
    ctx.sec("vector", f_vector)
    ctx.sec("scalar", f_scalar)
    ctx.barrier()


def ph_gather(ctx, layer):
    cfg = ctx.cfg
    n = cfg.n_own
    KD = cfg.D if layer == 0 else 128
    kown = ctx.dram("kown0" if layer == 0 else "kown1")
    vown = ctx.dram("vown0" if layer == 0 else "vown1")
    kg = ctx.dram("kg0" if layer == 0 else "kg1")
    vg = ctx.dram("vg0" if layer == 0 else "vg1")
    kn2own, kn2 = ctx.dram("kn2own"), ctx.dram("kn2")
    nc = ctx.nc
    sfx = f"_g{layer}"
    kb = nc.dram_tensor("kb" + sfx, [n * KD], BF16, kind="Internal").ap()
    kgb = nc.dram_tensor("kgb" + sfx, [2 * n * KD], BF16, kind="Internal").ap()
    vb = nc.dram_tensor("vb" + sfx, [n * KD], BF16, kind="Internal").ap()
    vgb = nc.dram_tensor("vgb" + sfx, [2 * n * KD], BF16, kind="Internal").ap()
    nb = nc.dram_tensor("nb" + sfx, [1, 16], F32, kind="Internal").ap()
    nrb = nc.dram_tensor("nrb" + sfx, [1, 16], F32, kind="Internal").ap()
    groups = [[2 * i, 2 * i + 1] for i in range(cfg.n_cores // 2)]
    dmas = ctx.dma_sem("gth_dma", sw=True)
    cc1, cc2, cc3 = (ctx.sem("gth_c1", keep=True), ctx.sem("gth_c2", keep=True),
                     ctx.sem("gth_c3", keep=True))
    Bv = {x.name: x.v for x in (dmas, cc1, cc2, cc3)}

    def f_gpsimd(g):
        ctx.dma(g, kb, kown.rearrange("a b -> (a b)"), dmas)
        ctx.dma(g, vb, vown.rearrange("a b c -> (a b c)") if layer == 1
                else vown.rearrange("a b c d -> (a b c d)"), dmas)
        ctx.dma(g, nb, kn2own, dmas)
        g.wait_ge(dmas.h, Bv["gth_dma"] + 48)
        cc1.inc(g.collective_compute(
            "AllGather", ALU.bypass, replica_groups=groups,
            ins=[kb.opt()], outs=[kgb.opt()]))
        cc2.inc(g.collective_compute(
            "AllGather", ALU.bypass, replica_groups=groups,
            ins=[vb.opt()], outs=[vgb.opt()]))
        cc3.inc(g.collective_compute(
            "AllReduce", ALU.max, replica_groups=groups,
            ins=[nb.opt()], outs=[nrb.opt()]))
        g.wait_ge(cc1.h, Bv["gth_c1"] + 1)
        ctx.dma(g, kg.rearrange("a b c -> (a b c)"), kgb, dmas)
        g.wait_ge(cc2.h, Bv["gth_c2"] + 1)
        ctx.dma(g, (vg.rearrange("a b c d -> (a b c d)") if layer == 1
                    else vg.rearrange("a b c d e -> (a b c d e)")), vgb, dmas)
        g.wait_ge(cc3.h, Bv["gth_c3"] + 1)
        ctx.dma(g, kn2, nrb, dmas)
        ctx.w(g, dmas)

    ctx.sec("gpsimd", f_gpsimd)
    ctx.barrier()


def ph_att(ctx, layer):
    cfg = ctx.cfg
    L0 = layer == 0
    NQC, NT, n, DC = cfg.NQC, cfg.NT, cfg.n_own, cfg.DC
    KT = 2 * NT
    H = 16
    ISQ = 1.0 / np.sqrt(128.0)

    qfm = ctx.dram("qfm")
    kg = None if L0 else ctx.dram("kg1")
    vg = None if L0 else ctx.dram("vg1")
    qn2_d, kn2_d = ctx.dram("qn2"), ctx.dram("kn2")
    wo_d = ctx.dram("wo0T" if L0 else "wo1T")
    sfm = ctx.dram("sfm")
    ident = ctx.dram("ident")

    k_sb = ctx.sbuf("att_k", [PAR, 2, 2, NT, 128], BF16)
    v_sb = ctx.sbuf("att_v", [PAR, 2, 2, NT, 136], BF16)
    q_sb = ctx.sbuf("att_q", [PAR, 2, n], BF16)
    attn_sb = ctx.sbuf("att_p", [PAR, 2, 512], BF16)
    ctile = ctx.sbuf("att_ct", [PAR, 2, 512], BF16)
    CTX = ctx.sbuf("att_ctx", [PAR, 16, n], BF16)
    negm_b = ctx.sbuf("att_negm", [PAR, 16], F32)
    nrows = ctx.sbuf("att_nrows", [1, 4, 16], F32)
    mask_sb = ctx.sbuf("att_mask", [PAR, 8, 512], BF16)
    wblk = ctx.sbuf("att_wblk", [PAR, 2, DC, 128], BF16)
    sstage = ctx.sbuf("att_sstage", [PAR, 2, 512], F32)
    bo_row = ctx.sbuf("att_bo", [1, cfg.D], BF16)
    ones_r512 = ctx.sbuf("ones_r512", [1, 512], BF16)
    ones_f1 = ctx.sbuf("ones_f32_row", [1, PAR], F32)
    idf = ctx.sbuf("idf_bf", [PAR, 128], BF16)
    pA = ctx.psum("pA", [PAR, 1024], F32)
    pB = ctx.psum("pB", [PAR, 1024], F32)
    pC = ctx.psum("pC", [PAR, 1024], BF16)
    pC2 = ctx.psum("pC2", [PAR, 1024], BF16)
    pD = ctx.psum("pD", [PAR, 512], F32)
    pE = ctx.psum("pE", [PAR, 512], F32)

    def ctx_ps(j):
        return [(pB, 0), (pB, 512), (pD, 0), (pE, 0)][j]

    ldn = ctx.dma_sem("att_ldn")
    ldkv = ctx.dma_sem("att_ldkv")
    hs = ctx.slots("att_h", 2, per=(5 if L0 else 1))
    wos = ctx.slots("att_wo", 2)
    sts = ctx.slots("att_sst", 2)
    gld = ctx.dma_sem("att_gld", sw=True)
    nm = ctx.sem("att_nm")
    sc_r = ctx.sem("att_scr")
    expd = ctx.sem("att_expd")
    mskd = ctx.sem("att_mskd")
    pvd = ctx.sem("att_pvd")
    ce = ctx.sem("att_ce")
    trd = ctx.sem("att_trd")
    cte = ctx.sem("att_cte")
    omm = ctx.sem("att_omm")
    oev = ctx.sem("att_oev")
    wodn = ctx.sem("att_wodn")
    cv = ctx.sem("chain_v")
    sems = (ldn, ldkv, gld, nm, sc_r, expd, mskd, pvd, ce, trd, cte,
            omm, oev, wodn)
    B = {x.name: x.v for x in sems}
    if L0:
        gkv = ctx.sem("att_gkv", keep=True)
        B["att_gkv"] = gkv.v
        kown_d, vown_d = ctx.dram("kown0"), ctx.dram("vown0")
        kn2own_d = ctx.dram("kn2own")
        nrb = ctx.nc.dram_tensor("att_nrb", [1, 16], F32, kind="Internal").ap()
        kghb = ctx.nc.dram_tensor("att_kghb", [H // 2, 2, 256, n], BF16,
                                  kind="Internal").ap()
        vghb = ctx.nc.dram_tensor("att_vghb", [H // 2, 2, 2, PAR, NT, 128], BF16,
                                  kind="Internal").ap()
        cgroups = [[2 * i, 2 * i + 1] for i in range(cfg.n_cores // 2)]

    sched = []
    s_count = 0
    hq_of = {}
    for h in range(H):
        for qc in range(NQC):
            first_s = s_count
            extent = KT if L0 else min(KT, 8 * qc + 8)
            for t in range(extent):
                if L0 or t < 8 * qc:
                    bt, qs = None, 0
                else:
                    bt = t - 8 * qc
                    qs = 128 * (bt // 2)
                sched.append((h, qc, t, bt, qs, s_count))
                s_count += 1
            hq_of[(h, qc)] = (first_s, extent)
    S_TOT = s_count
    HQ = H * NQC
    n_band = sum(1 for it in sched if it[3] is not None)

    def stop_t(j, qc, extent):
        return extent - 1 if L0 else min(extent - 1, 8 * qc + 2 * j + 1)

    def end_s(hq):
        h, qc = hq // NQC, hq % NQC
        fs, ext = hq_of[(h, qc)]
        return fs + ext

    def head_end_s(h):
        return end_s(h * NQC + NQC - 1)

    def f_sync(sync):
        ctx.dma(sync, nrows[0:1, 0], qn2_d, ldn)
        if L0:
            sync.wait_ge(gkv.h, B["att_gkv"] + 1)
            ctx.dma(sync, nrows[0:1, 1], nrb, ldn)
        else:
            ctx.dma(sync, nrows[0:1, 1], kn2_d, ldn)
        if not L0:
            for par in range(2):
                ctx.dma(sync, k_sb[:, 0, par],
                        kg[par].rearrange("p (lt m) -> p lt m", m=128), ldkv)
                ctx.dma(sync, v_sb[:, 0, par, :, 0:128], vg[par], ldkv)
        for h in range(H):
            if h >= 2:
                sync.wait_ge(pvd.h, B["att_pvd"] + head_end_s(h - 2))
            hs.dma(sync, h, q_sb[:, h % 2], qfm[h * 128:(h + 1) * 128, :])
            if L0:
                sync.wait_ge(gkv.h, B["att_gkv"] + 2 * (h // 2) + 3)
                for par in range(2):
                    hs.dma(sync, h, k_sb[:, h % 2, par],
                           kghb[h // 2, par, (h % 2) * 128:(h % 2) * 128 + 128]
                           .rearrange("p (lt m) -> p lt m", m=128))
                    hs.dma(sync, h, v_sb[:, h % 2, par, :, 0:128],
                           vghb[h // 2, par, h % 2])
        for i in range(16 * NQC):
            sync.wait_ge(oev.h, B["att_oev"] + i + 1)
            oc, ncl = i // NQC, i % NQC
            sts.dma(sync, i, sfm[oc * 128:(oc + 1) * 128,
                                 ncl * 512:(ncl + 1) * 512], sstage[:, i % 2])
        sts.wait_all(sync, 16 * NQC)
        hs.wait_all(sync, H)
        ctx.w(sync, ldn)
        if not L0:
            ctx.w(sync, ldkv)

    def f_gpsimd(g):
        if L0:
            ctx.dma(g, bo_row[0:1, :], ctx.dram("bo0"), gld)
        else:
            ctx.dma(g, mask_sb[:],
                    ctx.dram("masks1").rearrange("t p q -> p t q"), gld)
        ctx.dma(g, idf[:], ident, gld)
        if L0:
            gkv.inc(g.collective_compute(
                "AllReduce", ALU.max, replica_groups=cgroups,
                ins=[kn2own_d.opt()], outs=[nrb.opt()]))
            for j in range(H // 2):
                gkv.inc(g.collective_compute(
                    "AllGather", ALU.bypass, replica_groups=cgroups,
                    ins=[kown_d[j * 256:(j + 1) * 256, :]
                         .rearrange("a b -> (a b)").opt()],
                    outs=[kghb[j].rearrange("a b c -> (a b c)").opt()]))
                gkv.inc(g.collective_compute(
                    "AllGather", ALU.bypass, replica_groups=cgroups,
                    ins=[vown_d[2 * j:2 * j + 2]
                         .rearrange("a b c d -> (a b c d)").opt()],
                    outs=[vghb[j].rearrange("a b c d e -> (a b c d e)").opt()]))
        ctx.w(g, gld)

    def f_scalar(scalar):
        scalar.wait_ge(ldn.h, B["att_ldn"] + 32)
        scalar.activation(out=nrows[0:1, 2], in_=nrows[0:1, 0], func=AF.Sqrt)
        nm.inc(scalar.activation(out=nrows[0:1, 3], in_=nrows[0:1, 1], func=AF.Sqrt))
        for (h, qc, t, bt, qs, s) in sched:
            scalar.wait_ge(nm.h, B["att_nm"] + 4)
            scalar.wait_ge(sc_r.h, B["att_scr"] + s + 1)
            if s >= 2:
                scalar.wait_ge(pvd.h, B["att_pvd"] + s - 1)
            off = (s % 2) * 512
            expd.inc(scalar.activation(
                out=attn_sb[:, s % 2, qs:512], in_=pA[:, off + qs:off + 512],
                func=AF.Exp, bias=negm_b[:, h:h + 1], scale=ISQ))
        wos.dma(scalar, 0, wblk[:, 0], wo_d[0])
        wos.dma(scalar, 1, wblk[:, 1], wo_d[1])
        for i in range(16 * NQC):
            if i % NQC == 0:
                oc2 = i // NQC + 2
                if oc2 < 16:
                    scalar.wait_ge(wodn.h, B["att_wodn"] + oc2 - 1)
                    wos.dma(scalar, oc2, wblk[:, oc2 % 2], wo_d[oc2])
            scalar.wait_ge(omm.h, B["att_omm"] + i + 1)
            if i >= 2:
                sts.wait(scalar, i - 2)
            r = i % 4
            pt = (pA, pB)[r // 2]
            off = (r % 2) * 512
            oev.inc(scalar.copy(out=sstage[:, i % 2], in_=pt[:, off:off + 512]))
        wos.wait_all(scalar, 16)

    band_prefix = []
    nb = 0
    for it in sched:
        if it[3] is not None:
            nb += 1
        band_prefix.append(nb)

    def f_vector(vector):
        vector.memset(ones_r512[:], 1.0)
        vector.memset(ones_f1[:], 1.0)
        vector.memset(v_sb[:, :, :, :, 128:129], 1.0)
        if not L0:
            vector.memset(bo_row[:], 0.0)
        rz_sb = ctx.sbuf("att_rz", [PAR, 2], F32)
        vector.wait_ge(nm.h, B["att_nm"] + 1)
        if L0:
            cv.inc(vector.tensor_mul(out=nrows[0:1, 2], in0=nrows[0:1, 2],
                                     in1=nrows[0:1, 3]))
        else:
            cv.inc(vector.tensor_scalar_mul(out=nrows[0:1, 2], in0=nrows[0:1, 2],
                                            scalar1=nrows[0:1, 3, 0:1]))
        ctx.w(vector, cv)
        nm.inc(vector.tensor_scalar_mul(out=nrows[0:1, 2], in0=nrows[0:1, 2],
                                        scalar1=-ISQ))
        vector.wait_ge(nm.h, B["att_nm"] + 3)
        nm.inc(vector.tensor_copy(out=negm_b[:], in_=pD[:, 0:16]))
        for h in range(H):
            for qc in range(NQC):
                hq = h * NQC + qc
                first_s, extent = hq_of[(h, qc)]
                if not L0:
                    for t in range(extent):
                        it = sched[first_s + t]
                        _, _, _, bt, qs, sidx = it
                        if bt is None:
                            continue
                        vector.wait_ge(expd.h, B["att_expd"] + sidx + 1)
                        mskd.inc(vector.tensor_mul(
                            out=attn_sb[:, sidx % 2, qs:512],
                            in0=attn_sb[:, sidx % 2, qs:512],
                            in1=mask_sb[:, bt, qs:512]))
                vector.wait_ge(pvd.h, B["att_pvd"] + end_s(hq))
                if hq >= 2:
                    vector.wait_ge(trd.h, B["att_trd"] + hq - 1)
                for j in range(4):
                    pt_j, joff = ctx_ps(j)
                    cv.inc(vector.reciprocal(out=rz_sb[:, j % 2:j % 2 + 1],
                                             in_=pt_j[:, joff + 128:joff + 129]))
                    ctx.w(vector, cv)
                    inst = vector.tensor_scalar_mul(
                        out=ctile[:, hq % 2, j * 128:(j + 1) * 128],
                        in0=pt_j[:, joff:joff + 128], scalar1=rz_sb[:, j % 2:j % 2 + 1])
                ce.inc(inst)
                if hq >= 1:
                    ph_, pqc = (hq - 1) // NQC, (hq - 1) % NQC
                    vector.wait_ge(trd.h, B["att_trd"] + hq)
                    pt = pC if (hq - 1) % 2 == 0 else pC2
                    cte.inc(vector.tensor_copy(
                        out=CTX[:, ph_, pqc * 512:(pqc + 1) * 512], in_=pt[:, 0:512]))
        vector.wait_ge(trd.h, B["att_trd"] + HQ)
        ph_, pqc = (HQ - 1) // NQC, (HQ - 1) % NQC
        pt = pC if (HQ - 1) % 2 == 0 else pC2
        cte.inc(vector.tensor_copy(
            out=CTX[:, ph_, pqc * 512:(pqc + 1) * 512], in_=pt[:, 0:512]))

    def f_tensor(tensor):
        if not L0:
            tensor.wait_ge(ldkv.h, B["att_ldkv"] + 64)
        tensor.wait_ge(gld.h, B["att_gld"] + 32)
        tensor.wait_ge(nm.h, B["att_nm"] + 2)
        nm.inc(tensor.matmul(pD[:, 0:16], ones_f1[:], nrows[0:1, 2],
                             start=True, stop=True))

        def emit_transpose(hq):
            tensor.wait_ge(ce.h, B["att_ce"] + hq + 1)
            if hq >= 2:
                tensor.wait_ge(cte.h, B["att_cte"] + hq - 1)
            pt = pC if hq % 2 == 0 else pC2
            for j in range(4):
                inst = tensor.transpose(pt[:, j * 128:(j + 1) * 128],
                                        ctile[:, hq % 2, j * 128:(j + 1) * 128],
                                        idf[:])
            trd.inc(inst)

        for h in range(H):
            for qc in range(NQC):
                hq = h * NQC + qc
                first_s, extent = hq_of[(h, qc)]
                if hq >= 1:
                    emit_transpose(hq - 1)
                if qc == 0:
                    hs.wait(tensor, h)
                kslot = (h % 2) if L0 else 0

                def emit_score(t):
                    _, _, _, bt, qs, s = sched[first_s + t]
                    par, loc = t % 2, t // 2
                    if s >= 2:
                        tensor.wait_ge(expd.h, B["att_expd"] + s - 1)
                    off = (s % 2) * 512
                    sc_r.inc(tensor.matmul(
                        pA[:, off + qs:off + 512], k_sb[:, kslot, par, loc, :],
                        q_sb[:, h % 2, qc * 512 + qs:(qc + 1) * 512],
                        start=True, stop=True))

                emit_score(0)
                for t in range(extent):
                    _, _, _, bt, qs, s = sched[first_s + t]
                    par, loc = t % 2, t // 2
                    if t + 1 < extent:
                        emit_score(t + 1)
                    if bt is None:
                        tensor.wait_ge(expd.h, B["att_expd"] + s + 1)
                    else:
                        tensor.wait_ge(mskd.h, B["att_mskd"] + band_prefix[s])
                    if t == 0 and hq >= 1:
                        tensor.wait_ge(ce.h, B["att_ce"] + hq)
                    j0 = 0 if bt is None else bt // 2
                    for j in range(j0, 4):
                        pt_j, joff = ctx_ps(j)
                        inst = tensor.matmul(
                            pt_j[:, joff:joff + 129],
                            attn_sb[:, s % 2, j * 128:(j + 1) * 128],
                            v_sb[:, kslot, par, loc, 0:129],
                            start=(t == 0), stop=(t == stop_t(j, qc, extent)))
                    pvd.inc(inst)
        emit_transpose(HQ - 1)
        tensor.wait_ge(cte.h, B["att_cte"] + HQ)
        tensor.wait_ge(gld.h, B["att_gld"] + 32)
        for i in range(16 * NQC):
            oc, ncl = i // NQC, i % NQC
            wos.wait(tensor, oc)
            if i >= 4:
                tensor.wait_ge(oev.h, B["att_oev"] + i - 3)
            r = i % 4
            pt = (pA, pB)[r // 2]
            off = (r % 2) * 512
            for kc in range(DC):
                inst = tensor.matmul(pt[:, off:off + 512], wblk[:, oc % 2, kc, :],
                                     CTX[:, kc, ncl * 512:(ncl + 1) * 512],
                                     start=(kc == 0), stop=False)
            if ncl == NQC - 1:
                wodn.inc(inst)
            omm.inc(tensor.matmul(pt[:, off:off + 512],
                                  bo_row[0:1, oc * 128:(oc + 1) * 128],
                                  ones_r512[:], start=False, stop=True))
        ctx.w(tensor, omm)

    ctx.sec("sync", f_sync)
    ctx.sec("gpsimd", f_gpsimd)
    ctx.sec("scalar", f_scalar)
    ctx.sec("vector", f_vector)
    ctx.sec("tensor", f_tensor)
    ctx.barrier()


def ph_wprep(ctx):
    cfg = ctx.cfg
    FC, D = cfg.FC, cfg.D
    wq_d, sc_d, out_d = ctx.dram("wdnTq"), ctx.dram("scdnT"), ctx.dram("wdnbf")
    wi = ctx.sbuf("wp_wi", [PAR, 2, D], I32)
    wsc = ctx.sbuf("wp_sc", [PAR, 2, D], BF16)
    wtmp = ctx.sbuf("wp_tmp", [PAR, 2, D], F32)
    wbf = ctx.sbuf("wp_bf", [PAR, 2, D], BF16)
    wis = ctx.slots("wp_wi", 2)
    wss = ctx.slots("wp_ws", 2)
    sts = ctx.slots("wp_st", 2, sw=True)
    cv = ctx.sem("chain_v")
    dqd = ctx.sem("wp_dqd")
    B = {"cv": cv.v, "dqd": dqd.v}

    def f_sync(sync):
        for k in range(FC):
            if k >= 2:
                sync.wait_ge(dqd.h, B["dqd"] + 2 * (k - 1))
            wis.dma(sync, k, wi[:, k % 2],
                    wq_d[k // 4].rearrange("p a b -> p (a b)")
                    [:, (k % 4) * 2048:(k % 4 + 1) * 2048])
        wis.wait_all(sync, FC)
        wss.wait_all(sync, FC)

    def f_scalar(scalar):
        for k in range(FC):
            if k >= 2:
                scalar.wait_ge(dqd.h, B["dqd"] + 2 * (k - 1))
            wss.dma(scalar, k, wsc[:, k % 2],
                    sc_d[k // 4].rearrange("p a b -> p (a b)")
                    [:, (k % 4) * 2048:(k % 4 + 1) * 2048])

    def f_vector(vector):
        for k in range(FC):
            wis.wait(vector, k)
            dqd.inc(vector.tensor_scalar_add(out=wtmp[:, k % 2], in0=wi[:, k % 2],
                                             scalar1=-7.5))
            wss.wait(vector, k)
            if k >= 2:
                sts.wait(vector, k - 2)
            ctx.w(vector, dqd)
            dqd.inc(vector.tensor_mul(out=wbf[:, k % 2], in0=wtmp[:, k % 2],
                                      in1=wsc[:, k % 2]))

    def f_gpsimd(g):
        for k in range(FC):
            g.wait_ge(dqd.h, B["dqd"] + 2 * (k + 1))
            sts.dma(g, k, out_d[k // 4][:, (k % 4) * 2048:(k % 4 + 1) * 2048],
                    wbf[:, k % 2])
        sts.wait_all(g, FC)

    ctx.sec("sync", f_sync)
    ctx.sec("scalar", f_scalar)
    ctx.sec("vector", f_vector)
    ctx.sec("gpsimd", f_gpsimd)
    ctx.barrier()


def ph_ffna(ctx):
    cfg = ctx.cfg
    DC, NQC, n, FC = cfg.DC, cfg.NQC, cfg.n_own, cfg.FC
    XN = ctx.sbuf("XN", [PAR, DC, n], BF16)
    wi = ctx.sbuf("fa_wi", [PAR, 2, DC, 128], I32)
    wsc = ctx.sbuf("fa_sc", [PAR, 2, DC, 128], BF16)
    wtmp = ctx.sbuf("fa_tmp", [PAR, 2, DC, 128], F32)
    wbf = ctx.sbuf("fa_bf", [PAR, 2, DC, 128], BF16)
    ua = ctx.sbuf("fa_ua", [PAR, DC, 16], BF16)
    ub = ctx.sbuf("fa_ub", [16, cfg.FF], BF16)
    bup_r = ctx.sbuf("fa_bup", [1, cfg.FF], BF16)
    tsb = ctx.sbuf("fa_t", [16, NQC, 512], BF16)
    x2st = ctx.sbuf("fa_x2", [PAR, 2, n], BF16)
    ones_r512 = ctx.sbuf("ones_r512", [1, 512], BF16)
    pA = ctx.psum("pA", [PAR, 1024], F32)
    pB = ctx.psum("pB", [PAR, 1024], F32)
    pD = ctx.psum("pD", [PAR, 512], F32)
    pE = ctx.psum("pE", [PAR, 512], F32)
    xn_d = ctx.dram("xn")
    wq_d, sc_d = ctx.dram("wupTq"), ctx.dram("scupT")
    x2_d = ctx.dram("x2b")

    ld = ctx.dma_sem("fa_ld")
    gld = ctx.dma_sem("fa_gld", sw=True)
    wis = ctx.slots("fa_wi", 2)
    wss = ctx.slots("fa_ws", 2)
    sts = ctx.slots("fa_st", 2)
    cg = ctx.sem("chain_g")
    dqd = ctx.sem("fa_dqd")
    wdn = ctx.sem("fa_wdn")
    tmm = ctx.sem("fa_tmm")
    tv = ctx.sem("fa_tv")
    mmf = ctx.sem("fa_mmf")
    ev = ctx.sem("fa_ev")
    vin = ctx.sem("fa_vin")
    B = {x.name if hasattr(x, 'name') else i: x.v for i, x in
         enumerate((ld, gld, cg, dqd, wdn, tmm, tv, mmf, ev, vin))}

    def f_sync(sync):
        ctx.dma(sync, XN[:], xn_d, ld)
        for hc in range(FC):
            if hc >= 2:
                sync.wait_ge(dqd.h, B["fa_dqd"] + 2 * (hc - 1))
            wis.dma(sync, hc, wi[:, hc % 2], wq_d[hc])
            if hc >= 2:
                sync.wait_ge(ev.h, B["fa_ev"] + NQC * (hc - 1))
                sts.dma(sync, hc - 2, x2_d[:, :, hc - 2, :],
                        x2st[:, hc % 2].rearrange("p (a b) -> p a b", b=512))
        for hc in (FC - 2, FC - 1):
            sync.wait_ge(ev.h, B["fa_ev"] + NQC * (hc + 1))
            sts.dma(sync, hc, x2_d[:, :, hc, :],
                    x2st[:, hc % 2].rearrange("p (a b) -> p a b", b=512))
        sts.wait_all(sync, FC)
        wis.wait_all(sync, FC)
        ctx.w(sync, ld)

    def f_gpsimd(g):
        ctx.dma(g, ua[:], ctx.dram("upA").rearrange("(dc p) r -> p dc r", p=PAR), gld)
        ctx.dma(g, ub[:], ctx.dram("upB"), gld)
        ctx.dma(g, bup_r[:], ctx.dram("bup"), gld)
        ctx.w(g, gld)

    def f_tensor(tensor):
        tensor.wait_ge(ld.h, B["fa_ld"] + 16)
        tensor.wait_ge(gld.h, B["fa_gld"] + 48)
        tensor.wait_ge(vin.h, B["fa_vin"] + 1)
        for ncl in range(NQC):
            pt = pD if ncl % 2 == 0 else pE
            if ncl >= 2:
                tensor.wait_ge(tv.h, B["fa_tv"] + ncl - 1)
            for kc in range(DC):
                inst = tensor.matmul(pt[0:16, 0:512], ua[:, kc, :],
                                     XN[:, kc, ncl * 512:(ncl + 1) * 512],
                                     start=(kc == 0), stop=(kc == DC - 1))
            tmm.inc(inst)
        for hc in range(FC):
            tensor.wait_ge(dqd.h, B["fa_dqd"] + 2 * (hc + 1))
            if hc == 0:
                tensor.wait_ge(tv.h, B["fa_tv"] + NQC)
            for ncl in range(NQC):
                i = hc * NQC + ncl
                r = i % 4
                pt = (pA, pB)[r // 2]
                off = (r % 2) * 512
                if i >= 4:
                    tensor.wait_ge(ev.h, B["fa_ev"] + i - 3)
                for kc in range(DC):
                    inst = tensor.matmul(pt[:, off:off + 512], wbf[:, hc % 2, kc, :],
                                         XN[:, kc, ncl * 512:(ncl + 1) * 512],
                                         start=(kc == 0), stop=False)
                if ncl == NQC - 1:
                    wdn.inc(inst)
                tensor.matmul(pt[:, off:off + 512], ub[0:16, hc * 128:(hc + 1) * 128],
                              tsb[0:16, ncl, :], start=False, stop=False)
                mmf.inc(tensor.matmul(pt[:, off:off + 512],
                                      bup_r[0:1, hc * 128:(hc + 1) * 128],
                                      ones_r512[:], start=False, stop=True))
        ctx.w(tensor, mmf)

    def f_vector(vector):
        vin.inc(vector.memset(ones_r512[:], 1.0))
        for ncl in range(NQC):
            pt = pD if ncl % 2 == 0 else pE
            vector.wait_ge(tmm.h, B["fa_tmm"] + ncl + 1)
            tv.inc(vector.tensor_copy(out=tsb[0:16, ncl, :], in_=pt[0:16, 0:512]))
        for hc in range(FC):
            wis.wait(vector, hc)
            if hc >= 2:
                vector.wait_ge(dqd.h, B["fa_dqd"] + 2 * (hc - 1))
            dqd.inc(vector.tensor_scalar_add(out=wtmp[:, hc % 2],
                                             in0=wi[:, hc % 2], scalar1=-7.5))
            wss.wait(vector, hc)
            if hc >= 2:
                vector.wait_ge(wdn.h, B["fa_wdn"] + hc - 1)
            ctx.w(vector, dqd)
            dqd.inc(vector.tensor_mul(out=wbf[:, hc % 2], in0=wtmp[:, hc % 2],
                                      in1=wsc[:, hc % 2]))

    def f_scalar(scalar):
        wss.dma(scalar, 0, wsc[:, 0], sc_d[0])
        wss.dma(scalar, 1, wsc[:, 1], sc_d[1])
        for hc in range(FC):
            if hc + 2 < FC:
                scalar.wait_ge(dqd.h, B["fa_dqd"] + 2 * (hc + 1))
                wss.dma(scalar, hc + 2, wsc[:, hc % 2], sc_d[hc + 2])
            for ncl in range(NQC):
                i = hc * NQC + ncl
                r = i % 4
                pt = (pA, pB)[r // 2]
                off = (r % 2) * 512
                scalar.wait_ge(mmf.h, B["fa_mmf"] + i + 1)
                if ncl == 0 and hc >= 2:
                    sts.wait(scalar, hc - 2)
                ev.inc(scalar.activation(
                    out=x2st[:, hc % 2, ncl * 512:(ncl + 1) * 512],
                    in_=pt[:, off:off + 512], func=AF.Relu))

    ctx.sec("sync", f_sync)
    ctx.sec("gpsimd", f_gpsimd)
    ctx.sec("tensor", f_tensor)
    ctx.sec("vector", f_vector)
    ctx.sec("scalar", f_scalar)
    ctx.barrier()


def ph_ffnb(ctx, src="xres", dst="xres1"):
    cfg = ctx.cfg
    DC, NQC, n, FC = cfg.DC, cfg.NQC, cfg.n_own, cfg.FC
    GS = 2 if NQC % 2 == 0 else 1
    NG = NQC // GS
    NI = NQC * DC
    x2s = ctx.sbuf("fb_x2", [PAR, 2, FC, 512], BF16)
    wd = ctx.sbuf("fb_wd", [PAR, 2, FC, 128], BF16)
    da = ctx.sbuf("fb_da", [PAR, FC, 16], BF16)
    db = ctx.sbuf("fb_db", [16, cfg.D], BF16)
    bdn_r = ctx.sbuf("fb_bdn", [1, cfg.D], BF16)
    tsb = ctx.sbuf("fb_t", [16, 2, 512], BF16)
    gst = ctx.sbuf("fb_g", [PAR, 4, 512], F32)
    xcb = ctx.sbuf("fb_xc", [PAR, 2, 512], F32)
    scb = ctx.sbuf("fb_sc", [PAR, 2, 512], F32)
    prb = ctx.sbuf("fb_pr", [PAR, 2, 512], F32)
    ocb = ctx.sbuf("fb_oc", [PAR, 2, 512], F32)
    ones_r512 = ctx.sbuf("ones_r512", [1, 512], BF16)
    pA = ctx.psum("pA", [PAR, 1024], F32)
    pB = ctx.psum("pB", [PAR, 1024], F32)
    pD = ctx.psum("pD", [PAR, 512], F32)
    x2_d, wd_d = ctx.dram("x2b"), ctx.dram("wdnbf")
    x_d, s_d, o_d = ctx.dram(src), ctx.dram("sfm"), ctx.dram(dst)

    gld = ctx.dma_sem("fb_gld", sw=True)
    x2l = ctx.slots("fb_x2l", 2)
    wde = ctx.slots("fb_wde", 1)
    wdo = ctx.slots("fb_wdo", 1)
    xls = ctx.slots("fb_xl", 2)
    sls = ctx.slots("fb_sl", 2)
    sto = ctx.slots("fb_st", 2)
    tmm = ctx.sem("fb_tmm")
    tv = ctx.sem("fb_tv")
    mmf = ctx.sem("fb_mmf")
    ev = ctx.sem("fb_ev")
    dn = ctx.sem("fb_dn")
    vin = ctx.sem("fb_vin")
    B = {x.name: x.v for x in (gld, tmm, tv, mmf, ev, dn, vin)}

    def out_ap(i):
        w, sub = i // GS, i % GS
        dc, ncl = w % DC, GS * (w // DC) + sub
        return o_d[dc * 128:(dc + 1) * 128, ncl * 512:(ncl + 1) * 512]

    def xs_ap(d, i):
        w, sub = i // GS, i % GS
        dc, ncl = w % DC, GS * (w // DC) + sub
        return d[dc * 128:(dc + 1) * 128, ncl * 512:(ncl + 1) * 512]

    def f_sync(sync):
        for w in range(NG * DC):
            if w % 2 == 0:
                if w >= 2:
                    sync.wait_ge(mmf.h, B["fb_mmf"] + GS * (w - 1))
                wde.dma(sync, w // 2, wd[:, 0],
                        wd_d[w % DC].rearrange("p (a b) -> p a b", b=128))
            for sub in range(GS):
                i = GS * w + sub
                if i == 0:
                    xls.dma(sync, 0, xcb[:, 0], xs_ap(x_d, 0))
                    xls.dma(sync, 1, xcb[:, 1], xs_ap(x_d, 1))
                if i >= 1:
                    sync.wait_ge(dn.h, B["fb_dn"] + 2 * i)
                    sto.dma(sync, i - 1, out_ap(i - 1), ocb[:, (i - 1) % 2])
                    if i + 1 < NI:
                        xls.dma(sync, i + 1, xcb[:, (i + 1) % 2], xs_ap(x_d, i + 1))
        sync.wait_ge(dn.h, B["fb_dn"] + 2 * NI)
        sto.dma(sync, NI - 1, out_ap(NI - 1), ocb[:, (NI - 1) % 2])
        sto.wait_all(sync, NI)
        wde.wait_all(sync, (NG * DC + 1) // 2)
        wdo.wait_all(sync, NG * DC // 2)
        x2l.wait_all(sync, NQC)
        xls.wait_all(sync, NI)
        sls.wait_all(sync, NI)

    def f_gpsimd(g):
        ctx.dma(g, da[:], ctx.dram("dnA").rearrange("(hc p) r -> p hc r", p=PAR), gld)
        ctx.dma(g, db[:], ctx.dram("dnB"), gld)
        ctx.dma(g, bdn_r[:], ctx.dram("bdn"), gld)
        ctx.w(g, gld)

    def f_scalar(scalar):
        for sub in range(GS):
            x2l.dma(scalar, sub, x2s[:, sub], x2_d[:, sub])
        if NG * DC > 1:
            wdo.dma(scalar, 0, wd[:, 1],
                    wd_d[1 % DC].rearrange("p (a b) -> p a b", b=128))
        for i in range(NI):
            if (i + 3) % GS == 0:
                w = (i + 3) // GS
                if w < NG * DC and w % 2 == 1:
                    scalar.wait_ge(mmf.h, B["fb_mmf"] + GS * (w - 1))
                    wdo.dma(scalar, (w - 1) // 2, wd[:, 1],
                            wd_d[w % DC].rearrange("p (a b) -> p a b", b=128))
            if i == 0:
                sls.dma(scalar, 0, scb[:, 0], xs_ap(s_d, 0))
            if i + 1 < NI:
                if i >= 1:
                    scalar.wait_ge(dn.h, B["fb_dn"] + 2 * i)
                sls.dma(scalar, i + 1, scb[:, (i + 1) % 2], xs_ap(s_d, i + 1))
            r = i % 4
            pt = (pA, pB)[r // 2]
            off = (r % 2) * 512
            scalar.wait_ge(mmf.h, B["fb_mmf"] + i + 1)
            if i >= 4:
                scalar.wait_ge(dn.h, B["fb_dn"] + 2 * i - 7)
            ev.inc(scalar.activation(out=gst[:, i % 4], in_=pt[:, off:off + 512],
                                     func=AF.Silu))
            if i % (GS * DC) == GS * DC - 1:
                grp = i // (GS * DC) + 1
                if grp < NG:
                    scalar.wait_ge(mmf.h, B["fb_mmf"] + GS * DC * grp)
                    scalar.wait_ge(tmm.h, B["fb_tmm"] + GS * grp)
                    for sub in range(GS):
                        ncl = GS * grp + sub
                        x2l.dma(scalar, ncl, x2s[:, sub], x2_d[:, ncl])

    def f_tensor(tensor):
        tensor.wait_ge(gld.h, B["fb_gld"] + 48)
        tensor.wait_ge(vin.h, B["fb_vin"] + 1)
        for g in range(NG):
            for sub in range(GS):
                ncl = GS * g + sub
                x2l.wait(tensor, ncl)
                if ncl >= 1:
                    tensor.wait_ge(tv.h, B["fb_tv"] + ncl)
                for hb in range(FC):
                    inst = tensor.matmul(pD[0:16, 0:512], da[:, hb, :],
                                         x2s[:, sub, hb, :],
                                         start=(hb == 0), stop=(hb == FC - 1))
                tmm.inc(inst)
            for dc in range(DC):
                w = g * DC + dc
                for sub in range(GS):
                    i = GS * w + sub
                    if sub == 0:
                        if w % 2 == 0:
                            wde.wait(tensor, w // 2)
                        else:
                            wdo.wait(tensor, (w - 1) // 2)
                    if dc == 0 and sub == 0:
                        tensor.wait_ge(tv.h, B["fb_tv"] + GS * g + GS)
                    r = i % 4
                    pt = (pA, pB)[r // 2]
                    off = (r % 2) * 512
                    if i >= 4:
                        tensor.wait_ge(ev.h, B["fb_ev"] + i - 3)
                    for hb in range(FC):
                        tensor.matmul(pt[:, off:off + 512], wd[:, w % 2, hb, :],
                                      x2s[:, sub, hb, :],
                                      start=(hb == 0), stop=False)
                    tensor.matmul(pt[:, off:off + 512],
                                  db[0:16, dc * 128:(dc + 1) * 128],
                                  tsb[0:16, sub, :], start=False, stop=False)
                    mmf.inc(tensor.matmul(pt[:, off:off + 512],
                                          bdn_r[0:1, dc * 128:(dc + 1) * 128],
                                          ones_r512[:], start=False, stop=True))
        ctx.w(tensor, mmf)

    def f_vector(vector):
        vin.inc(vector.memset(ones_r512[:], 1.0))
        for g in range(NG):
            for sub in range(GS):
                ncl = GS * g + sub
                vector.wait_ge(tmm.h, B["fb_tmm"] + ncl + 1)
                if g >= 1:
                    vector.wait_ge(mmf.h, B["fb_mmf"] + GS * DC * g)
                tv.inc(vector.tensor_copy(out=tsb[0:16, sub, :],
                                          in_=pD[0:16, 0:512]))
            for dc in range(DC):
                for sub in range(GS):
                    i = GS * (g * DC + dc) + sub
                    vector.wait_ge(ev.h, B["fb_ev"] + i + 1)
                    xls.wait(vector, i)
                    sls.wait(vector, i)
                    if i >= 2:
                        sto.wait(vector, i - 2)
                    dn.inc(vector.tensor_mul(out=prb[:, i % 2], in0=gst[:, i % 4],
                                             in1=scb[:, i % 2]))
                    ctx.w(vector, dn)
                    dn.inc(vector.tensor_add(out=ocb[:, i % 2], in0=xcb[:, i % 2],
                                             in1=prb[:, i % 2]))

    ctx.sec("sync", f_sync)
    ctx.sec("gpsimd", f_gpsimd)
    ctx.sec("scalar", f_scalar)
    ctx.sec("tensor", f_tensor)
    ctx.sec("vector", f_vector)
    ctx.barrier()


def ph_gate(ctx, src, dst):
    cfg = ctx.cfg
    DC, n = cfg.DC, cfg.n_own
    x_d, s_d, g_d, o_d = (ctx.dram(src), ctx.dram("sfm"), ctx.dram("gfm"),
                          ctx.dram(dst))
    xc = ctx.sbuf("gt_x", [PAR, 2, n], F32)
    sc_ = ctx.sbuf("gt_s", [PAR, 2, n], F32)
    gc = ctx.sbuf("gt_g", [PAR, 2, n], F32)
    pr = ctx.sbuf("gt_p", [PAR, 2, n], F32)
    oc_ = ctx.sbuf("gt_o", [PAR, 2, n], F32)
    xl = ctx.slots("gt_xl", 2)
    sl = ctx.slots("gt_sl", 2)
    gl = ctx.slots("gt_gl", 2)
    sts = ctx.slots("gt_st", 2)
    cv = ctx.sem("chain_v")
    dn = ctx.sem("gt_dn")
    B = {"dn": dn.v}

    def f_sync(sync):
        for dc in range(DC):
            if dc >= 2:
                sync.wait_ge(dn.h, B["dn"] + 2 * (dc - 1))
            xl.dma(sync, dc, xc[:, dc % 2], x_d[dc * 128:(dc + 1) * 128, :])
            sl.dma(sync, dc, sc_[:, dc % 2], s_d[dc * 128:(dc + 1) * 128, :])
            gl.dma(sync, dc, gc[:, dc % 2], g_d[dc * 128:(dc + 1) * 128, :])
            sync.wait_ge(dn.h, B["dn"] + 2 * (dc + 1))
            sts.dma(sync, dc, o_d[dc * 128:(dc + 1) * 128, :], oc_[:, dc % 2])
        sts.wait_all(sync, DC)

    def f_vector(vector):
        for dc in range(DC):
            sl.wait(vector, dc)
            gl.wait(vector, dc)
            dn.inc(vector.tensor_mul(out=pr[:, dc % 2], in0=sc_[:, dc % 2],
                                     in1=gc[:, dc % 2]))
            xl.wait(vector, dc)
            if dc >= 2:
                sts.wait(vector, dc - 2)
            ctx.w(vector, dn)
            dn.inc(vector.tensor_add(out=oc_[:, dc % 2], in0=xc[:, dc % 2],
                                     in1=pr[:, dc % 2]))

    ctx.sec("sync", f_sync)
    ctx.sec("vector", f_vector)
    ctx.barrier()


def ph_muprep(ctx):
    cfg = ctx.cfg
    DC, NS = cfg.DC, cfg.NSLOT
    mu_d, mun_d = ctx.dram("mu"), ctx.dram("mun")
    ch = ctx.sbuf("mp_ch", [PAR, 2, NS], F32)
    msq = ctx.sbuf("mp_sq", [PAR, 2, NS], BF16)
    rows = ctx.sbuf("mp_rows", [1, 2, NS], F32)
    rb = ctx.sbuf("mp_rb", [PAR, NS], F32)
    mst = ctx.sbuf("mp_mst", [PAR, 2, NS], BF16)
    ones_bf = ctx.sbuf("ones_bf_col", [PAR, 1], BF16)
    ones_f1 = ctx.sbuf("ones_f32_row", [1, PAR], F32)
    pA = ctx.psum("pA", [PAR, 1024], F32)
    pB = ctx.psum("pB", [PAR, 1024], F32)

    def pstat(c):
        return (pA if c < 2 else pB), (c % 2) * 512
    NC512 = NS // 512

    chs = ctx.slots("mp_ch", 2)
    sts = ctx.slots("mp_st", 2)
    sqd = ctx.sem("mp_sq"); mmd = ctx.sem("mp_mm"); evs = ctx.sem("mp_ev")
    rowd = ctx.sem("mp_row"); bcd = ctx.sem("mp_bc"); bce = ctx.sem("mp_bce")
    p2 = ctx.sem("mp_p2"); vin = ctx.sem("mp_vin")
    cv = ctx.sem("chain_v")
    B = {x.name: x.v for x in (sqd, mmd, evs, rowd, bcd, bce, p2, vin)}

    def f_sync(sync):
        for dc in range(DC):
            if dc >= 2:
                sync.wait_ge(sqd.h, B["mp_sq"] + dc - 1)
            chs.dma(sync, dc, ch[:, dc % 2], mu_d[dc * 128:(dc + 1) * 128, :])
        for dc in range(DC):
            if dc >= 2:
                sync.wait_ge(p2.h, B["mp_p2"] + dc - 1)
            sync.wait_ge(bce.h, B["mp_bce"] + 1)
            chs.dma(sync, DC + dc, ch[:, dc % 2], mu_d[dc * 128:(dc + 1) * 128, :])
            sync.wait_ge(p2.h, B["mp_p2"] + dc + 1)
            sts.dma(sync, dc, mun_d[dc * 128:(dc + 1) * 128, :], mst[:, dc % 2])
        sts.wait_all(sync, DC)
        chs.wait_all(sync, 2 * DC)

    def f_scalar(scalar):
        for dc in range(DC):
            chs.wait(scalar, dc)
            if dc >= 2:
                scalar.wait_ge(mmd.h, B["mp_mm"] + dc - 1)
            sqd.inc(scalar.activation(out=msq[:, dc % 2], in_=ch[:, dc % 2],
                                      func=AF.Square))
        scalar.wait_ge(evs.h, B["mp_ev"] + 1)
        rowd.inc(scalar.activation(out=rows[0:1, 1], in_=rows[0:1, 0],
                                   func=AF.Sqrt))

    def f_tensor(tensor):
        tensor.wait_ge(vin.h, B["mp_vin"] + 1)
        for dc in range(DC):
            tensor.wait_ge(sqd.h, B["mp_sq"] + dc + 1)
            for c in range(NC512):
                pt, po = pstat(c)
                inst = tensor.matmul(pt[0:1, po:po + 512], ones_bf[:],
                                     msq[:, dc % 2, c * 512:(c + 1) * 512],
                                     start=(dc == 0), stop=(dc == DC - 1))
            mmd.inc(inst)
        tensor.wait_ge(rowd.h, B["mp_row"] + 2)
        for c in range(NC512):
            pt, po = pstat(c)
            inst = tensor.matmul(pt[:, po:po + 512], ones_f1[:],
                                 rows[0:1, 1, c * 512:(c + 1) * 512],
                                 start=True, stop=True)
        bcd.inc(inst)
        ctx.w(tensor, bcd)

    def f_vector(vector):
        vector.memset(ones_bf[:], 1.0)
        vin.inc(vector.memset(ones_f1[:], 1.0))
        vector.wait_ge(mmd.h, B["mp_mm"] + DC)
        for c in range(NC512):
            pt, po = pstat(c)
            inst = vector.tensor_copy(out=rows[0:1, 0, c * 512:(c + 1) * 512],
                                      in_=pt[0:1, po:po + 512])
        evs.inc(inst)
        vector.wait_ge(rowd.h, B["mp_row"] + 1)
        cv.inc(vector.tensor_scalar_max(out=rows[0:1, 1], in0=rows[0:1, 1],
                                        scalar1=1e-12))
        ctx.w(vector, cv)
        rowd.inc(vector.reciprocal(out=rows[0:1, 1], in_=rows[0:1, 1]))
        vector.wait_ge(bcd.h, B["mp_bc"] + 1)
        for c in range(NC512):
            pt, po = pstat(c)
            inst = vector.tensor_copy(out=rb[:, c * 512:(c + 1) * 512],
                                      in_=pt[:, po:po + 512])
        bce.inc(inst)
        for dc in range(DC):
            chs.wait(vector, DC + dc)
            if dc >= 2:
                sts.wait(vector, dc - 2)
            p2.inc(vector.tensor_mul(out=mst[:, dc % 2], in0=ch[:, dc % 2],
                                     in1=rb[:]))

    ctx.sec("sync", f_sync)
    ctx.sec("scalar", f_scalar)
    ctx.sec("tensor", f_tensor)
    ctx.sec("vector", f_vector)
    ctx.barrier()


def ph_moea(ctx):
    cfg = ctx.cfg
    DC, n, NT, NS = cfg.DC, cfg.n_own, cfg.NT, cfg.NSLOT
    D = cfg.D
    XN = ctx.sbuf("XN", [PAR, DC, n], BF16)
    XLN = ctx.sbuf("XLN", [PAR, DC, n], BF16)
    XNflat = XN[:].rearrange("p a b -> p (a b)")
    rows = ctx.sbuf("ma_rows", [1, 2, n], F32)
    R2b = ctx.sbuf("ma_r2b", [PAR, n], F32)
    gb_sb = ctx.sbuf("ma_gb", [PAR, DC, 2], F32)
    xsq = ctx.sbuf("ma_xsq", [PAR, 2, n], BF16)
    tmpf = ctx.sbuf("ma_tmp", [PAR, 2, n], F32)
    sc_sb = ctx.sbuf("ma_sc", [1, 4], F32)
    negst = ctx.sbuf("ma_negst", [PAR, 1], F32)
    munb = ctx.sbuf("ma_mun", [PAR, 1, DC, 512], BF16)
    est = ctx.sbuf("ma_est", [PAR, 2, 512], BF16)
    rowZt = ctx.sbuf("ma_rzt", [PAR, NT, 4], F32)
    rowZ_sb = ctx.sbuf("ma_rz", [PAR, NT], F32)
    etile = ctx.sbuf("ma_et", [PAR, 2, 128], BF16)
    ust = ctx.sbuf("ma_ust", [PAR, 1, 2052], F32)
    ones_bf = ctx.sbuf("ones_bf_col", [PAR, 1], BF16)
    ones_f1 = ctx.sbuf("ones_f32_row", [1, PAR], F32)
    idf = ctx.sbuf("idf_bf", [PAR, 128], BF16)
    pA = ctx.psum("pA", [PAR, 1024], F32)
    pB = ctx.psum("pB", [PAR, 1024], F32)
    pC = ctx.psum("pC", [PAR, 1024], BF16)
    pC2 = ctx.psum("pC2", [PAR, 1024], BF16)
    pD = ctx.psum("pD", [PAR, 512], F32)

    def pstat(c):
        return (pA if c < 2 else pB), (c % 2) * 512
    NC5 = n // 512
    ST = NS // 128

    xn_d, var_d = ctx.dram("xn"), ctx.dram("varrow")
    mun_d, E_d, xlt_d = ctx.dram("mun"), ctx.dram("Ebuf"), ctx.dram("xlnT")
    rz_d, us_d = ctx.dram("rowZ"), ctx.dram("Usum")
    nc = ctx.nc
    ub_i = nc.dram_tensor("ma_ub", [NS * 2052], F32, kind="Internal").ap()
    ug_i = nc.dram_tensor("ma_ug", [NS * 2052], F32, kind="Internal").ap()
    groups = [[2 * i, 2 * i + 1] for i in range(cfg.n_cores // 2)]

    ld = ctx.dma_sem("ma_ld")
    muls = ctx.slots("ma_mul", 2)
    els = ctx.slots("ma_el", 2)
    stE = ctx.slots("ma_stE", 2)
    stX = ctx.slots("ma_stX", 2)
    stU = ctx.slots("ma_stU", 2)
    stZ = ctx.dma_sem("ma_stZ")
    cv = ctx.sem("chain_v")
    r2a = ctx.sem("ma_r2a"); r2s = ctx.sem("ma_r2s"); r2c = ctx.sem("ma_r2c")
    scd = ctx.sem("ma_scd")
    bcd = ctx.sem("ma_bcd"); bce = ctx.sem("ma_bce")
    xld = ctx.sem("ma_xld")
    sqd = ctx.sem("ma_sqd")
    nmm = ctx.sem("ma_nmm"); nev = ctx.sem("ma_nev"); nsq = ctx.sem("ma_nsq")
    srd = ctx.sem("ma_srd")
    sbc = ctx.sem("ma_sbc"); sbe = ctx.sem("ma_sbe")
    xnd = ctx.sem("ma_xnd")
    trd = ctx.sem("ma_trd"); cte = ctx.sem("ma_cte")
    lmm = ctx.sem("ma_lmm"); eed = ctx.sem("ma_eed")
    rzd = ctx.sem("ma_rzd")
    umm = ctx.sem("ma_umm"); uev = ctx.sem("ma_uev")
    ccd = ctx.sem("ma_ccd", keep=True); gth = ctx.dma_sem("ma_gth", sw=True)
    gidf = ctx.dma_sem("ma_gidf", sw=True)
    vin = ctx.sem("ma_vin")
    B = {x.name: x.v for x in (ld, cv, r2a, r2s, r2c, scd, bcd, bce, xld, sqd,
                               nmm, nev, nsq, srd, sbc, sbe, xnd, trd, cte,
                               lmm, eed, rzd, umm, uev, ccd, gth, stZ, vin,
                               gidf)}
    NLD = 3 + 2 * DC

    def f_sync(sync):
        ctx.dma(sync, XN[:], xn_d, ld)
        ctx.dma(sync, rows[0:1, 0], var_d, ld)
        ctx.dma(sync, sc_sb[0:1, 0:1], ctx.dram("moescale"), ld)
        for dc in range(DC):
            ctx.dma(sync, gb_sb[:, dc, 0:1],
                    ctx.dram("gamma")[dc * 128:(dc + 1) * 128, :], ld)
            ctx.dma(sync, gb_sb[:, dc, 1:2],
                    ctx.dram("beta")[dc * 128:(dc + 1) * 128, :], ld)
        for nt in range(NT):
            sync.wait_ge(cte.h, B["ma_cte"] + 2 * (nt + 1))
            stX.dma(sync, nt, xlt_d[nt * 128:(nt + 1) * 128, :],
                    XNflat[:, nt * D:(nt + 1) * D])
        for sc in range(4):
            if sc >= 1:
                sync.wait_ge(lmm.h, B["ma_lmm"] + NT * sc)
            muls.dma(sync, sc, munb[:, 0],
                     mun_d[:, sc * 512:(sc + 1) * 512]
                     .rearrange("(dc p) m -> p dc m", p=PAR))
            for nt in range(NT):
                k = sc * NT + nt
                sync.wait_ge(eed.h, B["ma_eed"] + k + 1)
                stE.dma(sync, k, E_d[nt * 128:(nt + 1) * 128,
                                     sc * 512:(sc + 1) * 512], est[:, k % 2])
        sync.wait_ge(rzd.h, B["ma_rzd"] + 1)
        ctx.dma(sync, rz_d, rowZ_sb[:], stZ)
        stE.wait_all(sync, 4 * NT)
        for st in range(ST):
            for nt in range(NT):
                k = st * NT + nt
                if k >= 2:
                    sync.wait_ge(umm.h, B["ma_umm"] + k - 1)
                els.dma(sync, k, etile[:, k % 2],
                        E_d[nt * 128:(nt + 1) * 128, st * 128:(st + 1) * 128])
            sync.wait_ge(uev.h, B["ma_uev"] + st + 1)
            stU.dma(sync, st,
                    ub_i.rearrange("(a b) -> a b", b=2052)[st * 128:(st + 1) * 128, :],
                    ust[:, 0])
        stE.wait_all(sync, 4 * NT)
        stX.wait_all(sync, NT)
        stU.wait_all(sync, ST)
        muls.wait_all(sync, 4)
        els.wait_all(sync, ST * NT)
        ctx.w(sync, ld)
        ctx.w(sync, stZ)

    def f_gpsimd(g):
        ctx.dma(g, idf[:], ctx.dram("ident"), gidf)
        stU.wait_all(g, ST)
        ccd.inc(g.collective_compute(
            "AllReduce", ALU.add, replica_groups=groups,
            ins=[ub_i.opt()], outs=[ug_i.opt()]))
        ctx.w(g, ccd)
        ctx.dma(g, us_d.rearrange("a b -> (a b)"), ug_i, gth)
        ctx.w(g, gth)

    def f_vector(vector):
        vector.memset(ones_bf[:], 1.0)
        vector.memset(ust[:, :, 2049:2052], 0.0)
        vin.inc(vector.memset(ones_f1[:], 1.0))
        vector.wait_ge(ld.h, B["ma_ld"] + 16 * NLD)
        cv.inc(vector.tensor_scalar_add(out=rows[0:1, 1], in0=rows[0:1, 0],
                                        scalar1=1e-5))
        ctx.w(vector, cv)
        cv.inc(vector.reciprocal(out=rows[0:1, 1], in_=rows[0:1, 1]))
        ctx.w(vector, cv)
        cv.inc(vector.tensor_mul(out=rows[0:1, 1], in0=rows[0:1, 0],
                                 in1=rows[0:1, 1]))
        ctx.w(vector, cv)
        r2a.inc(vector.tensor_scalar_add(out=rows[0:1, 1], in0=rows[0:1, 1],
                                         scalar1=1e-5))
        vector.wait_ge(r2s.h, B["ma_r2s"] + 1)
        r2c.inc(vector.reciprocal(out=rows[0:1, 1], in_=rows[0:1, 0]))
        vector.wait_ge(bcd.h, B["ma_bcd"] + 1)
        for c in range(NC5):
            pt, po = pstat(c)
            inst = vector.tensor_copy(out=R2b[:, c * 512:(c + 1) * 512],
                                      in_=pt[:, po:po + 512])
        bce.inc(inst)
        ctx.w(vector, bce)
        for dc in range(DC):
            xld.inc(vector.tensor_mul(out=tmpf[:, dc % 2], in0=XN[:, dc, :],
                                      in1=R2b[:]))
            ctx.w(vector, xld)
            xld.inc(vector.tensor_scalar(out=XLN[:, dc, :], in0=tmpf[:, dc % 2],
                                         scalar1=gb_sb[:, dc, 0:1],
                                         scalar2=gb_sb[:, dc, 1:2],
                                         op0=ALU.mult, op1=ALU.add))
        vector.wait_ge(nmm.h, B["ma_nmm"] + DC)
        for c in range(NC5):
            pt, po = pstat(c)
            inst = vector.tensor_copy(out=rows[0:1, 0, c * 512:(c + 1) * 512],
                                      in_=pt[0:1, po:po + 512])
        nev.inc(inst)
        vector.wait_ge(nsq.h, B["ma_nsq"] + 1)
        vector.wait_ge(scd.h, B["ma_scd"] + 1)
        cv.inc(vector.tensor_scalar_max(out=rows[0:1, 1], in0=rows[0:1, 1],
                                        scalar1=1e-12))
        ctx.w(vector, cv)
        cv.inc(vector.reciprocal(out=rows[0:1, 1], in_=rows[0:1, 1]))
        ctx.w(vector, cv)
        srd.inc(vector.tensor_scalar_mul(out=rows[0:1, 1], in0=rows[0:1, 1],
                                         scalar1=sc_sb[0:1, 1:2]))
        vector.wait_ge(sbc.h, B["ma_sbc"] + 1)
        for c in range(NC5):
            pt, po = pstat(c)
            inst = vector.tensor_copy(out=R2b[:, c * 512:(c + 1) * 512],
                                      in_=pt[:, po:po + 512])
        sbe.inc(inst)
        vector.wait_ge(sbc.h, B["ma_sbc"] + 2)
        sbe.inc(vector.tensor_copy(out=negst[:], in_=pD[:, 0:1]))
        for nt in range(NT):
            for half in range(2):
                vector.wait_ge(trd.h, B["ma_trd"] + 2 * nt + half + 1)
                pt = pC if half == 0 else pC2
                cte.inc(vector.tensor_copy(
                    out=XNflat[:, nt * D + half * 1024:nt * D + half * 1024 + 1024],
                    in_=pt[:, 0:1024]))
        vector.wait_ge(sqd.h, B["ma_sqd"] + DC)
        ctx.w(vector, sbe)
        for dc in range(DC):
            xnd.inc(vector.tensor_mul(out=XLN[:, dc, :], in0=XLN[:, dc, :],
                                      in1=R2b[:]))
        vector.wait_ge(eed.h, B["ma_eed"] + 4 * NT)
        cv.inc(vector.reduce_sum(out=rowZt[:, :, 0:1], in_=rowZt[:],
                                 axis=mybir.AxisListType.X))
        ctx.w(vector, cv)
        rzd.inc(vector.tensor_copy(out=rowZ_sb[:], in_=rowZt[:, :, 0]))
        for st in range(ST):
            vector.wait_ge(umm.h, B["ma_umm"] + NT * (st + 1))
            if st >= 1:
                stU.wait(vector, st - 1)
            for q in range(4):
                pt, po = pstat(q)
                vector.tensor_copy(out=ust[:, 0, q * 512:(q + 1) * 512],
                                   in_=pt[:, po:po + 512])
            uev.inc(vector.tensor_copy(out=ust[:, 0, 2048:2049],
                                       in_=pD[:, 0:1]))

    def f_scalar(scalar):
        scalar.wait_ge(ld.h, B["ma_ld"] + 16 * NLD)
        scd.inc(scalar.activation(out=sc_sb[0:1, 1:2], in_=sc_sb[0:1, 0:1],
                                  func=AF.Abs))
        ctx.w(scalar, scd)
        scd.inc(scalar.mul(out=sc_sb[0:1, 2:3], in_=sc_sb[0:1, 1:2], mul=-1.0))
        scalar.wait_ge(r2a.h, B["ma_r2a"] + 1)
        r2s.inc(scalar.activation(out=rows[0:1, 0], in_=rows[0:1, 1],
                                  func=AF.Sqrt))
        for dc in range(DC):
            scalar.wait_ge(xld.h, B["ma_xld"] + 2 * (dc + 1))
            if dc >= 2:
                scalar.wait_ge(nmm.h, B["ma_nmm"] + dc - 1)
            sqd.inc(scalar.activation(out=xsq[:, dc % 2], in_=XLN[:, dc, :],
                                      func=AF.Square))
        scalar.wait_ge(nev.h, B["ma_nev"] + 1)
        scalar.wait_ge(bce.h, B["ma_bce"] + 1)
        nsq.inc(scalar.activation(out=rows[0:1, 1], in_=rows[0:1, 0],
                                  func=AF.Sqrt))
        scalar.wait_ge(sbe.h, B["ma_sbe"] + 2)
        for sc in range(4):
            for nt in range(NT):
                k = sc * NT + nt
                scalar.wait_ge(lmm.h, B["ma_lmm"] + k + 1)
                if k >= 2:
                    stE.wait(scalar, k - 2)
                r = k % 4
                pt = (pA, pB)[r // 2]
                off = (r % 2) * 512
                eed.inc(scalar.activation(
                    out=est[:, k % 2], in_=pt[:, off:off + 512], func=AF.Exp,
                    bias=negst[:, 0:1], accum_out=rowZt[:, nt, sc:sc + 1]))

    def f_tensor(tensor):
        tensor.wait_ge(vin.h, B["ma_vin"] + 1)
        tensor.wait_ge(ld.h, B["ma_ld"] + 16 * NLD)
        tensor.wait_ge(r2c.h, B["ma_r2c"] + 1)
        for c in range(NC5):
            pt, po = pstat(c)
            inst = tensor.matmul(pt[:, po:po + 512], ones_f1[:],
                                 rows[0:1, 1, c * 512:(c + 1) * 512],
                                 start=True, stop=True)
        bcd.inc(inst)
        tensor.wait_ge(bce.h, B["ma_bce"] + 1)
        for dc in range(DC):
            tensor.wait_ge(sqd.h, B["ma_sqd"] + dc + 1)
            for c in range(NC5):
                pt, po = pstat(c)
                inst = tensor.matmul(pt[0:1, po:po + 512], ones_bf[:],
                                     xsq[:, dc % 2, c * 512:(c + 1) * 512],
                                     start=(dc == 0), stop=(dc == DC - 1))
            nmm.inc(inst)
        tensor.wait_ge(srd.h, B["ma_srd"] + 1)
        tensor.wait_ge(nev.h, B["ma_nev"] + 1)
        for c in range(NC5):
            pt, po = pstat(c)
            inst = tensor.matmul(pt[:, po:po + 512], ones_f1[:],
                                 rows[0:1, 1, c * 512:(c + 1) * 512],
                                 start=True, stop=True)
        sbc.inc(inst)
        tensor.wait_ge(scd.h, B["ma_scd"] + 2)
        sbc.inc(tensor.matmul(pD[:, 0:1], ones_f1[:], sc_sb[0:1, 2:3],
                              start=True, stop=True))
        tensor.wait_ge(gidf.h, B["ma_gidf"] + 16)
        tensor.wait_ge(xld.h, B["ma_xld"] + 2 * DC)
        for nt in range(NT):
            for half in range(2):
                hidx = 2 * nt + half
                pt = pC if half == 0 else pC2
                if hidx >= 2:
                    tensor.wait_ge(cte.h, B["ma_cte"] + hidx - 1)
                for j in range(8):
                    dc = half * 8 + j
                    inst = tensor.transpose(pt[:, j * 128:(j + 1) * 128],
                                            XLN[:, dc, nt * 128:(nt + 1) * 128],
                                            idf[:])
                trd.inc(inst)
        tensor.wait_ge(xnd.h, B["ma_xnd"] + DC)
        tensor.wait_ge(sbe.h, B["ma_sbe"] + 2)
        for sc in range(4):
            muls.wait(tensor, sc)
            for nt in range(NT):
                k = sc * NT + nt
                r = k % 4
                pt = (pA, pB)[r // 2]
                off = (r % 2) * 512
                if k >= 4:
                    tensor.wait_ge(eed.h, B["ma_eed"] + k - 3)
                for kc in range(DC):
                    inst = tensor.matmul(pt[:, off:off + 512],
                                         XLN[:, kc, nt * 128:(nt + 1) * 128],
                                         munb[:, 0, kc, :],
                                         start=(kc == 0), stop=(kc == DC - 1))
                lmm.inc(inst)
        tensor.wait_ge(eed.h, B["ma_eed"] + 4 * NT)
        tensor.wait_ge(cte.h, B["ma_cte"] + 2 * NT)
        for st in range(ST):
            if st >= 2:
                tensor.wait_ge(uev.h, B["ma_uev"] + st - 1)
            for nt in range(NT):
                k = st * NT + nt
                els.wait(tensor, k)
                for q in range(4):
                    pt, po = pstat(q)
                    tensor.matmul(pt[:, po:po + 512], etile[:, k % 2],
                                  XNflat[:, nt * D + q * 512:nt * D + (q + 1) * 512],
                                  start=(nt == 0), stop=(nt == NT - 1))
                umm.inc(tensor.matmul(pD[:, 0:1], etile[:, k % 2], ones_bf[:],
                                      start=(nt == 0), stop=(nt == NT - 1)))
        ctx.w(tensor, umm)

    ctx.sec("sync", f_sync)
    ctx.sec("gpsimd", f_gpsimd)
    ctx.sec("vector", f_vector)
    ctx.sec("scalar", f_scalar)
    ctx.sec("tensor", f_tensor)
    ctx.barrier()


def ph_moeb(ctx, src="xres1", dst="xres2"):
    cfg = ctx.cfg
    DC, NQC, n, NT, NS = cfg.DC, cfg.NQC, cfg.n_own, cfg.NT, cfg.NSLOT
    D, E = cfg.D, cfg.E
    ST = NS // 128
    SO = ctx.sbuf("SO", [PAR, E, D], BF16)
    ust = ctx.sbuf("mb_ust", [PAR, 1, 2052], F32)
    sist = ctx.sbuf("mb_sist", [PAR, 2, NS], BF16)
    sitb = ctx.sbuf("mb_sitb", [PAR, 2, DC, 128], BF16)
    rz2 = ctx.sbuf("mb_rz", [PAR, 2], F32)
    w1s = ctx.sbuf("mb_w1", [PAR, 1, DC, 512], BF16)
    w2s = ctx.sbuf("mb_w2", [PAR, 2, 4, D], BF16)
    b1f = ctx.sbuf("mb_b1", [PAR, E, 4], F32)
    b2r = ctx.sbuf("mb_b2", [1, 2, D], BF16)
    h_sb = ctx.sbuf("mb_h", [PAR, 2, 4, 128], BF16)
    eblk = ctx.sbuf("mb_eblk", [PAR, 2, NS], BF16)
    cstg = ctx.sbuf("mb_cstg", [PAR, 2, NS], BF16)
    ctst = ctx.sbuf("mb_ctst", [PAR, 2, 1024], BF16)
    rowZ_sb = ctx.sbuf("mb_rowz", [PAR, NT], F32)
    cts = ctx.sbuf("mb_cts", [PAR, 16, 512], BF16)
    gst = ctx.sbuf("mb_g", [PAR, 2, 512], F32)
    xcb = ctx.sbuf("mb_xc", [PAR, 2, 512], F32)
    scb = ctx.sbuf("mb_sc", [PAR, 2, 512], F32)
    prb = ctx.sbuf("mb_pr", [PAR, 512], F32)
    ocb = ctx.sbuf("mb_oc", [PAR, 2, 512], F32)
    ones_r128 = ctx.sbuf("ones_r128", [1, PAR], BF16)
    idf = ctx.sbuf("idf_bf", [PAR, 128], BF16)
    pA = ctx.psum("pA", [PAR, 1024], F32)
    pB = ctx.psum("pB", [PAR, 1024], F32)
    pC = ctx.psum("pC", [PAR, 1024], BF16)
    pC2 = ctx.psum("pC2", [PAR, 1024], BF16)
    pD = ctx.psum("pD", [PAR, 512], F32)
    pE = ctx.psum("pE", [PAR, 512], F32)

    us_d, E_d, rz_d = ctx.dram("Usum"), ctx.dram("Ebuf"), ctx.dram("rowZ")
    ct_d = ctx.dram("ctb")
    x_d, s_d, o_d = ctx.dram(src), ctx.dram("sfm"), ctx.dram(dst)
    w1_d, b1_d = ctx.dram("W1"), ctx.dram("b1")
    w2_d, b2_d = ctx.dram("W2"), ctx.dram("b2")

    ld = ctx.dma_sem("mb_ld")
    uls = ctx.slots("mb_uls", 2)
    wl1 = ctx.slots("mb_wl1", 1)
    wl2 = ctx.slots("mb_wl2", 2)
    wl3 = ctx.slots("mb_wl3", 2)
    elb = ctx.slots("mb_elb", 2)
    ctl = ctx.slots("mb_ctl", 2)
    stC = ctx.slots("mb_stC", 2)
    stG = ctx.slots("mb_stG", 2)
    xls = ctx.slots("mb_xl", 2)
    sls = ctx.slots("mb_sl", 2)
    cv = ctx.sem("chain_v")
    vsi = ctx.sem("mb_vsi")
    trA = ctx.sem("mb_trA"); ctA = ctx.sem("mb_ctA")
    hmm_ = ctx.sem("mb_hmm")
    geld = ctx.sem("mb_geld")
    smm = ctx.sem("mb_smm")
    soe = ctx.sem("mb_soe")
    vcs = ctx.sem("mb_vcs")
    trC = ctx.sem("mb_trC"); ctC = ctx.sem("mb_ctC")
    fmm = ctx.sem("mb_fmm")
    fev = ctx.sem("mb_fev")
    dn = ctx.sem("mb_dn")
    gidf = ctx.dma_sem("mb_gidf", sw=True)
    vin = ctx.sem("mb_vin")
    B = {x.name: x.v for x in (ld, cv, vsi, trA, ctA, hmm_, geld, smm, soe,
                               vcs, trC, ctC, fmm, fev, dn, vin, gidf)}

    def xs_ap(d, i):
        ncl, dc = i // DC, i % DC
        return d[dc * 128:(dc + 1) * 128, ncl * 512:(ncl + 1) * 512]

    def f_sync(sync):
        ctx.dma(sync, rowZ_sb[:], rz_d, ld)
        for e in range(E):
            for hc in range(4):
                ctx.dma(sync, b1f[:, e, hc:hc + 1],
                        b1_d[e, hc * 128:(hc + 1) * 128]
                        .rearrange("(a b) -> a b", b=1), ld)
        for st in range(ST):
            if st >= 1:
                sync.wait_ge(vsi.h, B["mb_vsi"] + st)
            uls.dma(sync, st, ust[:, 0],
                    us_d[st * 128:(st + 1) * 128, :])
        for nt in range(NT):
            if nt >= 2:
                sync.wait_ge(vcs.h, B["mb_vcs"] + nt - 1)
            elb.dma(sync, nt, eblk[:, nt % 2], E_d[nt * 128:(nt + 1) * 128, :])
            for half in range(2):
                k = 2 * nt + half
                sync.wait_ge(ctC.h, B["mb_ctC"] + k + 1)
                stC.dma(sync, k,
                        ct_d[half * 1024:(half + 1) * 1024,
                             nt * 128:(nt + 1) * 128]
                        .rearrange("(j p) m -> p j m", p=PAR),
                        ctst[:, k % 2].rearrange("p (j m) -> p j m", m=128))
        stC.wait_all(sync, 2 * NT)
        for ncl in range(NQC):
            if ncl >= 1:
                sync.wait_ge(fmm.h, B["mb_fmm"] + DC * ncl)
            ctl.dma(sync, ncl, cts[:],
                    ct_d[:, ncl * 512:(ncl + 1) * 512]
                    .rearrange("(st p) m -> p st m", p=PAR))
            for dc in range(DC):
                i = ncl * DC + dc
                if i == 0:
                    xls.dma(sync, 0, xcb[:, 0], xs_ap(x_d, 0))
                    xls.dma(sync, 1, xcb[:, 1], xs_ap(x_d, 1))
                if i >= 1:
                    sync.wait_ge(dn.h, B["mb_dn"] + 2 * i)
                    stG.dma(sync, i - 1, xs_ap(o_d, i - 1), ocb[:, (i - 1) % 2])
                    if i + 1 < NQC * DC:
                        xls.dma(sync, i + 1, xcb[:, (i + 1) % 2], xs_ap(x_d, i + 1))
        sync.wait_ge(dn.h, B["mb_dn"] + 2 * NQC * DC)
        stG.dma(sync, NQC * DC - 1, xs_ap(o_d, NQC * DC - 1),
                ocb[:, (NQC * DC - 1) % 2])
        stG.wait_all(sync, NQC * DC)
        xls.wait_all(sync, NQC * DC)
        sls.wait_all(sync, NQC * DC)
        uls.wait_all(sync, ST)
        elb.wait_all(sync, NT)
        ctl.wait_all(sync, NQC)
        ctx.w(sync, ld)

    def f_gpsimd(g):
        ctx.dma(g, idf[:], ctx.dram("ident"), gidf)
        ctx.w(g, gidf)

    def load_w1(eng, e):
        wl1.dma(eng, e, w1s[:, 0],
                w1_d[e].rearrange("(kc p) h -> p kc h", p=PAR))

    def load_we(eng, e):
        wl2.dma(eng, e, w2s[:, e % 2],
                w2_d[e].rearrange("(hc p) d -> p hc d", p=PAR))
        wl3.dma(eng, e, b2r[0:1, e % 2], b2_d[e].rearrange("(b a) -> b a", b=1))

    def f_vector(vector):
        vin.inc(vector.memset(ones_r128[:], 1.0))
        def so_evacs(e):
            for dq in range(4):
                i = e * 4 + dq
                vector.wait_ge(smm.h, B["mb_smm"] + i + 1)
                r = i % 4
                pt = (pA, pB)[r // 2]
                off = (r % 2) * 512
                soe.inc(vector.tensor_copy(out=SO[:, e, dq * 512:(dq + 1) * 512],
                                           in_=pt[:, off:off + 512]))

        for st in range(ST):
            uls.wait(vector, st)
            cv.inc(vector.reciprocal(out=rz2[:, st % 2:st % 2 + 1],
                                     in_=ust[:, 0, 2048:2049]))
            ctx.w(vector, cv)
            if st >= 2:
                vector.wait_ge(trA.h, B["mb_trA"] + 2 * (st - 1))
            vsi.inc(vector.tensor_scalar_mul(out=sist[:, st % 2],
                                             in0=ust[:, 0, 0:2048],
                                             scalar1=rz2[:, st % 2:st % 2 + 1]))
            if st >= 2:
                vector.wait_ge(hmm_.h, B["mb_hmm"] + 4 * (st - 1))
            for half in range(2):
                k = 2 * st + half
                vector.wait_ge(trA.h, B["mb_trA"] + k + 1)
                pt = pC if half == 0 else pC2
                ctA.inc(vector.tensor_copy(
                    out=sitb[:, st % 2, half * 8:(half + 1) * 8, :],
                    in_=pt[:, 0:1024].rearrange("p (j m) -> p j m", m=128)))
            if st >= 1:
                so_evacs(st - 1)
        so_evacs(E - 1)
        for nt in range(NT):
            elb.wait(vector, nt)
            cv.inc(vector.reciprocal(out=rz2[:, nt % 2:nt % 2 + 1],
                                     in_=rowZ_sb[:, nt:nt + 1]))
            ctx.w(vector, cv)
            if nt >= 2:
                vector.wait_ge(trC.h, B["mb_trC"] + 2 * (nt - 1))
            vcs.inc(vector.tensor_scalar_mul(out=cstg[:, nt % 2],
                                             in0=eblk[:, nt % 2],
                                             scalar1=rz2[:, nt % 2:nt % 2 + 1]))
            for half in range(2):
                k = 2 * nt + half
                vector.wait_ge(trC.h, B["mb_trC"] + k + 1)
                if k >= 2:
                    stC.wait(vector, k - 2)
                pt = pC if half == 0 else pC2
                ctC.inc(vector.tensor_copy(out=ctst[:, k % 2], in_=pt[:, 0:1024]))
        for i in range(NQC * DC):
            vector.wait_ge(fev.h, B["mb_fev"] + i + 1)
            xls.wait(vector, i)
            sls.wait(vector, i)
            if i >= 2:
                stG.wait(vector, i - 2)
            if i >= 1:
                ctx.w(vector, dn)
            dn.inc(vector.tensor_mul(out=prb[:], in0=gst[:, i % 2],
                                     in1=scb[:, i % 2]))
            ctx.w(vector, dn)
            dn.inc(vector.tensor_add(out=ocb[:, i % 2], in0=xcb[:, i % 2],
                                     in1=prb[:]))

    def f_scalar(scalar):
        load_w1(scalar, 0)
        load_we(scalar, 0)
        load_we(scalar, 1)
        scalar.wait_ge(ld.h, B["mb_ld"] + 16 * (1 + 4 * E))
        for e in range(E):
            if e >= 2:
                scalar.wait_ge(smm.h, B["mb_smm"] + 4 * (e - 1))
            for hc in range(4):
                i = e * 4 + hc
                scalar.wait_ge(hmm_.h, B["mb_hmm"] + i + 1)
                pt = pD if hc % 2 == 0 else pE
                geld.inc(scalar.activation(out=h_sb[:, e % 2, hc, :],
                                           in_=pt[:, 0:128], func=AF.Gelu,
                                           bias=b1f[:, e, hc:hc + 1]))
            if e + 1 < E:
                scalar.wait_ge(hmm_.h, B["mb_hmm"] + 4 * (e + 1))
                load_w1(scalar, e + 1)
            if e + 2 < E:
                scalar.wait_ge(smm.h, B["mb_smm"] + 4 * (e + 1))
                load_we(scalar, e + 2)
        for ncl in range(NQC):
            for dc in range(DC):
                i = ncl * DC + dc
                if i == 0:
                    sls.dma(scalar, 0, scb[:, 0], xs_ap(s_d, 0))
                if i + 1 < NQC * DC:
                    if i >= 1:
                        scalar.wait_ge(dn.h, B["mb_dn"] + 2 * i)
                    sls.dma(scalar, i + 1, scb[:, (i + 1) % 2], xs_ap(s_d, i + 1))
                scalar.wait_ge(fmm.h, B["mb_fmm"] + i + 1)
                if i >= 2:
                    scalar.wait_ge(dn.h, B["mb_dn"] + 2 * i - 3)
                r = i % 4
                pt = (pA, pB)[r // 2]
                off = (r % 2) * 512
                fev.inc(scalar.activation(out=gst[:, i % 2],
                                          in_=pt[:, off:off + 512], func=AF.Silu))
        wl1.wait_all(scalar, E)
        wl2.wait_all(scalar, E)
        wl3.wait_all(scalar, E)

    def f_tensor(tensor):
        tensor.wait_ge(vin.h, B["mb_vin"] + 1)
        tensor.wait_ge(ld.h, B["mb_ld"] + 16 * (1 + 4 * E))
        tensor.wait_ge(gidf.h, B["mb_gidf"] + 16)
        for e in range(E):
            tensor.wait_ge(vsi.h, B["mb_vsi"] + e + 1)
            for half in range(2):
                k = 2 * e + half
                pt = pC if half == 0 else pC2
                if k >= 2:
                    tensor.wait_ge(ctA.h, B["mb_ctA"] + k - 1)
                for j in range(8):
                    dcc = half * 8 + j
                    inst = tensor.transpose(pt[:, j * 128:(j + 1) * 128],
                                            sist[:, e % 2,
                                                 dcc * 128:(dcc + 1) * 128],
                                            idf[:])
                trA.inc(inst)
            wl1.wait(tensor, e)
            tensor.wait_ge(ctA.h, B["mb_ctA"] + 2 * (e + 1))
            for hc in range(4):
                i = e * 4 + hc
                pt = pD if hc % 2 == 0 else pE
                if i >= 2:
                    tensor.wait_ge(geld.h, B["mb_geld"] + i - 1)
                for kc in range(DC):
                    inst = tensor.matmul(pt[:, 0:128],
                                         w1s[:, 0, kc, hc * 128:(hc + 1) * 128],
                                         sitb[:, e % 2, kc, :],
                                         start=(kc == 0), stop=(kc == DC - 1))
                hmm_.inc(inst)
            wl2.wait(tensor, e)
            wl3.wait(tensor, e)
            tensor.wait_ge(geld.h, B["mb_geld"] + 4 * (e + 1))
            for dq in range(4):
                i = e * 4 + dq
                r = i % 4
                pt = (pA, pB)[r // 2]
                off = (r % 2) * 512
                if i >= 4:
                    tensor.wait_ge(soe.h, B["mb_soe"] + i - 3)
                for hc in range(4):
                    tensor.matmul(pt[:, off:off + 512], h_sb[:, e % 2, hc, :],
                                  w2s[:, e % 2, hc, dq * 512:(dq + 1) * 512],
                                  start=(hc == 0), stop=False)
                smm.inc(tensor.matmul(pt[:, off:off + 512], ones_r128[:],
                                      b2r[0:1, e % 2, dq * 512:(dq + 1) * 512],
                                      start=False, stop=True))
        tensor.wait_ge(soe.h, B["mb_soe"] + 4 * E)
        for nt in range(NT):
            tensor.wait_ge(vcs.h, B["mb_vcs"] + nt + 1)
            for half in range(2):
                k = 2 * nt + half
                pt = pC if half == 0 else pC2
                if k >= 2:
                    tensor.wait_ge(ctC.h, B["mb_ctC"] + k - 1)
                for j in range(8):
                    scc = half * 8 + j
                    inst = tensor.transpose(pt[:, j * 128:(j + 1) * 128],
                                            cstg[:, nt % 2,
                                                 scc * 128:(scc + 1) * 128],
                                            idf[:])
                trC.inc(inst)
        for ncl in range(NQC):
            ctl.wait(tensor, ncl)
            for dc in range(DC):
                i = ncl * DC + dc
                r = i % 4
                pt = (pA, pB)[r // 2]
                off = (r % 2) * 512
                if i >= 4:
                    tensor.wait_ge(fev.h, B["mb_fev"] + i - 3)
                for st in range(ST):
                    inst = tensor.matmul(pt[:, off:off + 512],
                                         SO[:, st, dc * 128:(dc + 1) * 128],
                                         cts[:, st, :],
                                         start=(st == 0), stop=(st == ST - 1))
                fmm.inc(inst)
        ctx.w(tensor, fmm)

    ctx.sec("sync", f_sync)
    ctx.sec("gpsimd", f_gpsimd)
    ctx.sec("vector", f_vector)
    ctx.sec("scalar", f_scalar)
    ctx.sec("tensor", f_tensor)
    ctx.barrier()

